# version 05
# baseline (speedup 1.0000x reference)
import numpy as np
from contextlib import ExitStack

import concourse.bass as bass
import concourse.mybir as mybir
from concourse.bass_utils import run_bass_kernel_spmd

F32 = mybir.dt.float32
F16 = mybir.dt.float16
F8 = mybir.dt.float8e4
AF = mybir.ActivationFunctionType
ALU = mybir.AluOpType
AX = mybir.AxisListType

MAGIC = 12582912.0
EPS = 1e-5
D = 2048
T = 2048
I = 8192
HID = 5461
NH = 16
DH = 128
NT = T // 128
KC = D // 128
NO = I // 512
IC = I // 128
NW = 8192 * 2048
GAMMA = [1.0 - 2.0 ** (-5 - h) for h in range(NH)]
RDH = DH ** (-0.5)
INV127 = 1.0 / 127.0


class Sem:
    def __init__(self, es, nc, name):
        self.h = es.enter_context(nc.semaphore(name))
        self.n = 0

    def inc(self, instr, k=1):
        instr.then_inc(self.h, k)
        self.n += k
        return self.n


class Eng:
    def __init__(self, block, name):
        self._b = block
        self._n = name

    def batch(self, f):
        out = [None]

        def g(e):
            out[0] = f(e)

        getattr(self._b, self._n)(g)
        return out[0]

    def __getattr__(self, attr):
        def call(*a, **k):
            return self.batch(lambda e: getattr(e, attr)(*a, **k))

        return call


class Ctx:
    def __init__(self, nc, es, ncores):
        self.nc = nc
        self.es = es
        self.ncores = ncores
        self._sid = 0
        self.nbar = 0
        self.bar = None
        self.T = self.V = self.S = self.G = self.Y = None

    def sem(self, name):
        if not hasattr(self, "_semcache"):
            self._semcache = {}
        if name not in self._semcache:
            self._sid += 1
            self._semcache[name] = Sem(self.es, self.nc, f"{name}_{self._sid}")
        return self._semcache[name]

    def sb(self, name, shape, dtype):
        return self.es.enter_context(self.nc.sbuf_tensor(name, shape, dtype))

    def barrier(self):
        if not hasattr(self, "_semcache"):
            return
        sems = [s for s in self._semcache.values() if s.n > 0]
        for key in "TVSGY":
            eng = getattr(self, key)

            def f(e, sems=sems):
                for s in sems:
                    e.wait_ge(s.h, s.n)
            eng.batch(f)


def emit_prologue(C, dram, pers):
    nc = C.nc
    NC = C.ncores
    ROWG = I // NC
    ROWD = D // NC
    NRG = ROWG // 128
    NRD = ROWD // 128
    TE, VE, SE, GE, YE = C.T, C.V, C.S, C.G, C.Y

    wldp = [C.sem("p_wld0"), C.sem("p_wld1")]
    vst = C.sem("p_v")
    sst = C.sem("p_s")
    tst = C.sem("p_t")
    est = C.sem("p_e")
    ccs = C.sem("p_cc")
    sumd = C.sem("p_sumd")
    sumd2 = C.sem("p_sumd2")
    outd = C.sem("p_outd")
    fld = C.sem("p_fld")

    es2 = ExitStack()
    sbq = lambda n, s, d: es2.enter_context(nc.sbuf_tensor(n, s, d))
    psq = lambda n, s, d: es2.enter_context(nc.psum_tensor(n, s, d))

    row = [sbq(f"p_row{i}", [128, D], F32) for i in range(2)]
    rt = sbq("p_rt", [128, D], F32)
    q1 = sbq("p_q1", [128, D], F32)
    qrow = sbq("p_qrow", [128, D], F16)
    accw = None
    acc = sbq("p_acc", [128, 4], F32)
    sums = sbq("p_sums", [1, 4], F32)
    wsrow = sbq("p_wsrow", [1, 4], F32)
    winvrow = sbq("p_winvrow", [1, 4], F32)
    wgt_sb = sbq("p_wgt", [128, KC * ROWG], F8)
    wut_sb = sbq("p_wut", [128, KC * ROWG], F8)
    wdt_sb = sbq("p_wdt", [128, IC * ROWD], F8)
    pst = [psq(f"p_pst{i}", [128, 1024], F16) for i in range(2)]
    psb = psq("p_psb", [128, 512], F32)

    idl = C.sem("p_idl")
    YE.batch(lambda sy: idl.inc(
        sy.dma_start(out=pers["ident"][:, :], in_=dram["ident"][:, :]), 16))
    VE.batch(lambda v: (
        vst.inc(v.memset(pers["ones_col"][:, :], 1.0)),
        vst.inc(v.memset(pers["ones_row"][:, :], 1.0)),
        vst.inc(v.memset(pers["magic_b"][:, :], MAGIC)),
        vst.inc(v.memset(pers["eps_b"][:, :], EPS)),
        vst.inc(v.memset(acc[:, :], 0.0)),
    ))
    ones_cnt = vst.n

    items = ([(0, r, 0) for r in range(NRG)] +
             [(1, r, 0) for r in range(NRG)] +
             [(2, r, cc) for r in range(NRD) for cc in range(I // D)])
    slices = [dram["wg_sl"], dram["wu_sl"], dram["wd_sl"]]
    NIT = len(items)
    accw = sbq("p_accw", [128, NIT + 2], F32)

    red_cnt = {}
    wld_cnt = {}
    for j, (wi, r, cc) in enumerate(items):
        par = j % 2

        def g1(g, j=j, wi=wi, r=r, cc=cc, par=par):
            if j >= 2:
                g.wait_ge(vst.h, red_cnt[j - 2])
            wld_cnt[j] = wldp[par].inc(g.dma_start(
                out=row[par][:, :],
                in_=slices[wi][r * 128:(r + 1) * 128, cc * D:(cc + 1) * D]), 16)
        GE.batch(g1)

        def v1(v, j=j, par=par):
            v.wait_ge(wldp[par].h, wld_cnt[j])
            red_cnt[j] = vst.inc(
                v.tensor_reduce(accw[:, j:j + 1], row[par][:, :],
                                AX.X, ALU.add, apply_absolute_value=True))
        VE.batch(v1)

    off = 0
    for wi, n in enumerate([NRG, NRG, NRD * (I // D)]):
        def v2(v, wi=wi, off=off, n=n):
            v.wait_ge(vst.h, red_cnt[NIT - 1])
            vst.inc(v.tensor_reduce(acc[:, wi:wi + 1], accw[:, off:off + n],
                                    AX.X, ALU.add))
        VE.batch(v2)
        off += n

    TE.batch(lambda t: (
        t.wait_ge(vst.h, vst.n),
        tst.inc(t.matmul(psb[0:1, 0:4], pers["ones_col"][:, 0:1], acc[:, 0:4],
                         start=True, stop=True)),
    ))
    VE.batch(lambda v: (
        v.wait_ge(tst.h, tst.n),
        vst.inc(v.tensor_copy(sums[:, :], psb[0:1, 0:4])),
    ))
    YE.batch(lambda sy: (
        sy.wait_ge(vst.h, vst.n),
        sumd.inc(sy.dma_start(out=dram["ar_in"][:, :], in_=sums[:, :]), 16),
    ))
    GE.batch(lambda g: (
        g.wait_ge(sumd.h, 16),
        ccs.inc(g.collective_compute(
            "AllReduce", ALU.add, replica_groups=[list(range(NC))],
            ins=[dram["ar_in"].ap().opt()], outs=[dram["ar_out"].ap().opt()])),
    ))
    GE.batch(lambda g: (
        g.wait_ge(ccs.h, 1),
        sumd2.inc(g.dma_start(out=sums[:, :], in_=dram["ar_out"][:, :]), 16),
    ))
    VE.batch(lambda v: (
        v.wait_ge(sumd2.h, 16),
        vst.inc(v.tensor_scalar(wsrow[:, :], sums[:, :], 1.0 / NW, EPS,
                                ALU.mult, ALU.add)),
    ))
    VE.batch(lambda v: (
        v.wait_ge(vst.h, vst.n),
        vst.inc(v.reciprocal(winvrow[:, :], wsrow[:, :])),
    ))
    YE.batch(lambda sy: fld.inc(
        sy.dma_start(out=pers["flag_sb"][:, :], in_=dram["flag"][:, :]), 16))
    TE.batch(lambda t: (
        t.wait_ge(vst.h, vst.n),
        t.wait_ge(fld.h, 16),
        tst.inc(t.matmul(psb[:, 4:8], pers["ones_row"][0:1, 0:128],
                         wsrow[0:1, 0:4], start=True, stop=True)),
        tst.inc(t.matmul(psb[:, 8:12], pers["ones_row"][0:1, 0:128],
                         winvrow[0:1, 0:4], start=True, stop=True)),
        tst.inc(t.matmul(psb[:, 12:13], pers["ones_row"][0:1, 0:128],
                         pers["flag_sb"][0:1, 0:1], start=True, stop=True)),
    ))
    VE.batch(lambda v: (
        v.wait_ge(tst.h, tst.n),
        vst.inc(v.tensor_copy(pers["ws_b"][:, :], psb[:, 4:8])),
        vst.inc(v.tensor_copy(pers["winv_b"][:, :], psb[:, 8:12])),
        vst.inc(v.tensor_copy(pers["flag_b"][:, :], psb[:, 12:13])),
    ))
    bcast_cnt = vst.n

    r1_cnt = {}
    q_cnt = {}
    ev_hist = []
    tr_hist = []

    gidx = 0
    for j, (wi, r, cc) in enumerate(items):
        par = j % 2
        ngr = 2

        def g2(g, j=j, wi=wi, r=r, cc=cc, par=par):
            if j >= 2:
                g.wait_ge(sst.h, r1_cnt[j - 2])
            wld_cnt[j] = wldp[par].inc(g.dma_start(
                out=row[par][:, :],
                in_=slices[wi][r * 128:(r + 1) * 128, cc * D:(cc + 1) * D]), 16)
        GE.batch(g2)

        def s2(s, j=j, wi=wi, par=par):
            s.wait_ge(wldp[par].h, wld_cnt[j])
            s.wait_ge(vst.h, bcast_cnt)
            if j >= 1:
                s.wait_ge(vst.h, q_cnt[j - 1][0])
            r1_cnt[j] = sst.inc(s.activation(
                rt[:, :], row[par][:, :], AF.Identity,
                bias=pers["magic_b"][:, 0:1], scale=pers["winv_b"][:, wi:wi + 1]))
        SE.batch(s2)

        def v3(v, j=j):
            v.wait_ge(sst.h, r1_cnt[j])
            if j >= 1:
                v.wait_ge(vst.h, q_cnt[j - 1][1])
            a = vst.inc(v.tensor_scalar(q1[:, :], rt[:, :], -MAGIC,
                                        -1.0, ALU.add, ALU.max))
            v.wait_ge(vst.h, a)
            if tr_hist:
                v.wait_ge(tst.h, tr_hist[-1])
            bcnt = vst.inc(v.tensor_scalar(qrow[:, :], q1[:, :],
                                           1.0, None, ALU.min))
            q_cnt[j] = (a, bcnt)
        VE.batch(v3)

        dst_sb, rowg = [(wgt_sb, ROWG), (wut_sb, ROWG), (wdt_sb, ROWD)][wi]
        for gi in range(ngr):
            def t4(t, j=j, gi=gi, gidx=gidx):
                t.wait_ge(vst.h, q_cnt[j][1])
                t.wait_ge(idl.h, 16)
                if gidx >= 2:
                    t.wait_ge(est.h, ev_hist[gidx - 2])
                last = None
                for k in range(8):
                    kk = gi * 8 + k
                    last = t.matmul(pst[gidx % 2][:, k * 128:(k + 1) * 128],
                                    qrow[:, kk * 128:(kk + 1) * 128],
                                    pers["ident"][:, :], is_transpose=True,
                                    start=True, stop=True)
                tr_hist.append(tst.inc(last))
            TE.batch(t4)

            def v4(v, gi=gi, gidx=gidx, dst_sb=dst_sb, rowg=rowg, r=r, cc=cc):
                v.wait_ge(tst.h, tr_hist[gidx])
                out_ap = dst_sb[:].rearrange("p (k c) -> p k c", c=rowg)[
                    :, cc * KC + gi * 8: cc * KC + (gi + 1) * 8,
                    r * 128:(r + 1) * 128]
                ev_hist.append(est.inc(v.tensor_copy(out_ap, pst[gidx % 2][:, :])))
            VE.batch(v4)
            gidx += 1

    SZ = D * I // NC

    def g3(g):
        g.wait_ge(est.h, est.n)
        for wi, (src_sb, rowg) in enumerate(((wgt_sb, ROWG), (wut_sb, ROWG),
                                             (wdt_sb, ROWD))):
            nkc = (D if wi < 2 else I) // 128
            src = src_sb[:].rearrange("p (k c) -> p k c", c=rowg)
            dstap = bass.AP(dram["agw_in"], wi * SZ,
                            [[rowg, 128], [128 * rowg, nkc], [1, rowg]])
            outd.inc(g.dma_start(out=dstap, in_=src), 16)
    GE.batch(g3)

    GE.batch(lambda g: (
        g.wait_ge(outd.h, outd.n),
        ccs.inc(g.collective_compute(
            "AllGather", ALU.bypass, replica_groups=[list(range(NC))],
            ins=[dram["agw_in"].ap().opt()],
            outs=[dram["agw"].ap().opt()])),
    ))
    GE.batch(lambda g: g.wait_ge(ccs.h, ccs.n))
    C.pro_outd = (outd, outd.n)
    C.pro_ag = (ccs, ccs.n)
    C.pro_ws = (vst, vst.n)
    es2.close()


def emit_ffn(C, dram, pers, x_src, x_dst, tag, after_prologue=False):
    nc = C.nc
    NC = C.ncores
    RCG = I // NC
    RCD = D // NC
    TE, VE, SE, GE, YE = C.T, C.V, C.S, C.G, C.Y

    aLp = [C.sem("ffn_aL0"), C.sem("ffn_aL1")]
    aV = C.sem("ffn_aV")
    aS = C.sem("ffn_aS")
    aT = C.sem("ffn_aT")
    aG = C.sem("ffn_aG")
    aE = C.sem("ffn_aE")
    bWp = [C.sem("ffn_bW0"), C.sem("ffn_bW1")]
    bT = C.sem("ffn_bT")
    bS = C.sem("ffn_bS")
    bV = C.sem("ffn_bV")
    bYp = [C.sem("ffn_bY0"), C.sem("ffn_bY1")]
    cLp = [C.sem("ffn_cL0"), C.sem("ffn_cL1")]
    cV = C.sem("ffn_cV")
    cS = C.sem("ffn_cS")
    cT = C.sem("ffn_cT")
    cE = C.sem("ffn_cE")
    cWp = [C.sem("ffn_cW0"), C.sem("ffn_cW1")]
    cXp = [C.sem("ffn_cX0"), C.sem("ffn_cX1")]
    cYp = [C.sem("ffn_cY0"), C.sem("ffn_cY1")]

    es2 = ExitStack()
    sbq = lambda n, s, d: es2.enter_context(nc.sbuf_tensor(f"{tag}{n}", s, d))
    psq = lambda n, s, d: es2.enter_context(nc.psum_tensor(f"{tag}{n}", s, d))

    bnbuf = sbq("bn", [128, 24], F32)
    mv = sbq("mv", [128, 2], F32)
    lnv_all = sbq("lnv", [128, NT], F32)
    rstd_all = sbq("rstd", [128, NT], F32)
    nmr_all = sbq("nmr", [128, NT], F32)
    m_all = sbq("m", [128, NT], F32)
    s_all = sbq("s", [128, NT], F32)
    sinv_all = sbq("sinv", [128, NT], F32)
    sgv_all = sbq("sgv", [128, NT], F32)
    suv_all = sbq("suv", [128, NT], F32)
    m2_all = sbq("m2", [128, NT], F32)
    m2p = sbq("m2p", [128, NT * NO], F32)
    s2_all = sbq("s2", [128, NT], F32)
    sinv2_all = sbq("sinv2", [128, NT], F32)
    dnv_all = sbq("dnv", [128, NT], F32)

    pb = [psq(f"pb{i}", [128, 512], F32) for i in range(4)]
    pstf = [psq(f"pst{i}", [128, 1024], F16) for i in range(2)]

    esAB = ExitStack()
    sab = lambda n, s, d: esAB.enter_context(nc.sbuf_tensor(f"{tag}{n}", s, d))
    xt = [sab(f"xt{i}", [128, D], F32) for i in range(2)]
    ln_t = [sab(f"ln{i}", [128, D], F32) for i in range(2)]
    rt1 = [sab(f"rt1{i}", [128, D], F32) for i in range(2)]
    zq_tm = [sab(f"zqtm{i}", [128, D], F16) for i in range(2)]
    zq_fm = sab("zqfm", [128, KC * T], F16)
    wgt = [sab(f"wgt{i}", [128, KC * 512], F8) for i in range(2)]
    wut = [sab(f"wut{i}", [128, KC * 512], F8) for i in range(2)]
    sg = [sab(f"sgb{i}", [128, 512], F16) for i in range(2)]
    act_st = [sab(f"acst{i}", [128, 512], F16) for i in range(2)]

    ap_cnt = {}
    bn_cnt = {}
    r2_cnt = {}
    tr_cnt = {}
    ev_cnt = {}
    rm_cnt = {}
    r1_cnt = {}
    ln_cnt = {}
    nmr_cnt = {}
    al_cnt = {}

    for j in range(NT):
        par = j % 2

        def yl(sy, j=j, par=par):
            if after_prologue and j < 2:
                sy.wait_ge(C.pro_outd[0].h, C.pro_outd[1])
            if j >= 2:
                sy.wait_ge(aV.h, bn_cnt[j - 2])
                sy.wait_ge(aS.h, ap_cnt[j - 2])
            al_cnt[j] = aLp[par].inc(sy.dma_start(
                out=xt[par][:, :], in_=x_src[j * 128:(j + 1) * 128, :]), 16)
        YE.batch(yl)

        def v_stats(v, j=j, par=par):
            v.wait_ge(aLp[par].h, al_cnt[j])
            if j >= 1:
                v.wait_ge(aS.h, ln_cnt[j - 1])
                v.wait_ge(aV.h, nmr_cnt[j - 1])
            for q in range(4):
                aV.inc(v.bn_stats(bnbuf[:, q * 6:(q + 1) * 6],
                                  xt[par][:, q * 512:(q + 1) * 512]))
            v.wait_ge(aV.h, aV.n)
            bn_cnt[j] = aV.inc(v.bn_aggr(mv[:, :], bnbuf[:, :]))
        VE.batch(v_stats)

        def s_rstd(s, j=j):
            s.wait_ge(aV.h, bn_cnt[j])
            ln_cnt[j] = aS.inc(s.activation(lnv_all[:, j:j + 1], mv[:, 1:2],
                                            AF.Ln, bias=pers["eps_b"][:, 0:1], scale=1.0))
            s.wait_ge(aS.h, aS.n)
            aS.inc(s.activation(rstd_all[:, j:j + 1], lnv_all[:, j:j + 1],
                                AF.Exp, bias=0.0, scale=-0.5))
        SE.batch(s_rstd)

        def v_nmr(v, j=j):
            v.wait_ge(aS.h, aS.n)
            nmr_cnt[j] = aV.inc(v.tensor_scalar(
                nmr_all[:, j:j + 1], mv[:, 0:1],
                rstd_all[:, j:j + 1], -1.0, ALU.mult, ALU.mult))
        VE.batch(v_nmr)

        def s_apply(s, j=j, par=par):
            s.wait_ge(aV.h, aV.n)
            if j >= 2:
                s.wait_ge(aV.h, rm_cnt[j - 2])
                s.wait_ge(aS.h, r1_cnt[j - 2])
            ap_cnt[j] = aS.inc(s.activation(
                ln_t[par][:, :], xt[par][:, :], AF.Identity,
                bias=nmr_all[:, j:j + 1], scale=rstd_all[:, j:j + 1]))
        SE.batch(s_apply)

        def v_quant(v, j=j, par=par):
            v.wait_ge(aS.h, ap_cnt[j])
            rm_cnt[j] = aV.inc(v.tensor_reduce(
                m_all[:, j:j + 1], ln_t[par][:, :], AX.X, ALU.max,
                apply_absolute_value=True))
            v.wait_ge(aV.h, aV.n)
            aV.inc(v.tensor_scalar(s_all[:, j:j + 1], m_all[:, j:j + 1],
                                   INV127, EPS, ALU.mult, ALU.add))
            v.wait_ge(aV.h, aV.n)
            aV.inc(v.reciprocal(sinv_all[:, j:j + 1], s_all[:, j:j + 1]))
        VE.batch(v_quant)

        def s_round1(s, j=j, par=par):
            s.wait_ge(aV.h, aV.n)
            if j >= 2:
                s.wait_ge(aG.h, r2_cnt[j - 2])
            r1_cnt[j] = aS.inc(s.activation(
                rt1[par][:, :], ln_t[par][:, :], AF.Identity,
                bias=pers["magic_b"][:, 0:1], scale=sinv_all[:, j:j + 1]))
        SE.batch(s_round1)

        def g_round2(g, j=j, par=par):
            g.wait_ge(aS.h, r1_cnt[j])
            if j >= 2:
                g.wait_ge(aT.h, tr_cnt[(j - 2, 1)])
            r2_cnt[j] = aG.inc(g.tensor_scalar(
                zq_tm[par][:, :], rt1[par][:, :], -MAGIC, None, ALU.add))
        GE.batch(g_round2)

        for gi in range(2):
            def t_tr(t, j=j, gi=gi, par=par):
                t.wait_ge(aG.h, r2_cnt[j])
                if after_prologue and j < 1:
                    t.wait_ge(C.pro_outd[0].h, C.pro_outd[1])
                gl = j * 2 + gi
                if gl >= 2:
                    jj, gg = divmod(gl - 2, 2)
                    t.wait_ge(aE.h, ev_cnt[(jj, gg)])
                last = None
                for k in range(8):
                    kk = gi * 8 + k
                    last = t.matmul(pstf[gi][:, k * 128:(k + 1) * 128],
                                    zq_tm[par][:, kk * 128:(kk + 1) * 128],
                                    pers["ident"][:, :], is_transpose=True,
                                    start=True, stop=True)
                tr_cnt[(j, gi)] = aT.inc(last)
            TE.batch(t_tr)

            def v_ev(v, j=j, gi=gi):
                v.wait_ge(aT.h, tr_cnt[(j, gi)])
                out_ap = zq_fm[:].rearrange("p (k c) -> p k c", c=T)[
                    :, gi * 8:(gi + 1) * 8, j * 128:(j + 1) * 128]
                ev_cnt[(j, gi)] = aE.inc(v.tensor_copy(out_ap, pstf[gi][:, :]))
            VE.batch(v_ev)

    def v_sgv(v):
        v.wait_ge(aV.h, aV.n)
        if after_prologue:
            v.wait_ge(C.pro_ws[0].h, C.pro_ws[1])
        aV.inc(v.tensor_scalar(sgv_all[:, :], s_all[:, :],
                               pers["ws_b"][:, 0:1], None, ALU.mult))
        aV.inc(v.tensor_scalar(suv_all[:, :], s_all[:, :],
                               pers["ws_b"][:, 1:2], None, ALU.mult))
    VE.batch(v_sgv)
    sgv_cnt = aV.n

    SZ = D * I // NC

    def wtile_ap(wsel, ot):
        rank_cols = RCG
        o0 = ot * 512
        rank = o0 // rank_cols
        col = o0 % rank_cols
        base = rank * (3 * SZ) + wsel * SZ + col
        return bass.AP(dram["agw"], base,
                       [[rank_cols, 128], [128 * rank_cols, KC], [1, 512]])

    mmB = {}
    silB = {}
    sttB = {}
    stB = {}
    wldB = {}

    for ot in range(NO):
        def g_w(g, ot=ot):
            par = ot % 2
            if after_prologue and ot < 2:
                g.wait_ge(C.pro_ag[0].h, C.pro_ag[1])
            if ot >= 2:
                g.wait_ge(bT.h, mmB[(ot - 2, NT - 1)])
            bWp[par].inc(g.dma_start(
                out=wgt[par][:].rearrange("p (k c) -> p k c", c=512),
                in_=wtile_ap(0, ot)), 16)
            bWp[par].inc(g.dma_start(
                out=wut[par][:].rearrange("p (k c) -> p k c", c=512),
                in_=wtile_ap(1, ot)), 16)
            wldB[ot] = bWp[par].n
        GE.batch(g_w)

        for j in range(NT):
            par = ot % 2
            jp = j % 2
            lin = ot * NT + j

            def t_mm(t, ot=ot, j=j, par=par, jp=jp, lin=lin):
                t.wait_ge(bWp[par].h, wldB[ot])
                if lin == 0:
                    t.wait_ge(aE.h, aE.n)
                if lin >= 2:
                    oo, jj = divmod(lin - 2, NT)
                    t.wait_ge(bS.h, silB[(oo, jj)])
                    t.wait_ge(bV.h, sttB[(oo, jj)])
                last = None
                for k in range(KC):
                    lhs = zq_fm[:, k * T + j * 128: k * T + (j + 1) * 128]
                    t.matmul(pb[jp][:, :], lhs,
                             wgt[par][:, k * 512:(k + 1) * 512],
                             start=(k == 0), stop=(k == KC - 1))
                    last = t.matmul(pb[2 + jp][:, :], lhs,
                                    wut[par][:, k * 512:(k + 1) * 512],
                                    start=(k == 0), stop=(k == KC - 1))
                mmB[(ot, j)] = bT.inc(last)
            TE.batch(t_mm)

            def s_sil(s, ot=ot, j=j, jp=jp, lin=lin):
                s.wait_ge(bT.h, mmB[(ot, j)])
                s.wait_ge(aV.h, sgv_cnt)
                if lin >= 2:
                    oo, jj = divmod(lin - 2, NT)
                    s.wait_ge(bV.h, sttB[(oo, jj)])
                silB[(ot, j)] = bS.inc(s.activation(
                    sg[jp][:, :], pb[jp][:, :], AF.Silu,
                    bias=0.0, scale=sgv_all[:, j:j + 1]))
            SE.batch(s_sil)

            def v_stt(v, ot=ot, j=j, jp=jp, lin=lin):
                v.wait_ge(bS.h, silB[(ot, j)])
                if lin >= 2:
                    oo, jj = divmod(lin - 2, NT)
                    v.wait_ge(bYp[jj % 2].h, stB[(oo, jj)])
                a = bV.inc(v.scalar_tensor_tensor(
                    act_st[jp][:, :], pb[2 + jp][:, :], suv_all[:, j:j + 1],
                    sg[jp][:, :], ALU.mult, ALU.mult))
                v.wait_ge(bV.h, a)
                sttB[(ot, j)] = bV.inc(v.tensor_reduce(
                    m2p[:, j * NO + ot:j * NO + ot + 1], act_st[jp][:, :],
                    AX.X, ALU.max, apply_absolute_value=True))
            VE.batch(v_stt)

            def y_st(sy, ot=ot, j=j, jp=jp):
                sy.wait_ge(bV.h, sttB[(ot, j)])
                stB[(ot, j)] = bYp[jp].inc(sy.dma_start(
                    out=dram["acts"][j * 128:(j + 1) * 128,
                                     ot * 512:(ot + 1) * 512],
                    in_=act_st[jp][:, :]), 16)
            YE.batch(y_st)

    esAB.close()
    bY_fin = (bYp[0].n, bYp[1].n)
    C.barrier()

    aE_fin = aE.n
    esC = ExitStack()
    scb = lambda n, s, d: esC.enter_context(nc.sbuf_tensor(f"{tag}{n}", s, d))
    act_row = [scb(f"actr{i}", [128, I], F16) for i in range(2)]
    rt2 = [scb(f"rt2{i}", [128, I // 2], F32) for i in range(2)]
    aq_tm = [scb(f"aqtm{i}", [128, I], F16) for i in range(2)]
    aq_fm = scb("aqfm", [128, IC * 512], F16)
    wdt = [scb(f"wdtb{i}", [128, 8 * 512], F8) for i in range(2)]
    td = [scb(f"td{i}", [128, 512], F32) for i in range(2)]
    xc = [scb(f"xcb{i}", [128, 512], F32) for i in range(2)]
    xadd = [scb(f"xad{i}", [128, 512], F32) for i in range(2)]

    ldC = {}
    rmC = {}
    dnvC = {}
    r1C = {}
    r2C = {}
    trC = {}
    evC = {}
    wldC = {}
    mmC = {}
    evacC = {}
    addC = {}
    xldC = {}
    stC = {}

    for p in range(4):
        for jj in range(4):
            j = p * 4 + jj
            par = j % 2

            def y_act(sy, j=j, par=par):
                if j >= 2:
                    sy.wait_ge(cV.h, rmC[j - 2])
                    sy.wait_ge(cS.h, r1C[(j - 2, 1)])
                sy.wait_ge(bYp[0].h, bY_fin[0])
                sy.wait_ge(bYp[1].h, bY_fin[1])
                ldC[j] = cLp[par].inc(sy.dma_start(
                    out=act_row[par][:, :],
                    in_=dram["acts"][j * 128:(j + 1) * 128, :]), 16)
            YE.batch(y_act)

            def v_q(v, j=j, par=par):
                v.wait_ge(cLp[par].h, ldC[j])
                rmC[j] = cV.inc(v.tensor_reduce(
                    m2_all[:, j:j + 1], m2p[:, j * NO:(j + 1) * NO], AX.X,
                    ALU.max))
                v.wait_ge(cV.h, cV.n)
                cV.inc(v.tensor_scalar(s2_all[:, j:j + 1], m2_all[:, j:j + 1],
                                       INV127, EPS, ALU.mult, ALU.add))
                v.wait_ge(cV.h, cV.n)
                cV.inc(v.reciprocal(sinv2_all[:, j:j + 1], s2_all[:, j:j + 1]))
                v.wait_ge(cV.h, cV.n)
                dnvC[j] = cV.inc(v.tensor_scalar(
                    dnv_all[:, j:j + 1], s2_all[:, j:j + 1],
                    pers["ws_b"][:, 2:3], 0.5, ALU.mult, ALU.mult))
            VE.batch(v_q)

            for half in range(2):
                def s_r1(s, j=j, par=par, half=half):
                    s.wait_ge(cV.h, dnvC[j])
                    if j >= 1:
                        s.wait_ge(cV.h, r2C[(j - 1, half)])
                    r1C[(j, half)] = cS.inc(s.activation(
                        rt2[half][:, :],
                        act_row[par][:, half * (I // 2):(half + 1) * (I // 2)],
                        AF.Identity, bias=pers["magic_b"][:, 0:1],
                        scale=sinv2_all[:, j:j + 1]))
                SE.batch(s_r1)

                def v_r2(v, j=j, par=par, half=half):
                    v.wait_ge(cS.h, r1C[(j, half)])
                    if j >= 2 and half == 0:
                        v.wait_ge(cT.h, trC[(j - 2, 7)])
                    r2C[(j, half)] = cV.inc(v.tensor_scalar(
                        aq_tm[par][:, half * (I // 2):(half + 1) * (I // 2)],
                        rt2[half][:, :], -MAGIC, None, ALU.add))
                VE.batch(v_r2)
            for gi in range(8):
                def t_tr(t, j=j, jj=jj, gi=gi, par=par, p=p):
                    t.wait_ge(cV.h, r2C[(j, 1)] if gi >= 4 else r2C[(j, 0)])
                    gl = j * 8 + gi
                    if gl >= 2:
                        pjj, pg = divmod(gl - 2, 8)
                        t.wait_ge(cE.h, evC[(pjj, pg)])
                    if jj == 0 and gi == 0 and p >= 1:
                        t.wait_ge(cT.h, mmC[(p - 1, 3, 7)])
                    last = None
                    for k in range(8):
                        kk = gi * 8 + k
                        last = t.matmul(pstf[gi % 2][:, k * 128:(k + 1) * 128],
                                        aq_tm[par][:, kk * 128:(kk + 1) * 128],
                                        pers["ident"][:, :], is_transpose=True,
                                        start=True, stop=True)
                    trC[(j, gi)] = cT.inc(last)
                TE.batch(t_tr)

                def v_ev(v, j=j, jj=jj, gi=gi):
                    v.wait_ge(cT.h, trC[(j, gi)])
                    out_ap = aq_fm[:].rearrange("p (k c) -> p k c", c=512)[
                        :, gi * 8:(gi + 1) * 8, jj * 128:(jj + 1) * 128]
                    evC[(j, gi)] = cE.inc(v.tensor_copy(out_ap,
                                                        pstf[gi % 2][:, :]))
                VE.batch(v_ev)

        for h in range(4):
            for kg in range(8):
                def g_wd(g, p=p, h=h, kg=kg):
                    par = kg % 2
                    lin = h * 8 + kg
                    if lin >= 2:
                        hh, kk = divmod(lin - 2, 8)
                        g.wait_ge(cT.h, mmC[(p, hh, kk)])
                    elif p >= 1:
                        g.wait_ge(cT.h, mmC[(p - 1, 3, 6 + lin)])
                    o0 = h * 512
                    nr = 512 // RCD
                    SZl = D * I // NC
                    for ri in range(nr):
                        rank = o0 // RCD + ri
                        base = rank * (3 * SZl) + 2 * SZl
                        src = bass.AP(dram["agw"], base + (kg * 8 * 128) * RCD,
                                      [[RCD, 128], [128 * RCD, 8], [1, RCD]])
                        dst = wdt[par][:].rearrange("p (k c) -> p k c", c=512)[
                            :, :, ri * RCD:(ri + 1) * RCD]
                        cWp[par].inc(g.dma_start(out=dst, in_=src), 16)
                    wldC[(p, h, kg)] = cWp[par].n
                GE.batch(g_wd)

                def t_dn(t, p=p, h=h, kg=kg):
                    par = kg % 2
                    t.wait_ge(cWp[par].h, wldC[(p, h, kg)])
                    if kg == 0 and h >= 1:
                        for jj in range(4):
                            t.wait_ge(cS.h, evacC[(p, h - 1, jj)])
                    last = None
                    for jj in range(4):
                        if kg == 0 and h == 0:
                            t.wait_ge(cE.h, evC[(p * 4 + jj, 7)])
                        for k in range(8):
                            kk = kg * 8 + k
                            last = t.matmul(
                                pb[jj][:, :],
                                aq_fm[:, kk * 512 + jj * 128:
                                      kk * 512 + (jj + 1) * 128],
                                wdt[par][:, k * 512:(k + 1) * 512],
                                start=(kg == 0 and k == 0),
                                stop=(kg == 7 and k == 7))
                    mmC[(p, h, kg)] = cT.inc(last)
                TE.batch(t_dn)

            for jj in range(4):
                j = p * 4 + jj
                lin = (p * 4 + h) * 4 + jj

                def y_x(sy, j=j, h=h, lin=lin):
                    if lin >= 2:
                        sy.wait_ge(cV.h, addC[lin - 2])
                    xldC[lin] = cXp[lin % 2].inc(sy.dma_start(
                        out=xc[lin % 2][:, :],
                        in_=x_src[j * 128:(j + 1) * 128,
                                  h * 512:(h + 1) * 512]), 16)
                YE.batch(y_x)

                def s_ev(s, p=p, h=h, jj=jj, j=j, lin=lin):
                    s.wait_ge(cT.h, mmC[(p, h, 7)])
                    s.wait_ge(cV.h, dnvC[j])
                    if lin >= 2:
                        s.wait_ge(cV.h, addC[lin - 2])
                    evacC[(p, h, jj)] = cS.inc(s.activation(
                        td[lin % 2][:, :], pb[jj][:, :], AF.Identity,
                        bias=0.0, scale=dnv_all[:, j:j + 1]))
                SE.batch(s_ev)

                def v_add(v, p=p, h=h, jj=jj, lin=lin):
                    v.wait_ge(cS.h, evacC[(p, h, jj)])
                    v.wait_ge(cXp[lin % 2].h, xldC[lin])
                    if lin >= 2:
                        v.wait_ge(cYp[lin % 2].h, stC[lin - 2])
                    addC[lin] = cV.inc(v.tensor_tensor(
                        xadd[lin % 2][:, :], td[lin % 2][:, :],
                        xc[lin % 2][:, :], ALU.add))
                VE.batch(v_add)

                def y_st(sy, p=p, h=h, jj=jj, j=j, lin=lin):
                    sy.wait_ge(cV.h, addC[lin])
                    stC[lin] = cYp[lin % 2].inc(sy.dma_start(
                        out=x_dst[j * 128:(j + 1) * 128,
                                  h * 512:(h + 1) * 512],
                        in_=xadd[lin % 2][:, :]), 16)
                YE.batch(y_st)

    YE.batch(lambda sy: (sy.wait_ge(cYp[0].h, cYp[0].n),
                         sy.wait_ge(cYp[1].h, cYp[1].n)))
    esC.close()
    es2.close()


def build_nc(ncores=8, upto="all"):
    nc = bass.Bass("TRN2", target_bir_lowering=False)
    NC = ncores
    ROWG = I // NC
    ROWD = D // NC

    dram = {}
    dram["x_sl"] = nc.dram_tensor("x_sl", [T, D], F32, kind="ExternalInput")
    for nm in ("wq", "wk", "wv", "wgp", "wo"):
        dram[nm] = nc.dram_tensor(nm, [D, D], F32, kind="ExternalInput")
    dram["w1"] = nc.dram_tensor("w1", [D, HID], F32, kind="ExternalInput")
    dram["w2"] = nc.dram_tensor("w2", [D, HID], F32, kind="ExternalInput")
    dram["w3"] = nc.dram_tensor("w3", [HID, D], F32, kind="ExternalInput")
    dram["wg_sl"] = nc.dram_tensor("wg_sl", [ROWG, D], F32, kind="ExternalInput")
    dram["wu_sl"] = nc.dram_tensor("wu_sl", [ROWG, D], F32, kind="ExternalInput")
    dram["wd_sl"] = nc.dram_tensor("wd_sl", [ROWD, I], F32, kind="ExternalInput")
    for nm in ("n1w", "n1b", "n2w", "n2b"):
        dram[nm] = nc.dram_tensor(nm, [1, D], F32, kind="ExternalInput")
    dram["ident"] = nc.dram_tensor("ident", [128, 128], F16, kind="ExternalInput")
    dram["dmask"] = nc.dram_tensor("dmask", [128, NH * 128], F16, kind="ExternalInput")
    dram["qdec"] = nc.dram_tensor("qdec", [128, NH * 128], F16, kind="ExternalInput")
    dram["kdec"] = nc.dram_tensor("kdec", [128, NH * 128], F16, kind="ExternalInput")
    dram["flag"] = nc.dram_tensor("flag", [1, 1], F32, kind="ExternalInput")
    dram["out"] = nc.dram_tensor("out", [T, D], F32, kind="ExternalOutput")

    dram["x1s"] = nc.dram_tensor("x1s", [T, D], F32)
    dram["x2s"] = nc.dram_tensor("x2s", [T, D], F32)
    dram["xas"] = nc.dram_tensor("xas", [T, D], F32)
    dram["xbs"] = nc.dram_tensor("xbs", [T, D], F32)
    dram["acts"] = nc.dram_tensor("acts", [T, I], F16)
    dram["retgns"] = nc.dram_tensor("retgns", [T, D], F16)
    dram["stashs"] = nc.dram_tensor("stashs", [T, D], F16)
    dram["hs"] = nc.dram_tensor("hs", [D, T], F16)
    dram["act3s"] = nc.dram_tensor("act3s", [43 * 128, T], F16)
    dram["ar_in"] = nc.dram_tensor("ar_in", [1, 4], F32)
    dram["ar_out"] = nc.dram_tensor("ar_out", [1, 4], F32)
    SZ = D * I // NC
    dram["agw_in"] = nc.dram_tensor("agw_in", [3 * SZ], F8)
    dram["agw"] = nc.dram_tensor("agw", [NC * 3 * SZ], F8, addr_space="Shared")
    dram["st_in"] = [nc.dram_tensor(f"st_in{i}", [8 * 128, 128], F16)
                     for i in range(2)]
    dram["st_out"] = [nc.dram_tensor(f"st_out{i}", [2 * 8 * 128, 128], F16)
                      for i in range(2)]

    es = ExitStack()
    C = Ctx(nc, es, ncores)
    pers = {}
    pers["ident"] = C.sb("ident_sb", [128, 128], F16)
    pers["ones_col"] = C.sb("ones_col", [128, 1], F32)
    pers["ones_row"] = C.sb("ones_row", [1, 512], F32)
    pers["ws_b"] = C.sb("ws_b", [128, 4], F32)
    pers["winv_b"] = C.sb("winv_b", [128, 4], F32)
    pers["flag_sb"] = C.sb("flag_sb", [1, 1], F32)
    pers["flag_b"] = C.sb("flag_b", [128, 1], F32)
    pers["magic_b"] = C.sb("magic_b", [128, 1], F32)
    pers["eps_b"] = C.sb("eps_b", [128, 1], F32)

    block = es.enter_context(nc.Block())
    C.T = Eng(block, "tensor")
    C.V = Eng(block, "vector")
    C.S = Eng(block, "scalar")
    C.G = Eng(block, "gpsimd")
    C.Y = Eng(block, "sync")

    emit_prologue(C, dram, pers)
    if upto in ("ffn1", "attn", "all"):
        emit_ffn(C, dram, pers, dram["x_sl"], dram["x1s"], "f1",
                 after_prologue=True)
        C.barrier()
    if upto in ("attn", "all"):
        emit_attn(C, dram, pers)
        C.barrier()
    if upto == "all":
        emit_ffn(C, dram, pers, dram["x2s"], dram["out"], "f2")

    es.close()
    return nc


def emit_attn(C, dram, pers):
    nc = C.nc
    TE, VE, SE, GE, YE = C.T, C.V, C.S, C.G, C.Y
    esA = ExitStack()
    sba = lambda n, s, d: esA.enter_context(nc.sbuf_tensor(n, s, d))
    psa = lambda n, s, d: esA.enter_context(nc.psum_tensor(n, s, d))

    dmask_sb = sba("at_dmask", [128, NH * 128], F16)
    qdec_sb = sba("at_qdec", [128, NH * 128], F16)
    kdec_sb = sba("at_kdec", [128, NH * 128], F16)
    n1w_t = sba("at_n1wt", [128, KC], F32)
    n1b_t = sba("at_n1bt", [128, KC], F16)
    n2w_t = sba("at_n2wt", [128, KC], F32)
    n2b_t = sba("at_n2bt", [128, KC], F16)
    pa = [psa(f"at_pa{i}", [128, 512], F32) for i in range(6)]
    pkt = [psa(f"at_pkt{i}", [128, 1024], F16) for i in range(2)]

    cst = C.sem("at_cst")
    YE.batch(lambda sy: (
        cst.inc(sy.dma_start(out=dmask_sb[:, :], in_=dram["dmask"][:, :]), 16),
        cst.inc(sy.dma_start(out=qdec_sb[:, :], in_=dram["qdec"][:, :]), 16),
        cst.inc(sy.dma_start(out=kdec_sb[:, :], in_=dram["kdec"][:, :]), 16),
        cst.inc(sy.dma_start(
            out=n1w_t[:, :],
            in_=dram["n1w"].ap().rearrange("o (c p) -> p (o c)", p=128),
            allow_slow_non_contiguous=True), 16),
        cst.inc(sy.dma_start(
            out=n2w_t[:, :],
            in_=dram["n2w"].ap().rearrange("o (c p) -> p (o c)", p=128),
            allow_slow_non_contiguous=True), 16),
    ))
    cstg = C.sem("at_cstg")
    GE.batch(lambda g_: (
        cstg.inc(g_.dma_start(
            out=n1b_t[:, :],
            in_=dram["n1b"].ap().rearrange("o (c p) -> p (o c)", p=128),
            allow_slow_non_contiguous=True), 16),
        cstg.inc(g_.dma_start(
            out=n2b_t[:, :],
            in_=dram["n2b"].ap().rearrange("o (c p) -> p (o c)", p=128),
            allow_slow_non_contiguous=True), 16),
    ))

    aHs = C.sem("at_aHs")

    def build_norm_fm(tag2, src_dram, wrow_dram, wt_unused, store_xa, hs_dst):
        esH = ExitStack()
        h_fm = esH.enter_context(nc.sbuf_tensor(f"{tag2}hfm", [128, KC * T], F16, side="right"))
        esHt = ExitStack()
        sbh = lambda n, s, d: esHt.enter_context(nc.sbuf_tensor(n, s, d))
        xt = [sbh(f"{tag2}xt{i}", [128, D], F32) for i in range(2)]
        xa_t = [sbh(f"{tag2}xat{i}", [128, D], F32) for i in range(2)]
        t2 = [sbh(f"{tag2}t2{i}", [128, D], F32) for i in range(2)]
        t3 = [sbh(f"{tag2}t3{i}", [128, D], F16) for i in range(2)]
        nwb = sbh(f"{tag2}nwb", [128, D], F32)
        bnb = sbh(f"{tag2}bnb", [128, 24], F32)
        mv2 = sbh(f"{tag2}mv", [128, 4], F32)
        vecs = sbh(f"{tag2}vecs", [128, 6], F32)

        aL = [C.sem("bn_aL0"), C.sem("bn_aL1")]
        aV = C.sem("bn_aV")
        aS = C.sem("bn_aS")
        aT = C.sem("bn_aT")
        aE = C.sem("bn_aE")
        aXa = [C.sem("bn_aXa0"), C.sem("bn_aXa1")]
        aG2 = C.sem("bn_aG2")
        aWr = C.sem("bn_aWr")
        wrow = sbh(f"{tag2}wrow", [1, D], F32)
        YE.batch(lambda sy: aWr.inc(
            sy.dma_start(out=wrow[:, :], in_=wrow_dram[:, :]), 16))
        wrc = aWr.n

        TE.batch(lambda t: (
            t.wait_ge(cst.h, cst.n),
            t.wait_ge(aWr.h, wrc),
            [aT.inc(t.matmul(pa[i][:, :], pers["ones_row"][0:1, 0:128],
                             wrow[0:1, i * 512:(i + 1) * 512],
                             start=True, stop=True)) for i in range(4)],
        ))
        VE.batch(lambda v: (
            v.wait_ge(aT.h, aT.n),
            [aV.inc(v.tensor_copy(nwb[:, i * 512:(i + 1) * 512], pa[i][:, :]))
             for i in range(4)],
        ))
        nwb_cnt = aV.n
        bnc = {}
        bnc2 = {}
        xastc = {}
        apc = {}
        ap2c = {}
        t3c = {}
        trc = {}
        evc = {}
        alc = {}
        for j in range(NT):
            par = j % 2

            def yl(sy, j=j, par=par):
                if j >= 2:
                    sy.wait_ge(aV.h, bnc[j - 2])
                    sy.wait_ge(aS.h, apc[j - 2])
                alc[j] = aL[par].inc(sy.dma_start(
                    out=xt[par][:, :],
                    in_=src_dram[j * 128:(j + 1) * 128, :]), 16)
            YE.batch(yl)

            def v_s1(v, j=j, par=par):
                v.wait_ge(aL[par].h, alc[j])
                if j >= 1:
                    v.wait_ge(aS.h, apc[j - 1])
                    v.wait_ge(aG2.h, t3c.get(j - 1, 0))
                for q in range(4):
                    aV.inc(v.bn_stats(bnb[:, q * 6:(q + 1) * 6],
                                      xt[par][:, q * 512:(q + 1) * 512]))
                v.wait_ge(aV.h, aV.n)
                bnc[j] = aV.inc(v.bn_aggr(mv2[:, 0:2], bnb[:, :]))
            VE.batch(v_s1)

            def s_r1(s, j=j):
                s.wait_ge(aV.h, aV.n)
                aS.inc(s.activation(vecs[:, 4:5], mv2[:, 1:2], AF.Ln,
                                    bias=pers["eps_b"][:, 0:1], scale=1.0))
                s.wait_ge(aS.h, aS.n)
                aS.inc(s.activation(vecs[:, 0:1], vecs[:, 4:5], AF.Exp,
                                    bias=0.0, scale=-0.5))
            SE.batch(s_r1)

            def v_n1(v, j=j):
                v.wait_ge(aS.h, aS.n)
                aV.inc(v.tensor_scalar(vecs[:, 1:2], mv2[:, 0:1], vecs[:, 0:1],
                                       -1.0, ALU.mult, ALU.mult))
            VE.batch(v_n1)

            if store_xa:
                def s_a1(s, j=j, par=par):
                    s.wait_ge(aV.h, aV.n)
                    if j >= 2:
                        s.wait_ge(aV.h, bnc2[j - 2])
                        s.wait_ge(aXa[par].h, xastc[j - 2])
                        s.wait_ge(aS.h, ap2c[j - 2])
                    apc[j] = aS.inc(s.activation(
                        xa_t[par][:, :], xt[par][:, :], AF.Identity,
                        bias=vecs[:, 1:2], scale=vecs[:, 0:1]))
                SE.batch(s_a1)

                def y_xa(sy, j=j, par=par):
                    sy.wait_ge(aS.h, apc[j])
                    xastc[j] = aXa[par].inc(sy.dma_start(
                        out=dram["xas"][j * 128:(j + 1) * 128, :],
                        in_=xa_t[par][:, :]), 16)
                YE.batch(y_xa)

                def v_s2(v, j=j, par=par):
                    v.wait_ge(aS.h, apc[j])
                    for q in range(4):
                        aV.inc(v.bn_stats(bnb[:, q * 6:(q + 1) * 6],
                                          xa_t[par][:, q * 512:(q + 1) * 512]))
                    v.wait_ge(aV.h, aV.n)
                    bnc2[j] = aV.inc(v.bn_aggr(mv2[:, 2:4], bnb[:, :]))
                VE.batch(v_s2)

                def s_r2(s, j=j):
                    s.wait_ge(aV.h, bnc2[j])
                    aS.inc(s.activation(vecs[:, 5:6], mv2[:, 3:4], AF.Ln,
                                        bias=pers["eps_b"][:, 0:1], scale=1.0))
                    s.wait_ge(aS.h, aS.n)
                    aS.inc(s.activation(vecs[:, 2:3], vecs[:, 5:6], AF.Exp,
                                        bias=0.0, scale=-0.5))
                SE.batch(s_r2)

                def v_n2(v, j=j):
                    v.wait_ge(aS.h, aS.n)
                    aV.inc(v.tensor_scalar(vecs[:, 3:4], mv2[:, 2:3],
                                           vecs[:, 2:3], -1.0,
                                           ALU.mult, ALU.mult))
                VE.batch(v_n2)

                def s_a2(s, j=j, par=par):
                    s.wait_ge(aV.h, aV.n)
                    if j >= 2:
                        s.wait_ge(aG2.h, t3c[j - 2])
                    ap2c[j] = aS.inc(s.activation(
                        t2[par][:, :], xa_t[par][:, :], AF.Identity,
                        bias=vecs[:, 3:4], scale=vecs[:, 2:3]))
                SE.batch(s_a2)
            else:
                def s_a2b(s, j=j, par=par):
                    s.wait_ge(aV.h, aV.n)
                    if j >= 2:
                        s.wait_ge(aG2.h, t3c[j - 2])
                    r = aS.inc(s.activation(
                        t2[par][:, :], xt[par][:, :], AF.Identity,
                        bias=vecs[:, 1:2], scale=vecs[:, 0:1]))
                    apc[j] = r
                    ap2c[j] = r
                SE.batch(s_a2b)

            def g_t3(g, j=j, par=par):
                g.wait_ge(aS.h, ap2c[j])
                g.wait_ge(aV.h, nwb_cnt)
                if j >= 2:
                    g.wait_ge(aT.h, trc[(j - 2, 1)])
                t3c[j] = aG2.inc(g.tensor_mul(t3[par][:, :], t2[par][:, :],
                                              nwb[:, :]))
            GE.batch(g_t3)

            for gi in range(2):
                def t_tr(t, j=j, gi=gi, par=par):
                    t.wait_ge(aG2.h, t3c[j])
                    gl = j * 2 + gi
                    if gl >= 2:
                        pj, pg = divmod(gl - 2, 2)
                        t.wait_ge(aE.h, evc[(pj, pg)])
                    last = None
                    for k in range(8):
                        kk = gi * 8 + k
                        last = t.matmul(
                            pkt[gi][:, k * 128:(k + 1) * 128],
                            t3[par][:, kk * 128:(kk + 1) * 128],
                            pers["ident"][:, :], is_transpose=True,
                            start=True, stop=True)
                    trc[(j, gi)] = aT.inc(last)
                TE.batch(t_tr)

                def v_ev(v, j=j, gi=gi):
                    v.wait_ge(aT.h, trc[(j, gi)])
                    out_ap = h_fm[:].rearrange("p (k c) -> p k c", c=T)[
                        :, gi * 8:(gi + 1) * 8, j * 128:(j + 1) * 128]
                    evc[(j, gi)] = aE.inc(v.tensor_copy(out_ap,
                                                        pkt[gi][:, :]))
                VE.batch(v_ev)

        if store_xa:
            YE.batch(lambda sy: (sy.wait_ge(aXa[0].h, aXa[0].n),
                                 sy.wait_ge(aXa[1].h, aXa[1].n)))
        if hs_dst is not None:
            def y_hs(sy):
                sy.wait_ge(aE.h, aE.n)
                for k in range(KC):
                    aHs.inc(sy.dma_start(
                        out=hs_dst[k * 128:(k + 1) * 128, :],
                        in_=h_fm[:, k * T:(k + 1) * T]), 16)
            YE.batch(y_hs)
        VE.batch(lambda v: v.wait_ge(aE.h, aE.n))
        esHt.close()
        return esH, h_fm

    esH, h_fm = build_norm_fm("b0", dram["x1s"], dram["n1w"], None, True, dram["hs"])
    C.barrier()

    stash_wr = {}
    retg_wr = [None, None]
    for g in range(2):
        if g == 1:
            esH = ExitStack()
            sbh2 = lambda n, s, d: esH.enter_context(nc.sbuf_tensor(n, s, d))
            h_fm = esH.enter_context(nc.sbuf_tensor("at_hfm1", [128, KC * T], F16, side="right"))
            hre = C.sem("bn_aWr")

            def y_hre(sy):
                sy.wait_ge(aHs.h, aHs.n)
                for k in range(KC):
                    hre.inc(sy.dma_start(
                        out=h_fm[:, k * T:(k + 1) * T],
                        in_=dram["hs"][k * 128:(k + 1) * 128, :]), 16)
            YE.batch(y_hre)
            YE.batch(lambda sy: sy.wait_ge(hre.h, hre.n))
            C.barrier()

        esP = ExitStack()
        sbp = lambda n, s, d: esP.enter_context(nc.sbuf_tensor(n, s, d))
        q_fm = sbp(f"at_qfm{g}", [128, 8 * T], F16)
        k_fm = sbp(f"at_kfm{g}", [128, 8 * T], F16)
        v_tm = sbp(f"at_vtm{g}", [128, NT * 1024], F16)
        esPw = ExitStack()
        sbpw = lambda n, s, d: esPw.enter_context(nc.sbuf_tensor(n, s, d))
        wsl = [sbpw(f"at_wsl{g}{i}", [128, KC * 128], F16) for i in range(2)]
        bvq = sbpw(f"at_bvq{g}", [1, 1024], F32)
        bvcol = sbpw(f"at_bvcol{g}", [128, 1], F32)
        bvcol_b = bass.AP(bvcol, 0, [[1, 128], [0, 512]])
        bvv_b = sbpw(f"at_bvvb{g}", [128, 1024], F32)

        pW = [C.sem("gr_pWa"), C.sem("gr_pWb")]
        pT = C.sem("gr_pT")
        pE = C.sem("gr_pE")
        pEv = C.sem("gr_pEv")
        pV = C.sem("gr_pV")

        mmq = {}
        evq = {}
        wlq = {}
        bvc = {}
        for pi, (wnm, dst) in enumerate((("wk", k_fm), ("wq", q_fm))):
            for hloc in range(8):
                hh = g * 8 + hloc
                lin = pi * 8 + hloc
                par = lin % 2

                def g_w(g_, hh=hh, lin=lin, par=par, wnm=wnm):
                    if lin >= 2:
                        g_.wait_ge(pT.h, mmq[lin - 2])
                    src = bass.AP(dram[wnm], hh * 128,
                                  [[D, 128], [128 * D, KC], [1, 128]])
                    wlq[lin] = pW[par].inc(g_.dma_start(
                        out=wsl[par][:].rearrange("p (k c) -> p k c", c=128),
                        in_=src), 16)
                GE.batch(g_w)

                def t_p(t, lin=lin, par=par):
                    t.wait_ge(pW[par].h, wlq[lin])
                    t.wait_ge(cstg.h, 32)
                    if lin >= 1:
                        t.wait_ge(pE.h, evq[(lin - 1, "s")])
                        t.wait_ge(pEv.h, evq[lin - 1])
                    if lin >= 1:
                        t.wait_ge(pV.h, bvc[lin - 1])
                    last = None
                    for k in range(KC):
                        lhs = wsl[par][:, k * 128:(k + 1) * 128]
                        for tt in range(4):
                            t.matmul(
                                pa[tt][:, :], lhs,
                                h_fm[:, k * T + tt * 512:k * T + (tt + 1) * 512],
                                start=(k == 0), stop=(k == KC - 1))
                        last = t.matmul(pa[4][:, 0:1], lhs, n1b_t[:, k:k + 1],
                                        start=(k == 0), stop=(k == KC - 1))
                    mmq[lin] = pT.inc(last)
                TE.batch(t_p)

                def v_bv(v, lin=lin):
                    v.wait_ge(pT.h, mmq[lin])
                    if lin >= 1:
                        v.wait_ge(pE.h, evq[(lin - 1, "s")])
                        v.wait_ge(pEv.h, evq[lin - 1])
                    bvc[lin] = pV.inc(v.tensor_copy(bvcol[:, 0:1],
                                                    pa[4][:, 0:1]))
                VE.batch(v_bv)

                def s_ev(s, lin=lin, hloc=hloc, dst=dst):
                    s.wait_ge(pT.h, mmq[lin])
                    s.wait_ge(pV.h, bvc[lin])
                    last = None
                    for tt in range(2):
                        last = s.activation(
                            dst[:, hloc * T + tt * 512:hloc * T + (tt + 1) * 512],
                            pa[tt][:, :], AF.Identity, bias=bvcol[:, 0:1],
                            scale=1.0)
                    evq[(lin, "s")] = pE.inc(last)
                SE.batch(s_ev)

                def v_ev2(v, lin=lin, hloc=hloc, dst=dst):
                    v.wait_ge(pT.h, mmq[lin])
                    v.wait_ge(pV.h, bvc[lin])
                    last = None
                    for tt in range(2, 4):
                        last = v.scalar_tensor_tensor(
                            dst[:, hloc * T + tt * 512:hloc * T + (tt + 1) * 512],
                            pa[tt][:, :], 1.0, bvcol_b,
                            ALU.mult, ALU.add)
                    evq[lin] = pEv.inc(last)
                VE.batch(v_ev2)

        wvb = sbpw(f"at_wvb{g}", [128, KC * 512], F16)
        wlv = C.sem("gr_wlv")
        mmv = {}
        evv = {}
        for n in range(2):
            def g_wv(g_, n=n, g2=g):
                if n == 1:
                    g_.wait_ge(pT.h, mmv[(0, NT - 1)])
                src = bass.AP(dram["wv"], g2 * 1024 + n * 512,
                              [[D, 128], [128 * D, KC], [1, 512]])
                wlv.inc(g_.dma_start(
                    out=wvb[:].rearrange("p (k c) -> p k c", c=512),
                    in_=src), 16)
            GE.batch(g_wv)

            def t_bv(t, n=n):
                t.wait_ge(wlv.h, wlv.n)
                t.wait_ge(cstg.h, 32)
                if n == 0:
                    t.wait_ge(pE.h, pE.n)
                    t.wait_ge(pV.h, pV.n)
                else:
                    t.wait_ge(pV.h, pV.n)
                last = None
                for k in range(KC):
                    last = t.matmul(
                        pa[4][0:1, 0:512], n1b_t[:, k:k + 1],
                        wvb[:, k * 512:(k + 1) * 512],
                        start=(k == 0), stop=(k == KC - 1))
                mmv[("bv", n)] = pT.inc(last)
            TE.batch(t_bv)

            def v_cbv(v, n=n):
                v.wait_ge(pT.h, mmv[("bv", n)])
                bvc[("v", n)] = pV.inc(v.tensor_copy(
                    bvq[0:1, n * 512:(n + 1) * 512], pa[4][0:1, 0:512]))
            VE.batch(v_cbv)

            def t_bcast(t, n=n):
                t.wait_ge(pV.h, bvc[("v", n)])
                mmv[("bc", n)] = pT.inc(t.matmul(
                    pa[5][:, :], pers["ones_row"][0:1, 0:128],
                    bvq[0:1, n * 512:(n + 1) * 512], start=True, stop=True))
            TE.batch(t_bcast)

            def v_cbc(v, n=n):
                v.wait_ge(pT.h, mmv[("bc", n)])
                bvc[("vb", n)] = pV.inc(v.tensor_copy(
                    bvv_b[:, n * 512:(n + 1) * 512], pa[5][:, :]))
            VE.batch(v_cbc)

            for t_i in range(NT):
                def t_v(t, t_i=t_i, n=n):
                    t.wait_ge(pV.h, bvc[("vb", n)])
                    if t_i >= 2:
                        t.wait_ge(pEv.h, evv[(n, t_i - 2)])
                    bank = pa[2 * (t_i % 2) + n]
                    last = None
                    for k in range(KC):
                        last = t.matmul(
                            bank[:, :],
                            h_fm[:, k * T + t_i * 128:k * T + (t_i + 1) * 128],
                            wvb[:, k * 512:(k + 1) * 512],
                            start=(k == 0), stop=(k == KC - 1))
                    mmv[(n, t_i)] = pT.inc(last)
                TE.batch(t_v)

                def v_vv(v, t_i=t_i, n=n):
                    v.wait_ge(pT.h, mmv[(n, t_i)])
                    bank = pa[2 * (t_i % 2) + n]
                    last = v.tensor_tensor(
                        v_tm[:, t_i * 1024 + n * 512:t_i * 1024 + (n + 1) * 512],
                        bank[:, :], bvv_b[:, n * 512:(n + 1) * 512], ALU.add)
                    evv[(n, t_i)] = pEv.inc(last)
                VE.batch(v_vv)

        C.barrier()
        esPw.close()
        esH.close()
        esR2 = ExitStack()
        esR = ExitStack()
        sbr = lambda n, s, d: esR.enter_context(nc.sbuf_tensor(n, s, d))
        R_hist = sbr(f"at_Rh{g}", [128, (NT + 1) * 1024], F16)
        s_til = sbr(f"at_stil{g}", [128, 1024], F16)
        kt_all = sbr(f"at_kta{g}", [128, NT * 1024], F16)
        stash_sb = [sbr(f"at_stsh{g}{i}", [128, 1024], F16) for i in range(2)]
        qtb = sbr(f"at_qtb{g}", [128, 2 * 1024], F16)

        rT = C.sem("gr_rT")
        rS2 = C.sem("gr_rS2")
        rV = C.sem("gr_rV")
        rY = [C.sem("gr_rYa"), C.sem("gr_rYb")]
        rSnd = C.sem("gr_rSnd")
        rCc = C.sem("gr_rCc")

        VE.batch(lambda v: (
            v.wait_ge(pE.h, pE.n),
            rV.inc(v.memset(R_hist[:, 0:1024], 0.0)),
        ))
        ktr = {}
        ktv = {}
        for i2 in range(NT):
            def t_kt0(t, i2=i2):
                t.wait_ge(pE.h, evq[(7, "s")])
                t.wait_ge(pEv.h, evq[7])
                if i2 >= 2:
                    t.wait_ge(rV.h, ktv[i2 - 2])
                last = None
                for hloc in range(8):
                    last = t.matmul(
                        pkt[i2 % 2][:, hloc * 128:(hloc + 1) * 128],
                        k_fm[:, hloc * T + i2 * 128:hloc * T + (i2 + 1) * 128],
                        pers["ident"][:, :], is_transpose=True,
                        start=True, stop=True)
                ktr[i2] = rT.inc(last)
            TE.batch(t_kt0)

            def v_kt0(v, i2=i2, g=g):
                v.wait_ge(rT.h, ktr[i2])
                ktv[i2] = rV.inc(v.tensor_tensor(
                    kt_all[:, i2 * 1024:(i2 + 1) * 1024], pkt[i2 % 2][:, :],
                    kdec_sb[:, g * 1024:(g + 1) * 1024], ALU.mult))
            VE.batch(v_kt0)

        msk = {}
        sstc = {}
        ktl = {}
        qtl = {}
        stu = {}
        rhc = {}
        sev = {}
        mmst = {}
        mmA = {}
        mmI = {}
        mmK = {}
        for i in range(NT):
            def t_s(t, i=i, g=g):
                t.wait_ge(pE.h, pE.n)
                t.wait_ge(pEv.h, pEv.n)
                if i >= 1:
                    t.wait_ge(rV.h, msk[i - 1])
                last = None
                for hloc in range(8):
                    bank = pa[hloc // 4]
                    last = t.matmul(
                        bank[:, (hloc % 4) * 128:(hloc % 4 + 1) * 128],
                        k_fm[:, hloc * T + i * 128:hloc * T + (i + 1) * 128],
                        q_fm[:, hloc * T + i * 128:hloc * T + (i + 1) * 128],
                        start=True, stop=True)
                mmst[i] = rT.inc(last)
            TE.batch(t_s)

            def v_msk(v, i=i, g=g):
                v.wait_ge(rT.h, mmst[i])
                if i >= 1:
                    v.wait_ge(rT.h, mmI[i - 1])
                v.tensor_tensor(s_til[:, 0:512], pa[0][:, :],
                                dmask_sb[:, g * 1024:g * 1024 + 512], ALU.mult)
                msk[i] = rV.inc(v.tensor_tensor(
                    s_til[:, 512:1024], pa[1][:, :],
                    dmask_sb[:, g * 1024 + 512:g * 1024 + 1024], ALU.mult))
            VE.batch(v_msk)

            def t_A(t, i=i):
                t.wait_ge(rV.h, ktv[i])
                if i >= 1:
                    t.wait_ge(rV.h, stu[i - 1])
                last = None
                for hloc in range(8):
                    bank = pa[4 + hloc // 4]
                    last = t.matmul(
                        bank[:, (hloc % 4) * 128:(hloc % 4 + 1) * 128],
                        kt_all[:, i * 1024 + hloc * 128:i * 1024 + (hloc + 1) * 128],
                        v_tm[:, i * 1024 + hloc * 128:i * 1024 + (hloc + 1) * 128],
                        start=True, stop=True)
                mmA[i] = rT.inc(last)
            TE.batch(t_A)

            def v_qt(v, i=i, g=g):
                if i >= 2:
                    v.wait_ge(rT.h, mmI[i - 2])
                qsrc = bass.AP(q_fm, i * 128, [[8 * T, 128], [T, 8], [1, 128]])
                qd = bass.AP(qdec_sb, g * 1024,
                             [[NH * 128, 128], [128, 8], [1, 128]])
                dst = bass.AP(qtb, (i % 2) * 1024,
                              [[2 * 1024, 128], [128, 8], [1, 128]])
                qtl[i] = rV.inc(v.tensor_tensor(dst, qsrc, qd, ALU.mult))
            VE.batch(v_qt)

            def t_ic(t, i=i):
                t.wait_ge(rV.h, msk[i])
                t.wait_ge(rV.h, qtl[i])
                if i >= 1:
                    t.wait_ge(rS2.h, sev[i - 1])
                    t.wait_ge(rV.h, rhc[i - 1])
                last = None
                for hloc in range(8):
                    bank = pa[2 + hloc // 4]
                    sl = bank[:, (hloc % 4) * 128:(hloc % 4 + 1) * 128]
                    t.matmul(sl, s_til[:, hloc * 128:(hloc + 1) * 128],
                             v_tm[:, i * 1024 + hloc * 128:i * 1024 + (hloc + 1) * 128],
                             start=True, stop=False)
                    last = t.matmul(
                        sl, qtb[:, (i % 2) * 1024 + hloc * 128:
                                (i % 2) * 1024 + (hloc + 1) * 128],
                        R_hist[:, i * 1024 + hloc * 128:i * 1024 + (hloc + 1) * 128],
                        start=False, stop=True)
                mmI[i] = rT.inc(last)
            TE.batch(t_ic)

            def v_st(v, i=i, g=g):
                v.wait_ge(rT.h, mmA[i])
                last = None
                for hloc in range(8):
                    gam = float(GAMMA[g * 8 + hloc] ** 128)
                    last = v.scalar_tensor_tensor(
                        R_hist[:, (i + 1) * 1024 + hloc * 128:
                               (i + 1) * 1024 + (hloc + 1) * 128],
                        R_hist[:, i * 1024 + hloc * 128:
                               i * 1024 + (hloc + 1) * 128], gam,
                        pa[4 + hloc // 4][:, (hloc % 4) * 128:(hloc % 4 + 1) * 128],
                        ALU.mult, ALU.add)
                stu[i] = rV.inc(last)
                rhc[i] = stu[i]
            VE.batch(v_st)

            def s_sev(s, i=i):
                s.wait_ge(rT.h, mmI[i])
                if i >= 2:
                    s.wait_ge(rY[i % 2].h, sstc[i - 2])
                s.activation(stash_sb[i % 2][:, 0:512], pa[2][:, :],
                             AF.Identity, bias=0.0, scale=1.0)
                sev[i] = rS2.inc(s.activation(
                    stash_sb[i % 2][:, 512:1024], pa[3][:, :],
                    AF.Identity, bias=0.0, scale=1.0))
            SE.batch(s_sev)

            def y_sst(sy, i=i, g=g):
                sy.wait_ge(rS2.h, sev[i])
                sstc[i] = rY[i % 2].inc(sy.dma_start(
                    out=dram["stashs"][i * 128:(i + 1) * 128,
                                       g * 1024:(g + 1) * 1024],
                    in_=stash_sb[i % 2][:, :]), 16)
            YE.batch(y_sst)

        def y_snd(sy, g=g):
            sy.wait_ge(rV.h, stu[NT - 1])
            dst = bass.AP(dram["st_in"][g], 0,
                          [[128, 128], [128 * 128, 8], [1, 128]])
            rSnd.inc(sy.dma_start(
                out=dst, in_=R_hist[:, NT * 1024:(NT + 1) * 1024]), 16)
        YE.batch(y_snd)

        pairs = [[2 * i, 2 * i + 1] for i in range(C.ncores // 2)]
        GE.batch(lambda g_, g=g: (
            g_.wait_ge(rSnd.h, rSnd.n),
            rCc.inc(g_.collective_compute(
                "AllGather", ALU.bypass, replica_groups=pairs,
                ins=[dram["st_in"][g].ap().opt()],
                outs=[dram["st_out"][g].ap().opt()])),
        ))

        C.barrier()
        esR.close()
        esF = ExitStack()
        sbf = lambda n, s, d: esF.enter_context(nc.sbuf_tensor(n, s, d))
        PF32 = sbf(f"at_PF{g}", [128, 1024], F32)
        P16 = [sbf(f"at_P16{g}{i}", [128, 1024], F16) for i in range(2)]
        stld = [sbf(f"at_stld{g}{i}", [128, 1024], F16) for i in range(2)]
        tf32 = sbf(f"at_tf{g}", [128, 1024], F32)
        sq = sbf(f"at_sq{g}", [128, 1024], F32)
        gnv = sbf(f"at_gnv{g}", [128, 40], F32)
        retg = [sbf(f"at_retg{g}{i}", [128, 1024], F16) for i in range(2)]
        qtb2 = sbf(f"at_qtb2{g}", [128, 2 * 1024], F16)

        fL = [C.sem("gr_fLa"), C.sem("gr_fLb")]
        fV = C.sem("gr_fV")
        fS = C.sem("gr_fS")
        fT = C.sem("gr_fT")
        fY = [C.sem("gr_fYa"), C.sem("gr_fYb")]
        fP = C.sem("gr_fP")

        def g_pld(g_, g2=g):
            g_.wait_ge(rCc.h, rCc.n)
            src = bass.AP(dram["st_out"][g2], 0,
                          [[128, 128], [128 * 128, 8], [1, 128]])
            fP.inc(g_.dma_start(out=PF32[:, :], in_=src), 16)
        GE.batch(g_pld)
        fPc = fP.n

        VE.batch(lambda v: (
            v.wait_ge(fP.h, fPc),
            fV.inc(v.tensor_scalar(PF32[:, :], PF32[:, :],
                                   pers["flag_b"][:, 0:1], None, ALU.mult)),
        ))
        pf_cnt = fV.n

        ld2 = {}
        qtl2 = {}
        rgstc = {}
        p16c = {}
        mmP = {}
        gnc = {}
        rgc = {}
        stc2 = {}
        for i in range(NT):
            par = i % 2

            def y_ld(sy, i=i, par=par, g=g):
                if i >= 2:
                    sy.wait_ge(fV.h, gnc[(i - 2, 0)])
                sy.wait_ge(rY[i % 2].h, rY[i % 2].n)
                ld2[i] = fL[par].inc(sy.dma_start(
                    out=stld[par][:, :],
                    in_=dram["stashs"][i * 128:(i + 1) * 128,
                                       g * 1024:(g + 1) * 1024]), 16)
            YE.batch(y_ld)

            def v_p16(v, i=i, par=par, g=g):
                v.wait_ge(fV.h, pf_cnt)
                if i >= 2:
                    v.wait_ge(fT.h, mmP[i - 2])
                last = None
                for hloc in range(8):
                    gam = float(GAMMA[g * 8 + hloc] ** (128 * i))
                    last = v.tensor_scalar(
                        P16[par][:, hloc * 128:(hloc + 1) * 128],
                        PF32[:, hloc * 128:(hloc + 1) * 128], gam, None,
                        ALU.mult)
                p16c[i] = fV.inc(last)
            VE.batch(v_p16)

            def v_qt2(v, i=i, par=par, g=g):
                if i >= 2:
                    v.wait_ge(fT.h, mmP[i - 2])
                qsrc = bass.AP(q_fm, i * 128, [[8 * T, 128], [T, 8], [1, 128]])
                qd = bass.AP(qdec_sb, g * 1024,
                             [[NH * 128, 128], [128, 8], [1, 128]])
                dst = bass.AP(qtb2, (i % 2) * 1024,
                              [[2 * 1024, 128], [128, 8], [1, 128]])
                qtl2[i] = fV.inc(v.tensor_tensor(dst, qsrc, qd, ALU.mult))
            VE.batch(v_qt2)

            def t_P(t, i=i, par=par):
                t.wait_ge(fV.h, p16c[i])
                t.wait_ge(fV.h, qtl2[i])
                if i == 0:
                    t.wait_ge(rS2.h, sev[NT - 1])
                if i == 1:
                    t.wait_ge(rV.h, msk[NT - 1])
                if i >= 2:
                    t.wait_ge(fV.h, gnc[(i - 2, 0)])
                last = None
                base = 2 - 2 * (i % 2)
                for hloc in range(8):
                    bank = pa[base + hloc // 4]
                    last = t.matmul(
                        bank[:, (hloc % 4) * 128:(hloc % 4 + 1) * 128],
                        qtb2[:, (i % 2) * 1024 + hloc * 128:
                             (i % 2) * 1024 + (hloc + 1) * 128],
                        P16[par][:, hloc * 128:(hloc + 1) * 128],
                        start=True, stop=True)
                mmP[i] = fT.inc(last)
            TE.batch(t_P)

            def v_gn1(v, i=i, par=par):
                v.wait_ge(fT.h, mmP[i])
                v.wait_ge(fL[par].h, ld2[i])
                if i >= 1:
                    v.wait_ge(fS.h, gnc[(i - 1, 1)])
                    v.wait_ge(fV.h, gnc[(i - 1, 2)])
                base = 2 - 2 * (i % 2)
                v.tensor_tensor(tf32[:, 0:512], pa[base][:, :],
                                stld[par][:, 0:512], ALU.add)
                a = v.tensor_tensor(tf32[:, 512:1024], pa[base + 1][:, :],
                                    stld[par][:, 512:1024], ALU.add)
                gnc[(i, 0)] = fV.inc(a)
                v.wait_ge(fV.h, gnc[(i, 0)])
                gnc[(i, 3)] = fV.inc(v.tensor_reduce(
                    gnv[:, 0:8],
                    tf32[:].rearrange("p (h c) -> p h c", c=128),
                    AX.X, ALU.add))
            VE.batch(v_gn1)

            def s_sq(s, i=i):
                s.wait_ge(fV.h, gnc[(i, 0)])
                if i >= 1:
                    s.wait_ge(fV.h, gnc[(i - 1, 4)])
                gnc[(i, 1)] = fS.inc(s.activation(sq[:, :], tf32[:, :],
                                                  AF.Square, bias=0.0,
                                                  scale=1.0))
            SE.batch(s_sq)

            def v_gn2(v, i=i):
                v.wait_ge(fS.h, gnc[(i, 1)])
                v.wait_ge(fV.h, gnc[(i, 3)])
                a = v.tensor_reduce(
                    gnv[:, 8:16],
                    sq[:].rearrange("p (h c) -> p h c", c=128),
                    AX.X, ALU.add)
                gnc[(i, 4)] = fV.inc(a)
                v.wait_ge(fV.h, gnc[(i, 4)])
                fV.inc(v.tensor_scalar(gnv[:, 16:24], gnv[:, 0:8], 1.0 / 128,
                                       None, ALU.mult))
                fV.inc(v.tensor_scalar(gnv[:, 24:32], gnv[:, 8:16], 1.0 / 128,
                                       None, ALU.mult))
                v.wait_ge(fV.h, fV.n)
                fV.inc(v.tensor_tensor(gnv[:, 32:40], gnv[:, 16:24],
                                       gnv[:, 16:24], ALU.mult))
                v.wait_ge(fV.h, fV.n)
                gnc[(i, 5)] = fV.inc(v.tensor_tensor(
                    gnv[:, 24:32], gnv[:, 24:32], gnv[:, 32:40], ALU.subtract))
            VE.batch(v_gn2)

            def s_rstd(s, i=i):
                s.wait_ge(fV.h, gnc[(i, 5)])
                fS.inc(s.activation(gnv[:, 32:40], gnv[:, 24:32], AF.Ln,
                                    bias=pers["eps_b"][:, 0:1], scale=1.0))
                s.wait_ge(fS.h, fS.n)
                gnc[(i, 6)] = fS.inc(s.activation(
                    gnv[:, 24:32], gnv[:, 32:40], AF.Exp, bias=0.0,
                    scale=-0.5))
            SE.batch(s_rstd)

            def v_app(v, i=i, par=par, g=g):
                v.wait_ge(fS.h, gnc[(i, 6)])
                if i >= 2:
                    v.wait_ge(fY[par].h, rgstc[i - 2])
                t3d = tf32[:].rearrange("p (h c) -> p h c", c=128)
                mu3 = bass.AP(gnv, 16, [[40, 128], [1, 8], [0, 128]])
                rs3 = bass.AP(gnv, 24, [[40, 128], [1, 8], [0, 128]])
                fV.inc(v.tensor_tensor(t3d, t3d, mu3, ALU.subtract))
                v.wait_ge(fV.h, fV.n)
                gnc[(i, 2)] = fV.inc(v.tensor_tensor(
                    retg[par][:].rearrange("p (h c) -> p h c", c=128),
                    t3d, rs3, ALU.mult))
            VE.batch(v_app)

            def y_rg(sy, i=i, par=par, g=g):
                sy.wait_ge(fV.h, gnc[(i, 2)])
                rgstc[i] = fY[par].inc(sy.dma_start(
                    out=dram["retgns"][i * 128:(i + 1) * 128,
                                       g * 1024:(g + 1) * 1024],
                    in_=retg[par][:, :]), 16)
            YE.batch(y_rg)

        YE.batch(lambda sy: (sy.wait_ge(fY[0].h, fY[0].n),
                             sy.wait_ge(fY[1].h, fY[1].n)))
        retg_wr[g] = (fY[0].n, fY[1].n, fY)
        esF.close()
        esR2.close()
        esP.close()
        C.barrier()

    esG = ExitStack()
    sbg = lambda n, s, d: esG.enter_context(nc.sbuf_tensor(n, s, d))
    esGh = ExitStack()
    h_fm = esGh.enter_context(nc.sbuf_tensor("at_hfmg", [128, KC * T], F16, side="right"))
    yin_fm = sbg("at_yinfm", [128, KC * T], F16)
    esG1 = ExitStack()
    sbg1 = lambda n, s, d: esG1.enter_context(nc.sbuf_tensor(n, s, d))
    wgsb = [sbg1(f"at_wgsb{i}", [128, KC * 512], F16) for i in range(2)]
    gtmp = [sbg1(f"at_gtmp{i}", [128, 512], F32) for i in range(2)]
    bvg_b = sbg1("at_bvgb", [128, 512], F32)
    sgt = [sbg1(f"at_sgt{i}", [128, 512], F16) for i in range(2)]
    rgt = [sbg1(f"at_rgt{i}", [128, 512], F16) for i in range(2)]
    ymt = [sbg1(f"at_ymt{i}", [128, 512], F16) for i in range(2)]
    bvg = sbg1("at_bvg", [1, 512], F32)

    hre2 = C.sem("bn_aWr")

    def y_hre2(sy):
        sy.wait_ge(aHs.h, aHs.n)
        for k in range(KC):
            hre2.inc(sy.dma_start(
                out=h_fm[:, k * T:(k + 1) * T],
                in_=dram["hs"][k * 128:(k + 1) * 128, :]), 16)
    YE.batch(y_hre2)
    YE.batch(lambda sy: sy.wait_ge(hre2.h, hre2.n))
    C.barrier()

    gW = [C.sem("gr_pWa"), C.sem("gr_pWb")]
    gT = C.sem("gr_pT")
    gS = C.sem("ffn_bS")
    gV = C.sem("ffn_bV")
    gE = C.sem("gr_pE")
    gG = C.sem("gr_gG")
    gL = [C.sem("gr_fLa"), C.sem("gr_fLb")]
    gB = C.sem("gr_fP")

    mmg = {}
    silg = {}
    gaddc = {}
    ttg = {}
    trg = {}
    evg = {}
    ldg = {}
    wlg = {}
    bvgc = {}
    for n in range(4):
        def g_wg(g_, n=n):
            par = n % 2
            if n >= 2:
                g_.wait_ge(gT.h, mmg[(n - 2, NT - 1)])
            src = bass.AP(dram["wgp"], n * 512,
                          [[D, 128], [128 * D, KC], [1, 512]])
            wlg[n] = gW[par].inc(g_.dma_start(
                out=wgsb[par][:].rearrange("p (k c) -> p k c", c=512),
                in_=src), 16)
        GE.batch(g_wg)

        def t_bvg(t, n=n):
            par = n % 2
            t.wait_ge(gW[par].h, wlg[n])
            t.wait_ge(cstg.h, 32)
            if n >= 1:
                t.wait_ge(gV.h, bvgc[n - 1])
            last = None
            for k in range(KC):
                last = t.matmul(pa[4][0:1, 0:512], n1b_t[:, k:k + 1],
                                wgsb[par][:, k * 512:(k + 1) * 512],
                                start=(k == 0), stop=(k == KC - 1))
            mmg[(n, "bv")] = gT.inc(last)
        TE.batch(t_bvg)

        def v_bvg(v, n=n):
            v.wait_ge(gT.h, mmg[(n, "bv")])
            bvgc[n] = gV.inc(v.tensor_copy(bvg[0:1, :], pa[4][0:1, 0:512]))
        VE.batch(v_bvg)

        def t_bcg(t, n=n):
            t.wait_ge(gV.h, bvgc[n])
            mmg[(n, "bc")] = gT.inc(t.matmul(
                pa[5][:, :], pers["ones_row"][0:1, 0:128], bvg[0:1, :],
                start=True, stop=True))
        TE.batch(t_bcg)

        def v_bcg(v, n=n):
            v.wait_ge(gT.h, mmg[(n, "bc")])
            bvgc[n] = gV.inc(v.tensor_copy(bvg_b[:, :], pa[5][:, :]))
        VE.batch(v_bcg)

        for t_i in range(NT):
            par = n % 2
            tp = t_i % 2
            lin = n * NT + t_i

            def t_g(t, n=n, t_i=t_i, par=par, tp=tp, lin=lin):
                t.wait_ge(gV.h, bvgc[n])
                if lin >= 2:
                    nn, jt = divmod(lin - 2, NT)
                    t.wait_ge(gV.h, gaddc[(nn, jt)])
                last = None
                for k in range(KC):
                    last = t.matmul(
                        pa[tp][:, :],
                        h_fm[:, k * T + t_i * 128:k * T + (t_i + 1) * 128],
                        wgsb[par][:, k * 512:(k + 1) * 512],
                        start=(k == 0), stop=(k == KC - 1))
                mmg[(n, t_i)] = gT.inc(last)
            TE.batch(t_g)

            def v_gadd(v, n=n, t_i=t_i, tp=tp, lin=lin):
                v.wait_ge(gT.h, mmg[(n, t_i)])
                v.wait_ge(gV.h, bvgc[n])
                if lin >= 2:
                    nn, jt = divmod(lin - 2, NT)
                    v.wait_ge(gS.h, silg[(nn, jt)])
                gaddc[(n, t_i)] = gV.inc(v.tensor_tensor(
                    gtmp[tp][:, :], pa[tp][:, :], bvg_b[:, :], ALU.add))
            VE.batch(v_gadd)

            def s_g(s, n=n, t_i=t_i, tp=tp, lin=lin):
                s.wait_ge(gV.h, gaddc[(n, t_i)])
                if lin >= 2:
                    nn, jt = divmod(lin - 2, NT)
                    s.wait_ge(gG.h, ttg[(nn, jt)])
                silg[(n, t_i)] = gS.inc(s.activation(
                    sgt[tp][:, :], gtmp[tp][:, :], AF.Silu, bias=0.0,
                    scale=1.0))
            SE.batch(s_g)

            def y_rgl(sy, n=n, t_i=t_i, tp=tp, lin=lin, g2=None):
                if lin >= 2:
                    nn, jt = divmod(lin - 2, NT)
                    sy.wait_ge(gG.h, ttg[(nn, jt)])
                for gg2 in range(2):
                    sy.wait_ge(retg_wr[gg2][2][0].h, retg_wr[gg2][0])
                    sy.wait_ge(retg_wr[gg2][2][1].h, retg_wr[gg2][1])
                ldg[(n, t_i)] = gL[tp].inc(sy.dma_start(
                    out=rgt[tp][:, :],
                    in_=dram["retgns"][t_i * 128:(t_i + 1) * 128,
                                       n * 512:(n + 1) * 512]), 16)
            YE.batch(y_rgl)

            def g_g(g, n=n, t_i=t_i, tp=tp, lin=lin):
                g.wait_ge(gS.h, silg[(n, t_i)])
                g.wait_ge(gL[tp].h, ldg[(n, t_i)])
                if lin >= 2:
                    nn, jt = divmod(lin - 2, NT)
                    g.wait_ge(gT.h, trg[(nn, jt)])
                ttg[(n, t_i)] = gG.inc(g.tensor_mul(ymt[tp][:, :],
                                                    sgt[tp][:, :],
                                                    rgt[tp][:, :]))
            GE.batch(g_g)

            def t_try(t, n=n, t_i=t_i, tp=tp, lin=lin):
                t.wait_ge(gG.h, ttg[(n, t_i)])
                if lin >= 2:
                    nn, jt = divmod(lin - 2, NT)
                    t.wait_ge(gE.h, evg[(nn, jt)])
                last = None
                for ii in range(4):
                    last = t.matmul(
                        pkt[lin % 2][:, ii * 128:(ii + 1) * 128],
                        ymt[tp][:, ii * 128:(ii + 1) * 128],
                        pers["ident"][:, :], is_transpose=True,
                        start=True, stop=True)
                trg[(n, t_i)] = gT.inc(last)
            TE.batch(t_try)

            def v_evy(v, n=n, t_i=t_i, lin=lin):
                v.wait_ge(gT.h, trg[(n, t_i)])
                out_ap = yin_fm[:].rearrange("p (k c) -> p k c", c=T)[
                    :, 4 * n:4 * n + 4, t_i * 128:(t_i + 1) * 128]
                evg[(n, t_i)] = gE.inc(v.tensor_copy(
                    out_ap, pkt[lin % 2][:, 0:512]))
            VE.batch(v_evy)

    C.barrier()
    esG1.close()
    esGh.close()
    woW = [C.sem("ffn_bW0"), C.sem("ffn_bW1")]
    woT = C.sem("ffn_bT")
    woV = C.sem("ffn_cV")
    woXa = [C.sem("ffn_cX0"), C.sem("ffn_cX1")]
    woY = [C.sem("ffn_cY0"), C.sem("ffn_cY1")]
    wosb = [sbg(f"at_wosb{i}", [128, KC * 512], F16) for i in range(2)]
    xac = [sbg(f"at_xac{i}", [128, 512], F32) for i in range(2)]
    xbc = [sbg(f"at_xbc{i}", [128, 512], F32) for i in range(2)]

    mmo = {}
    addo = {}
    sto = {}
    xal = {}
    wlo = {}
    for n in range(4):
        def g_wo(g_, n=n):
            par = n % 2
            if n >= 2:
                g_.wait_ge(woT.h, mmo[(n - 2, NT - 1)])
            src = bass.AP(dram["wo"], n * 512,
                          [[D, 128], [128 * D, KC], [1, 512]])
            wlo[n] = woW[par].inc(g_.dma_start(
                out=wosb[par][:].rearrange("p (k c) -> p k c", c=512),
                in_=src), 16)
        GE.batch(g_wo)

        for t_i in range(NT):
            par = n % 2
            lin = n * NT + t_i
            lp = lin % 2

            def y_xal(sy, n=n, t_i=t_i, lin=lin, lp=lp):
                if lin >= 2:
                    sy.wait_ge(woV.h, addo[lin - 2])
                xal[lin] = woXa[lp].inc(sy.dma_start(
                    out=xac[lp][:, :],
                    in_=dram["xas"][t_i * 128:(t_i + 1) * 128,
                                    n * 512:(n + 1) * 512]), 16)
            YE.batch(y_xal)

            def t_o(t, n=n, t_i=t_i, par=par, lin=lin):
                t.wait_ge(woW[par].h, wlo[n])
                if n == 0 and t_i == 0:
                    t.wait_ge(gE.h, gE.n)
                if lin >= 4:
                    t.wait_ge(woV.h, addo[lin - 4])
                last = None
                for k in range(KC):
                    last = t.matmul(
                        pa[lin % 4][:, :],
                        yin_fm[:, k * T + t_i * 128:k * T + (t_i + 1) * 128],
                        wosb[par][:, k * 512:(k + 1) * 512],
                        start=(k == 0), stop=(k == KC - 1))
                mmo[(n, t_i)] = woT.inc(last)
            TE.batch(t_o)

            def v_o(v, n=n, t_i=t_i, lin=lin, lp=lp):
                v.wait_ge(woT.h, mmo[(n, t_i)])
                v.wait_ge(woXa[lp].h, xal[lin])
                if lin >= 2:
                    v.wait_ge(woY[lp].h, sto[lin - 2])
                addo[lin] = woV.inc(v.tensor_tensor(
                    xbc[lp][:, :], pa[lin % 4][:, :], xac[lp][:, :], ALU.add))
            VE.batch(v_o)

            def y_o(sy, n=n, t_i=t_i, lin=lin, lp=lp):
                sy.wait_ge(woV.h, addo[lin])
                sto[lin] = woY[lp].inc(sy.dma_start(
                    out=dram["xbs"][t_i * 128:(t_i + 1) * 128,
                                    n * 512:(n + 1) * 512],
                    in_=xbc[lp][:, :]), 16)
            YE.batch(y_o)

    YE.batch(lambda sy: (sy.wait_ge(woY[0].h, woY[0].n),
                         sy.wait_ge(woY[1].h, woY[1].n)))
    esG.close()
    C.barrier()

    esH2, h2_fm = build_norm_fm("b9", dram["xbs"], dram["n2w"], None, False, None)
    C.barrier()

    es10 = ExitStack()
    sb10 = lambda n, s, d: es10.enter_context(nc.sbuf_tensor(n, s, d))
    w1sb = [sb10(f"at_w1sb{i}", [128, KC * 128], F16) for i in range(2)]
    w2sb = [sb10(f"at_w2sb{i}", [128, KC * 128], F16) for i in range(2)]
    bv12 = sb10("at_bv12", [128, 2], F32)
    sgf = [sb10(f"at_sgf{i}", [128, 512], F16) for i in range(2)]
    a3t = [sb10(f"at_a3t{i}", [128, 512], F16) for i in range(2)]

    iW = [C.sem("gr_pWa"), C.sem("gr_pWb")]
    iT = C.sem("gr_pT")
    iS = C.sem("gr_iS")
    iV = C.sem("gr_iV")
    iY = [C.sem("gr_fYa"), C.sem("gr_fYb")]

    NOC = 43
    mmi = {}
    sili = {}
    tti = {}
    sti = {}
    wli = {}
    bvi = {}
    for oc in range(NOC):
        ow = 85 if oc == NOC - 1 else 128
        par = oc % 2

        def g_w12(g_, oc=oc, ow=ow, par=par):
            if oc >= 2:
                g_.wait_ge(iT.h, mmi[(oc - 2, 3)])
            src1 = bass.AP(dram["w1"], oc * 128,
                           [[HID, 128], [128 * HID, KC], [1, ow]])
            src2 = bass.AP(dram["w2"], oc * 128,
                           [[HID, 128], [128 * HID, KC], [1, ow]])
            wr1 = w1sb[par][:].rearrange("p (k c) -> p k c", c=128)[:, :, 0:ow]
            wr2 = w2sb[par][:].rearrange("p (k c) -> p k c", c=128)[:, :, 0:ow]
            g_.dma_start(out=wr1, in_=src1).then_inc(iW[par].h, 16)
            iW[par].n += 16
            wli[oc] = iW[par].inc(g_.dma_start(out=wr2, in_=src2), 16)
        GE.batch(g_w12)

        def t_bv12(t, oc=oc, ow=ow, par=par):
            t.wait_ge(iW[par].h, wli[oc])
            t.wait_ge(cstg.h, 32)
            if oc == 0:
                t.wait_ge(cst.h, cst.n)
            if oc >= 1:
                t.wait_ge(iV.h, bvi[oc - 1])
            last = None
            for k in range(KC):
                t.matmul(pa[4][0:ow, 0:1],
                         w1sb[par][:, k * 128:k * 128 + ow],
                         n2b_t[:, k:k + 1], start=(k == 0), stop=(k == KC - 1))
                last = t.matmul(pa[5][0:ow, 0:1],
                                w2sb[par][:, k * 128:k * 128 + ow],
                                n2b_t[:, k:k + 1], start=(k == 0),
                                stop=(k == KC - 1))
            mmi[(oc, "bv")] = iT.inc(last)
        TE.batch(t_bv12)

        def v_bv12(v, oc=oc, ow=ow):
            v.wait_ge(iT.h, mmi[(oc, "bv")])
            if oc >= 1:
                v.wait_ge(iV.h, tti[(oc - 1, 3)])
                v.wait_ge(iS.h, sili[(oc - 1, 3)])
            iV.inc(v.tensor_copy(bv12[0:ow, 0:1], pa[4][0:ow, 0:1]))
            bvi[oc] = iV.inc(v.tensor_copy(bv12[0:ow, 1:2], pa[5][0:ow, 0:1]))
        VE.batch(v_bv12)

        for tt in range(4):
            lin = oc * 4 + tt

            def t_12(t, oc=oc, ow=ow, par=par, tt=tt, lin=lin):
                t.wait_ge(iV.h, bvi[oc])
                if lin >= 2:
                    po, pt = divmod(lin - 2, 4)
                    t.wait_ge(iS.h, sili[(po, pt)])
                    t.wait_ge(iV.h, tti[(po, pt)])
                last = None
                for k in range(KC):
                    t.matmul(pa[2 * (tt % 2)][0:ow, :],
                             w1sb[par][:, k * 128:k * 128 + ow],
                             h2_fm[:, k * T + tt * 512:k * T + (tt + 1) * 512],
                             start=(k == 0), stop=(k == KC - 1))
                    last = t.matmul(
                        pa[2 * (tt % 2) + 1][0:ow, :],
                        w2sb[par][:, k * 128:k * 128 + ow],
                        h2_fm[:, k * T + tt * 512:k * T + (tt + 1) * 512],
                        start=(k == 0), stop=(k == KC - 1))
                mmi[(oc, tt)] = iT.inc(last)
            TE.batch(t_12)

            def s_12(s, oc=oc, ow=ow, tt=tt, lin=lin):
                s.wait_ge(iT.h, mmi[(oc, tt)])
                if lin >= 2:
                    po, pt = divmod(lin - 2, 4)
                    s.wait_ge(iV.h, tti[(po, pt)])
                sili[(oc, tt)] = iS.inc(s.activation(
                    sgf[tt % 2][0:ow, :], pa[2 * (tt % 2)][0:ow, :], AF.Silu,
                    bias=bv12[0:ow, 0:1], scale=1.0))
            SE.batch(s_12)

            def v_12(v, oc=oc, ow=ow, tt=tt, lin=lin):
                v.wait_ge(iS.h, sili[(oc, tt)])
                if lin >= 2:
                    po, pt = divmod(lin - 2, 4)
                    v.wait_ge(iY[lin % 2].h, sti[(po, pt)])
                tti[(oc, tt)] = iV.inc(v.scalar_tensor_tensor(
                    a3t[tt % 2][0:ow, :], pa[2 * (tt % 2) + 1][0:ow, :],
                    bv12[0:ow, 1:2], sgf[tt % 2][0:ow, :],
                    ALU.add, ALU.mult))
            VE.batch(v_12)

            def y_12(sy, oc=oc, ow=ow, tt=tt, lin=lin):
                sy.wait_ge(iV.h, tti[(oc, tt)])
                sti[(oc, tt)] = iY[lin % 2].inc(sy.dma_start(
                    out=dram["act3s"][oc * 128:oc * 128 + ow,
                                      tt * 512:(tt + 1) * 512],
                    in_=a3t[tt % 2][0:ow, :]), 16)
            YE.batch(y_12)

    YE.batch(lambda sy: (sy.wait_ge(iY[0].h, iY[0].n),
                         sy.wait_ge(iY[1].h, iY[1].n)))
    iY_fin = (iY[0].n, iY[1].n)
    es10.close()
    esH2.close()
    C.barrier()

    es11 = ExitStack()
    sb11 = lambda n, s, d: es11.enter_context(nc.sbuf_tensor(n, s, d))
    a3 = sb11("at_a3", [128, NOC * 1024], F16)
    w3sb = [sb11(f"at_w3sb{i}", [128, NOC * 512], F16) for i in range(2)]
    xbld = [sb11(f"at_xbld{i}", [128, 512], F32) for i in range(2)]
    x1ld = [sb11(f"at_x1ld{i}", [128, 512], F32) for i in range(2)]
    ures = [sb11(f"at_ures{i}", [128, 512], F32) for i in range(2)]
    x2c = [sb11(f"at_x2c{i}", [128, 512], F32) for i in range(2)]

    zWp = [C.sem("gr_zWa"), C.sem("gr_zWb")]
    zA = C.sem("gr_zA")
    zT = C.sem("gr_pT")
    zV = C.sem("gr_iV")
    zXb = [C.sem("ffn_cX0"), C.sem("ffn_cX1")]
    zX1 = [C.sem("gr_fLa"), C.sem("gr_fLb")]
    zY = [C.sem("ffn_cY0"), C.sem("ffn_cY1")]

    mm3 = {}
    add3 = {}
    st3 = {}
    xbl = {}
    x1l = {}
    a3l = {}
    w3l = {}
    for pnl in range(2):
        def g_a3(g_, pnl=pnl):
            if pnl == 1:
                g_.wait_ge(zT.h, mm3[(0, 3, 7)])
            src = bass.AP(dram["act3s"], pnl * 1024,
                          [[T, 128], [128 * T, 42], [1, 1024]])
            dst = a3[:].rearrange("p (k c) -> p k c", c=1024)[:, 0:42, :]
            g_.dma_start(out=dst, in_=src).then_inc(zA.h, 16)
            zA.n += 16
            src2 = bass.AP(dram["act3s"], 42 * 128 * T + pnl * 1024,
                           [[T, 85], [1, 1024]])
            a3l[pnl] = zA.inc(g_.dma_start(
                out=a3[0:85, 42 * 1024:43 * 1024], in_=src2), 16)
        GE.batch(g_a3)

        for n in range(4):
            def g_w3(g_, pnl=pnl, n=n):
                lin2 = pnl * 4 + n
                wpar = lin2 % 2
                if lin2 >= 2:
                    pp, pn = divmod(lin2 - 2, 4)
                    g_.wait_ge(zT.h, mm3[(pp, pn, 7)])
                src = bass.AP(dram["w3"], n * 512,
                              [[D, 128], [128 * D, 42], [1, 512]])
                g_.dma_start(
                    out=w3sb[wpar][:].rearrange("p (k c) -> p k c", c=512)[:, 0:42, :],
                    in_=src).then_inc(zWp[wpar].h, 16)
                zWp[wpar].n += 16
                src2 = bass.AP(dram["w3"], 42 * 128 * D + n * 512,
                               [[D, 85], [1, 512]])
                w3l[(pnl, n)] = zWp[wpar].inc(g_.dma_start(
                    out=w3sb[wpar][0:85, 42 * 512:43 * 512], in_=src2), 16)
            GE.batch(g_w3)

            for ti in range(8):
                t_i = pnl * 8 + ti
                lin = (pnl * 4 + n) * 8 + ti
                lp = lin % 2

                def y_res(sy, t_i=t_i, n=n, lin=lin, lp=lp):
                    if lin >= 2:
                        sy.wait_ge(zV.h, add3[lin - 2])
                    xbl[lin] = zXb[lp].inc(sy.dma_start(
                        out=xbld[lp][:, :],
                        in_=dram["xbs"][t_i * 128:(t_i + 1) * 128,
                                        n * 512:(n + 1) * 512]), 16)
                    x1l[lin] = zX1[lp].inc(sy.dma_start(
                        out=x1ld[lp][:, :],
                        in_=dram["x1s"][t_i * 128:(t_i + 1) * 128,
                                        n * 512:(n + 1) * 512]), 16)
                YE.batch(y_res)

                def t_3(t, pnl=pnl, n=n, ti=ti, t_i=t_i, lin=lin):
                    wpar0 = (pnl * 4 + n) % 2
                    t.wait_ge(zWp[wpar0].h, w3l[(pnl, n)])
                    t.wait_ge(zA.h, a3l[pnl])
                    if lin >= 4:
                        t.wait_ge(zV.h, add3[lin - 4])
                    wpar = (pnl * 4 + n) % 2
                    last = None
                    for k in range(NOC):
                        kw = 85 if k == NOC - 1 else 128
                        last = t.matmul(
                            pa[lin % 4][:, :],
                            a3[0:kw, k * 1024 + ti * 128:k * 1024 + (ti + 1) * 128],
                            w3sb[wpar][0:kw, k * 512:(k + 1) * 512],
                            start=(k == 0), stop=(k == NOC - 1))
                    mm3[(pnl, n, ti)] = zT.inc(last)
                TE.batch(t_3)

                def v_3(v, pnl=pnl, n=n, ti=ti, lin=lin, lp=lp):
                    v.wait_ge(zT.h, mm3[(pnl, n, ti)])
                    v.wait_ge(zXb[lp].h, xbl[lin])
                    v.wait_ge(zX1[lp].h, x1l[lin])
                    if lin >= 2:
                        v.wait_ge(zY[lp].h, st3[lin - 2])
                    zV.inc(v.tensor_tensor(ures[lp][:, :], pa[lin % 4][:, :],
                                           xbld[lp][:, :], ALU.add))
                    v.wait_ge(zV.h, zV.n)
                    add3[lin] = zV.inc(v.tensor_tensor(
                        x2c[lp][:, :], ures[lp][:, :], x1ld[lp][:, :],
                        ALU.add))
                VE.batch(v_3)

                def y_3(sy, t_i=t_i, n=n, lin=lin, lp=lp):
                    sy.wait_ge(zV.h, add3[lin])
                    st3[lin] = zY[lp].inc(sy.dma_start(
                        out=dram["x2s"][t_i * 128:(t_i + 1) * 128,
                                        n * 512:(n + 1) * 512],
                        in_=x2c[lp][:, :]), 16)
                YE.batch(y_3)

    YE.batch(lambda sy: (sy.wait_ge(zY[0].h, zY[0].n),
                         sy.wait_ge(zY[1].h, zY[1].n)))
    es11.close()
    esA.close()


def host_consts():
    ident = np.eye(128, dtype=np.float16)
    dmask = np.zeros((128, NH * 128), dtype=np.float64)
    qdec = np.zeros((128, NH * 128), dtype=np.float64)
    kdec = np.zeros((128, NH * 128), dtype=np.float64)
    w = np.arange(128)
    u = np.arange(128)
    for h in range(NH):
        g = GAMMA[h]
        blk = np.where(u[None, :] >= w[:, None],
                       np.power(g, (u[None, :] - w[:, None]).astype(np.float64)),
                       0.0) * RDH
        dmask[:, h * 128:(h + 1) * 128] = blk
        qdec[:, h * 128:(h + 1) * 128] = np.power(g, (u + 1).astype(np.float64))[None, :]
        kdec[:, h * 128:(h + 1) * 128] = (np.power(g, (127 - w).astype(np.float64)) * RDH)[:, None]
    return (ident, dmask.astype(np.float16), qdec.astype(np.float16),
            kdec.astype(np.float16))


_BUILD_CACHE = {}


def kernel(**inputs):
    NC = 8
    nc = _BUILD_CACHE.get(NC)
    if nc is None:
        nc = build_nc(NC, "all")
        _BUILD_CACHE[NC] = nc

    x = np.asarray(inputs["L0_attn_x"])
    ident, dmask, qdec, kdec = host_consts()
    ROWG = I // NC
    ROWD = D // NC

    def asf32(a):
        return np.ascontiguousarray(np.asarray(a), dtype=np.float32)

    wg = asf32(inputs["L0_ffn_w_gate"])
    wu = asf32(inputs["L0_ffn_w_up"])
    wd = asf32(inputs["L0_ffn_w_down"])
    common = {
        "wq": asf32(inputs["L0_attn_wq"]),
        "wk": asf32(inputs["L0_attn_wk"]),
        "wv": asf32(inputs["L0_attn_wv"]),
        "wgp": asf32(inputs["L0_attn_wg"]),
        "wo": asf32(inputs["L0_attn_wo"]),
        "w1": asf32(inputs["L0_attn_w1"]),
        "w2": asf32(inputs["L0_attn_w2"]),
        "w3": asf32(inputs["L0_attn_w3"]),
        "n1w": asf32(inputs["L0_attn_norm1_w"]).reshape(1, D),
        "n1b": asf32(inputs["L0_attn_norm1_b"]).reshape(1, D),
        "n2w": asf32(inputs["L0_attn_norm2_w"]).reshape(1, D),
        "n2b": asf32(inputs["L0_attn_norm2_b"]).reshape(1, D),
        "ident": ident,
        "dmask": dmask,
        "qdec": qdec,
        "kdec": kdec,
    }
    in_maps = []
    for c in range(NC):
        b, half = c // 2, c % 2
        m = dict(common)
        m["x_sl"] = np.ascontiguousarray(
            x[b, half * T:(half + 1) * T, :], dtype=np.float32)
        m["wg_sl"] = np.ascontiguousarray(wg[c * ROWG:(c + 1) * ROWG, :])
        m["wu_sl"] = np.ascontiguousarray(wu[c * ROWG:(c + 1) * ROWG, :])
        m["wd_sl"] = np.ascontiguousarray(wd[c * ROWD:(c + 1) * ROWD, :])
        m["flag"] = np.array([[1.0 if half == 1 else 0.0]], dtype=np.float32)
        in_maps.append(m)

    res = run_bass_kernel_spmd(nc, in_maps, core_ids=list(range(NC)))
    out = np.empty((4, 4096, D), dtype=np.float32)
    for c in range(NC):
        b, half = c // 2, c % 2
        out[b, half * T:(half + 1) * T, :] = res.results[c]["out"]
    return out


# version 06
# speedup vs baseline: 1.0847x; 1.0847x over previous
import numpy as np
from contextlib import ExitStack

import concourse.bass as bass
import concourse.mybir as mybir
from concourse.bass_utils import run_bass_kernel_spmd

F32 = mybir.dt.float32
F16 = mybir.dt.float16
F8 = mybir.dt.float8e4
AF = mybir.ActivationFunctionType
ALU = mybir.AluOpType
AX = mybir.AxisListType

MAGIC = 12582912.0
EPS = 1e-5
D = 2048
T = 2048
I = 8192
HID = 5461
NH = 16
DH = 128
NT = T // 128
KC = D // 128
NO = I // 512
IC = I // 128
NW = 8192 * 2048
GAMMA = [1.0 - 2.0 ** (-5 - h) for h in range(NH)]
RDH = DH ** (-0.5)
INV127 = 1.0 / 127.0


class Sem:
    def __init__(self, es, nc, name):
        self.h = es.enter_context(nc.semaphore(name))
        self.n = 0

    def inc(self, instr, k=1):
        instr.then_inc(self.h, k)
        self.n += k
        return self.n


class Eng:
    def __init__(self, block, name):
        self._b = block
        self._n = name

    def batch(self, f):
        out = [None]

        def g(e):
            out[0] = f(e)

        getattr(self._b, self._n)(g)
        return out[0]

    def __getattr__(self, attr):
        def call(*a, **k):
            return self.batch(lambda e: getattr(e, attr)(*a, **k))

        return call


class Ctx:
    def __init__(self, nc, es, ncores):
        self.nc = nc
        self.es = es
        self.ncores = ncores
        self._sid = 0
        self.nbar = 0
        self.bar = None
        self.T = self.V = self.S = self.G = self.Y = None

    def sem(self, name):
        if not hasattr(self, "_semcache"):
            self._semcache = {}
        if name not in self._semcache:
            self._sid += 1
            self._semcache[name] = Sem(self.es, self.nc, f"{name}_{self._sid}")
        return self._semcache[name]

    def sb(self, name, shape, dtype):
        return self.es.enter_context(self.nc.sbuf_tensor(name, shape, dtype))

    def barrier(self):
        if not hasattr(self, "_semcache"):
            return
        sems = [s for s in self._semcache.values() if s.n > 0]
        for key in "TVSGY":
            eng = getattr(self, key)

            def f(e, sems=sems):
                for s in sems:
                    e.wait_ge(s.h, s.n)
            eng.batch(f)


def emit_prologue(C, dram, pers):
    nc = C.nc
    NC = C.ncores
    ROWG = I // NC
    ROWD = D // NC
    NRG = ROWG // 128
    NRD = ROWD // 128
    TE, VE, SE, GE, YE = C.T, C.V, C.S, C.G, C.Y

    wldp = [C.sem("p_wld0"), C.sem("p_wld1")]
    vst = C.sem("p_v")
    sst = C.sem("p_s")
    tst = C.sem("p_t")
    est = C.sem("p_e")
    ccs = C.sem("p_cc")
    sumd = C.sem("p_sumd")
    sumd2 = C.sem("p_sumd2")
    outd = C.sem("p_outd")
    fld = C.sem("p_fld")

    es2 = ExitStack()
    sbq = lambda n, s, d: es2.enter_context(nc.sbuf_tensor(n, s, d))
    psq = lambda n, s, d: es2.enter_context(nc.psum_tensor(n, s, d))

    row = [sbq(f"p_row{i}", [128, D], F32) for i in range(2)]
    rt = sbq("p_rt", [128, D], F32)
    q1 = sbq("p_q1", [128, D], F32)
    qrow = sbq("p_qrow", [128, D], F16)
    accw = None
    acc = sbq("p_acc", [128, 4], F32)
    sums = sbq("p_sums", [1, 4], F32)
    wsrow = sbq("p_wsrow", [1, 4], F32)
    winvrow = sbq("p_winvrow", [1, 4], F32)
    wgt_sb = sbq("p_wgt", [128, KC * ROWG], F8)
    wut_sb = sbq("p_wut", [128, KC * ROWG], F8)
    wdt_sb = sbq("p_wdt", [128, IC * ROWD], F8)
    pst = [psq(f"p_pst{i}", [128, 1024], F16) for i in range(2)]
    psb = psq("p_psb", [128, 512], F32)

    idl = C.sem("p_idl")
    YE.batch(lambda sy: idl.inc(
        sy.dma_start(out=pers["ident"][:, :], in_=dram["ident"][:, :]), 16))
    VE.batch(lambda v: (
        vst.inc(v.memset(pers["ones_col"][:, :], 1.0)),
        vst.inc(v.memset(pers["ones_row"][:, :], 1.0)),
        vst.inc(v.memset(pers["magic_b"][:, :], MAGIC)),
        vst.inc(v.memset(pers["eps_b"][:, :], EPS)),
        vst.inc(v.memset(acc[:, :], 0.0)),
    ))
    ones_cnt = vst.n

    items = ([(0, r, 0) for r in range(NRG)] +
             [(1, r, 0) for r in range(NRG)] +
             [(2, r, cc) for r in range(NRD) for cc in range(I // D)])
    slices = [dram["wg_sl"], dram["wu_sl"], dram["wd_sl"]]
    NIT = len(items)
    accw = sbq("p_accw", [128, NIT + 2], F32)

    red_cnt = {}
    wld_cnt = {}
    for j, (wi, r, cc) in enumerate(items):
        par = j % 2

        def g1(g, j=j, wi=wi, r=r, cc=cc, par=par):
            if j >= 2:
                g.wait_ge(vst.h, red_cnt[j - 2])
            wld_cnt[j] = wldp[par].inc(g.dma_start(
                out=row[par][:, :],
                in_=slices[wi][r * 128:(r + 1) * 128, cc * D:(cc + 1) * D]), 16)
        GE.batch(g1)

        def v1(v, j=j, par=par):
            v.wait_ge(wldp[par].h, wld_cnt[j])
            red_cnt[j] = vst.inc(
                v.tensor_reduce(accw[:, j:j + 1], row[par][:, :],
                                AX.X, ALU.add, apply_absolute_value=True))
        VE.batch(v1)

    off = 0
    for wi, n in enumerate([NRG, NRG, NRD * (I // D)]):
        def v2(v, wi=wi, off=off, n=n):
            v.wait_ge(vst.h, red_cnt[NIT - 1])
            vst.inc(v.tensor_reduce(acc[:, wi:wi + 1], accw[:, off:off + n],
                                    AX.X, ALU.add))
        VE.batch(v2)
        off += n

    TE.batch(lambda t: (
        t.wait_ge(vst.h, vst.n),
        tst.inc(t.matmul(psb[0:1, 0:4], pers["ones_col"][:, 0:1], acc[:, 0:4],
                         start=True, stop=True)),
    ))
    VE.batch(lambda v: (
        v.wait_ge(tst.h, tst.n),
        vst.inc(v.tensor_copy(sums[:, :], psb[0:1, 0:4])),
    ))
    YE.batch(lambda sy: (
        sy.wait_ge(vst.h, vst.n),
        sumd.inc(sy.dma_start(out=dram["ar_in"][:, :], in_=sums[:, :]), 16),
    ))
    GE.batch(lambda g: (
        g.wait_ge(sumd.h, 16),
        ccs.inc(g.collective_compute(
            "AllReduce", ALU.add, replica_groups=[list(range(NC))],
            ins=[dram["ar_in"].ap().opt()], outs=[dram["ar_out"].ap().opt()])),
    ))
    GE.batch(lambda g: (
        g.wait_ge(ccs.h, 1),
        sumd2.inc(g.dma_start(out=sums[:, :], in_=dram["ar_out"][:, :]), 16),
    ))
    VE.batch(lambda v: (
        v.wait_ge(sumd2.h, 16),
        vst.inc(v.tensor_scalar(wsrow[:, :], sums[:, :], 1.0 / NW, EPS,
                                ALU.mult, ALU.add)),
    ))
    VE.batch(lambda v: (
        v.wait_ge(vst.h, vst.n),
        vst.inc(v.reciprocal(winvrow[:, :], wsrow[:, :])),
    ))
    YE.batch(lambda sy: fld.inc(
        sy.dma_start(out=pers["flag_sb"][:, :], in_=dram["flag"][:, :]), 16))
    TE.batch(lambda t: (
        t.wait_ge(vst.h, vst.n),
        t.wait_ge(fld.h, 16),
        tst.inc(t.matmul(psb[:, 4:8], pers["ones_row"][0:1, 0:128],
                         wsrow[0:1, 0:4], start=True, stop=True)),
        tst.inc(t.matmul(psb[:, 8:12], pers["ones_row"][0:1, 0:128],
                         winvrow[0:1, 0:4], start=True, stop=True)),
        tst.inc(t.matmul(psb[:, 12:13], pers["ones_row"][0:1, 0:128],
                         pers["flag_sb"][0:1, 0:1], start=True, stop=True)),
    ))
    VE.batch(lambda v: (
        v.wait_ge(tst.h, tst.n),
        vst.inc(v.tensor_copy(pers["ws_b"][:, :], psb[:, 4:8])),
        vst.inc(v.tensor_copy(pers["winv_b"][:, :], psb[:, 8:12])),
        vst.inc(v.tensor_copy(pers["flag_b"][:, :], psb[:, 12:13])),
    ))
    bcast_cnt = vst.n

    r1_cnt = {}
    q_cnt = {}
    ev_hist = []
    tr_hist = []

    gidx = 0
    for j, (wi, r, cc) in enumerate(items):
        par = j % 2
        ngr = 2

        def g2(g, j=j, wi=wi, r=r, cc=cc, par=par):
            if j >= 2:
                g.wait_ge(sst.h, r1_cnt[j - 2])
            wld_cnt[j] = wldp[par].inc(g.dma_start(
                out=row[par][:, :],
                in_=slices[wi][r * 128:(r + 1) * 128, cc * D:(cc + 1) * D]), 16)
        GE.batch(g2)

        def s2(s, j=j, wi=wi, par=par):
            s.wait_ge(wldp[par].h, wld_cnt[j])
            s.wait_ge(vst.h, bcast_cnt)
            if j >= 1:
                s.wait_ge(vst.h, q_cnt[j - 1][0])
            r1_cnt[j] = sst.inc(s.activation(
                rt[:, :], row[par][:, :], AF.Identity,
                bias=pers["magic_b"][:, 0:1], scale=pers["winv_b"][:, wi:wi + 1]))
        SE.batch(s2)

        def v3(v, j=j):
            v.wait_ge(sst.h, r1_cnt[j])
            if j >= 1:
                v.wait_ge(vst.h, q_cnt[j - 1][1])
            a = vst.inc(v.tensor_scalar(q1[:, :], rt[:, :], -MAGIC,
                                        -1.0, ALU.add, ALU.max))
            v.wait_ge(vst.h, a)
            if tr_hist:
                v.wait_ge(tst.h, tr_hist[-1])
            bcnt = vst.inc(v.tensor_scalar(qrow[:, :], q1[:, :],
                                           1.0, None, ALU.min))
            q_cnt[j] = (a, bcnt)
        VE.batch(v3)

        dst_sb, rowg = [(wgt_sb, ROWG), (wut_sb, ROWG), (wdt_sb, ROWD)][wi]
        for gi in range(ngr):
            def t4(t, j=j, gi=gi, gidx=gidx):
                t.wait_ge(vst.h, q_cnt[j][1])
                t.wait_ge(idl.h, 16)
                if gidx >= 2:
                    t.wait_ge(est.h, ev_hist[gidx - 2])
                last = None
                for k in range(8):
                    kk = gi * 8 + k
                    last = t.matmul(pst[gidx % 2][:, k * 128:(k + 1) * 128],
                                    qrow[:, kk * 128:(kk + 1) * 128],
                                    pers["ident"][:, :], is_transpose=True,
                                    start=True, stop=True)
                tr_hist.append(tst.inc(last))
            TE.batch(t4)

            def v4(v, gi=gi, gidx=gidx, dst_sb=dst_sb, rowg=rowg, r=r, cc=cc):
                v.wait_ge(tst.h, tr_hist[gidx])
                out_ap = dst_sb[:].rearrange("p (k c) -> p k c", c=rowg)[
                    :, cc * KC + gi * 8: cc * KC + (gi + 1) * 8,
                    r * 128:(r + 1) * 128]
                ev_hist.append(est.inc(v.tensor_copy(out_ap, pst[gidx % 2][:, :])))
            VE.batch(v4)
            gidx += 1

    SZ = D * I // NC

    def g3(g):
        g.wait_ge(est.h, est.n)
        for wi, (src_sb, rowg) in enumerate(((wgt_sb, ROWG), (wut_sb, ROWG),
                                             (wdt_sb, ROWD))):
            nkc = (D if wi < 2 else I) // 128
            src = src_sb[:].rearrange("p (k c) -> p k c", c=rowg)
            dstap = bass.AP(dram["agw_in"], wi * SZ,
                            [[rowg, 128], [128 * rowg, nkc], [1, rowg]])
            outd.inc(g.dma_start(out=dstap, in_=src), 16)
    GE.batch(g3)

    GE.batch(lambda g: (
        g.wait_ge(outd.h, outd.n),
        ccs.inc(g.collective_compute(
            "AllGather", ALU.bypass, replica_groups=[list(range(NC))],
            ins=[dram["agw_in"].ap().opt()],
            outs=[dram["agw"].ap().opt()])),
    ))
    GE.batch(lambda g: g.wait_ge(ccs.h, ccs.n))
    C.pro_outd = (outd, outd.n)
    C.pro_ag = (ccs, ccs.n)
    C.pro_ws = (vst, vst.n)
    es2.close()


def emit_ffn(C, dram, pers, x_src, x_dst, tag, after_prologue=False):
    nc = C.nc
    NC = C.ncores
    RCG = I // NC
    RCD = D // NC
    TE, VE, SE, GE, YE = C.T, C.V, C.S, C.G, C.Y

    aLp = [C.sem("ffn_aL0"), C.sem("ffn_aL1")]
    aV = C.sem("ffn_aV")
    aS = C.sem("ffn_aS")
    aT = C.sem("ffn_aT")
    aG = C.sem("ffn_aG")
    aE = C.sem("ffn_aE")
    bWp = [C.sem("ffn_bW0"), C.sem("ffn_bW1")]
    bT = C.sem("ffn_bT")
    bS = C.sem("ffn_bS")
    bV = C.sem("ffn_bV")
    bYp = [C.sem("ffn_bY0"), C.sem("ffn_bY1")]
    cLp = [C.sem("ffn_cL0"), C.sem("ffn_cL1")]
    cV = C.sem("ffn_cV")
    cS = C.sem("ffn_cS")
    cT = C.sem("ffn_cT")
    cE = C.sem("ffn_cE")
    cWp = [C.sem("ffn_cW0"), C.sem("ffn_cW1")]
    cXp = [C.sem("ffn_cX0"), C.sem("ffn_cX1")]
    cYp = [C.sem("ffn_cY0"), C.sem("ffn_cY1")]

    es2 = ExitStack()
    sbq = lambda n, s, d: es2.enter_context(nc.sbuf_tensor(f"{tag}{n}", s, d))
    psq = lambda n, s, d: es2.enter_context(nc.psum_tensor(f"{tag}{n}", s, d))

    bnbuf = sbq("bn", [128, 24], F32)
    mv = sbq("mv", [128, 2], F32)
    lnv_all = sbq("lnv", [128, NT], F32)
    rstd_all = sbq("rstd", [128, NT], F32)
    nmr_all = sbq("nmr", [128, NT], F32)
    m_all = sbq("m", [128, NT], F32)
    s_all = sbq("s", [128, NT], F32)
    sinv_all = sbq("sinv", [128, NT], F32)
    sgv_all = sbq("sgv", [128, NT], F32)
    suv_all = sbq("suv", [128, NT], F32)
    m2_all = sbq("m2", [128, NT], F32)
    m2p = sbq("m2p", [128, NT * NO], F32)
    s2_all = sbq("s2", [128, NT], F32)
    sinv2_all = sbq("sinv2", [128, NT], F32)
    dnv_all = sbq("dnv", [128, NT], F32)

    pb = [psq(f"pb{i}", [128, 512], F32) for i in range(4)]
    pstf = [psq(f"pst{i}", [128, 1024], F16) for i in range(2)]

    esAB = ExitStack()
    sab = lambda n, s, d: esAB.enter_context(nc.sbuf_tensor(f"{tag}{n}", s, d))
    xt = [sab(f"xt{i}", [128, D], F32) for i in range(2)]
    ln_t = [sab(f"ln{i}", [128, D], F32) for i in range(2)]
    rt1 = [sab(f"rt1{i}", [128, D], F32) for i in range(2)]
    zq_tm = [sab(f"zqtm{i}", [128, D], F16) for i in range(2)]
    zq_fm = sab("zqfm", [128, KC * T], F16)
    wgt = [sab(f"wgt{i}", [128, KC * 512], F8) for i in range(2)]
    wut = [sab(f"wut{i}", [128, KC * 512], F8) for i in range(2)]
    sg = [sab(f"sgb{i}", [128, 512], F16) for i in range(2)]
    act_st = [sab(f"acst{i}", [128, 512], F16) for i in range(2)]

    ap_cnt = {}
    bn_cnt = {}
    r2_cnt = {}
    tr_cnt = {}
    ev_cnt = {}
    rm_cnt = {}
    r1_cnt = {}
    ln_cnt = {}
    nmr_cnt = {}
    al_cnt = {}

    for j in range(NT):
        par = j % 2

        def yl(sy, j=j, par=par):
            if after_prologue and j < 2:
                sy.wait_ge(C.pro_outd[0].h, C.pro_outd[1])
            if j >= 2:
                sy.wait_ge(aV.h, bn_cnt[j - 2])
                sy.wait_ge(aS.h, ap_cnt[j - 2])
            al_cnt[j] = aLp[par].inc(sy.dma_start(
                out=xt[par][:, :], in_=x_src[j * 128:(j + 1) * 128, :]), 16)
        YE.batch(yl)

        def v_stats(v, j=j, par=par):
            v.wait_ge(aLp[par].h, al_cnt[j])
            if j >= 1:
                v.wait_ge(aS.h, ln_cnt[j - 1])
                v.wait_ge(aV.h, nmr_cnt[j - 1])
            for q in range(4):
                aV.inc(v.bn_stats(bnbuf[:, q * 6:(q + 1) * 6],
                                  xt[par][:, q * 512:(q + 1) * 512]))
            v.wait_ge(aV.h, aV.n)
            bn_cnt[j] = aV.inc(v.bn_aggr(mv[:, :], bnbuf[:, :]))
        VE.batch(v_stats)

        def s_rstd(s, j=j):
            s.wait_ge(aV.h, bn_cnt[j])
            ln_cnt[j] = aS.inc(s.activation(lnv_all[:, j:j + 1], mv[:, 1:2],
                                            AF.Ln, bias=pers["eps_b"][:, 0:1], scale=1.0))
            s.wait_ge(aS.h, aS.n)
            aS.inc(s.activation(rstd_all[:, j:j + 1], lnv_all[:, j:j + 1],
                                AF.Exp, bias=0.0, scale=-0.5))
        SE.batch(s_rstd)

        def v_nmr(v, j=j):
            v.wait_ge(aS.h, aS.n)
            nmr_cnt[j] = aV.inc(v.tensor_scalar(
                nmr_all[:, j:j + 1], mv[:, 0:1],
                rstd_all[:, j:j + 1], -1.0, ALU.mult, ALU.mult))
        VE.batch(v_nmr)

        def s_apply(s, j=j, par=par):
            s.wait_ge(aV.h, aV.n)
            if j >= 2:
                s.wait_ge(aV.h, rm_cnt[j - 2])
                s.wait_ge(aS.h, r1_cnt[j - 2])
            ap_cnt[j] = aS.inc(s.activation(
                ln_t[par][:, :], xt[par][:, :], AF.Identity,
                bias=nmr_all[:, j:j + 1], scale=rstd_all[:, j:j + 1]))
        SE.batch(s_apply)

        def v_quant(v, j=j, par=par):
            v.wait_ge(aS.h, ap_cnt[j])
            rm_cnt[j] = aV.inc(v.tensor_reduce(
                m_all[:, j:j + 1], ln_t[par][:, :], AX.X, ALU.max,
                apply_absolute_value=True))
            v.wait_ge(aV.h, aV.n)
            aV.inc(v.tensor_scalar(s_all[:, j:j + 1], m_all[:, j:j + 1],
                                   INV127, EPS, ALU.mult, ALU.add))
            v.wait_ge(aV.h, aV.n)
            aV.inc(v.reciprocal(sinv_all[:, j:j + 1], s_all[:, j:j + 1]))
        VE.batch(v_quant)

        def s_round1(s, j=j, par=par):
            s.wait_ge(aV.h, aV.n)
            if j >= 2:
                s.wait_ge(aG.h, r2_cnt[j - 2])
            r1_cnt[j] = aS.inc(s.activation(
                rt1[par][:, :], ln_t[par][:, :], AF.Identity,
                bias=pers["magic_b"][:, 0:1], scale=sinv_all[:, j:j + 1]))
        SE.batch(s_round1)

        def v_round2(v, j=j, par=par):
            v.wait_ge(aS.h, r1_cnt[j])
            if j >= 2:
                v.wait_ge(aT.h, tr_cnt[(j - 2, 1)])
            r2_cnt[j] = aG.inc(v.tensor_scalar(
                zq_tm[par][:, :], rt1[par][:, :], -MAGIC, None, ALU.add))
        VE.batch(v_round2)

        for gi in range(2):
            def t_tr(t, j=j, gi=gi, par=par):
                t.wait_ge(aG.h, r2_cnt[j])
                if after_prologue and j < 1:
                    t.wait_ge(C.pro_outd[0].h, C.pro_outd[1])
                gl = j * 2 + gi
                if gl >= 2:
                    jj, gg = divmod(gl - 2, 2)
                    t.wait_ge(aE.h, ev_cnt[(jj, gg)])
                last = None
                for k in range(8):
                    kk = gi * 8 + k
                    last = t.matmul(pstf[gi][:, k * 128:(k + 1) * 128],
                                    zq_tm[par][:, kk * 128:(kk + 1) * 128],
                                    pers["ident"][:, :], is_transpose=True,
                                    start=True, stop=True)
                tr_cnt[(j, gi)] = aT.inc(last)
            TE.batch(t_tr)

            def v_ev(v, j=j, gi=gi):
                v.wait_ge(aT.h, tr_cnt[(j, gi)])
                out_ap = zq_fm[:].rearrange("p (k c) -> p k c", c=T)[
                    :, gi * 8:(gi + 1) * 8, j * 128:(j + 1) * 128]
                ev_cnt[(j, gi)] = aE.inc(v.tensor_copy(out_ap, pstf[gi][:, :]))
            VE.batch(v_ev)

    def v_sgv(v):
        v.wait_ge(aV.h, aV.n)
        if after_prologue:
            v.wait_ge(C.pro_ws[0].h, C.pro_ws[1])
        aV.inc(v.tensor_scalar(sgv_all[:, :], s_all[:, :],
                               pers["ws_b"][:, 0:1], None, ALU.mult))
        aV.inc(v.tensor_scalar(suv_all[:, :], s_all[:, :],
                               pers["ws_b"][:, 1:2], None, ALU.mult))
    VE.batch(v_sgv)
    sgv_cnt = aV.n

    SZ = D * I // NC

    def wtile_ap(wsel, ot):
        rank_cols = RCG
        o0 = ot * 512
        rank = o0 // rank_cols
        col = o0 % rank_cols
        base = rank * (3 * SZ) + wsel * SZ + col
        return bass.AP(dram["agw"], base,
                       [[rank_cols, 128], [128 * rank_cols, KC], [1, 512]])

    mmB = {}
    silB = {}
    sttB = {}
    stB = {}
    wldB = {}

    for ot in range(NO):
        def g_w(g, ot=ot):
            par = ot % 2
            if after_prologue and ot < 2:
                g.wait_ge(C.pro_ag[0].h, C.pro_ag[1])
            if ot >= 2:
                g.wait_ge(bT.h, mmB[(ot - 2, NT - 1)])
            bWp[par].inc(g.dma_start(
                out=wgt[par][:].rearrange("p (k c) -> p k c", c=512),
                in_=wtile_ap(0, ot)), 16)
            bWp[par].inc(g.dma_start(
                out=wut[par][:].rearrange("p (k c) -> p k c", c=512),
                in_=wtile_ap(1, ot)), 16)
            wldB[ot] = bWp[par].n
        GE.batch(g_w)

        for j in range(NT):
            par = ot % 2
            jp = j % 2
            lin = ot * NT + j

            def t_mm(t, ot=ot, j=j, par=par, jp=jp, lin=lin):
                t.wait_ge(bWp[par].h, wldB[ot])
                if lin == 0:
                    t.wait_ge(aE.h, aE.n)
                if lin >= 2:
                    oo, jj = divmod(lin - 2, NT)
                    t.wait_ge(bS.h, silB[(oo, jj)])
                    t.wait_ge(bV.h, sttB[(oo, jj)])
                last = None
                for k in range(KC):
                    lhs = zq_fm[:, k * T + j * 128: k * T + (j + 1) * 128]
                    t.matmul(pb[jp][:, :], lhs,
                             wgt[par][:, k * 512:(k + 1) * 512],
                             start=(k == 0), stop=(k == KC - 1))
                    last = t.matmul(pb[2 + jp][:, :], lhs,
                                    wut[par][:, k * 512:(k + 1) * 512],
                                    start=(k == 0), stop=(k == KC - 1))
                mmB[(ot, j)] = bT.inc(last)
            TE.batch(t_mm)

            def s_sil(s, ot=ot, j=j, jp=jp, lin=lin):
                s.wait_ge(bT.h, mmB[(ot, j)])
                s.wait_ge(aV.h, sgv_cnt)
                if lin >= 2:
                    oo, jj = divmod(lin - 2, NT)
                    s.wait_ge(bV.h, sttB[(oo, jj)])
                silB[(ot, j)] = bS.inc(s.activation(
                    sg[jp][:, :], pb[jp][:, :], AF.Silu,
                    bias=0.0, scale=sgv_all[:, j:j + 1]))
            SE.batch(s_sil)

            def v_stt(v, ot=ot, j=j, jp=jp, lin=lin):
                v.wait_ge(bS.h, silB[(ot, j)])
                if lin >= 2:
                    oo, jj = divmod(lin - 2, NT)
                    v.wait_ge(bYp[jj % 2].h, stB[(oo, jj)])
                a = bV.inc(v.scalar_tensor_tensor(
                    act_st[jp][:, :], pb[2 + jp][:, :], suv_all[:, j:j + 1],
                    sg[jp][:, :], ALU.mult, ALU.mult))
                v.wait_ge(bV.h, a)
                sttB[(ot, j)] = bV.inc(v.tensor_reduce(
                    m2p[:, j * NO + ot:j * NO + ot + 1], act_st[jp][:, :],
                    AX.X, ALU.max, apply_absolute_value=True))
            VE.batch(v_stt)

            def y_st(sy, ot=ot, j=j, jp=jp):
                sy.wait_ge(bV.h, sttB[(ot, j)])
                stB[(ot, j)] = bYp[jp].inc(sy.dma_start(
                    out=dram["acts"][j * 128:(j + 1) * 128,
                                     ot * 512:(ot + 1) * 512],
                    in_=act_st[jp][:, :]), 16)
            YE.batch(y_st)

    esAB.close()
    bY_fin = (bYp[0].n, bYp[1].n)
    C.barrier()

    aE_fin = aE.n
    esC = ExitStack()
    scb = lambda n, s, d: esC.enter_context(nc.sbuf_tensor(f"{tag}{n}", s, d))
    act_row = [scb(f"actr{i}", [128, I], F16) for i in range(2)]
    rt2 = [scb(f"rt2{i}", [128, I // 2], F32) for i in range(2)]
    aq_tm = [scb(f"aqtm{i}", [128, I], F16) for i in range(2)]
    aq_fm = scb("aqfm", [128, IC * 512], F16)
    wdt = [scb(f"wdtb{i}", [128, 8 * 512], F8) for i in range(2)]
    td = [scb(f"td{i}", [128, 512], F32) for i in range(2)]
    xc = [scb(f"xcb{i}", [128, 512], F32) for i in range(2)]
    xadd = [scb(f"xad{i}", [128, 512], F32) for i in range(2)]

    ldC = {}
    rmC = {}
    dnvC = {}
    r1C = {}
    r2C = {}
    trC = {}
    evC = {}
    wldC = {}
    mmC = {}
    evacC = {}
    addC = {}
    xldC = {}
    stC = {}

    for p in range(4):
        for jj in range(4):
            j = p * 4 + jj
            par = j % 2

            def y_act(sy, j=j, par=par):
                if j >= 2:
                    sy.wait_ge(cV.h, rmC[j - 2])
                    sy.wait_ge(cS.h, r1C[(j - 2, 1)])
                sy.wait_ge(bYp[0].h, bY_fin[0])
                sy.wait_ge(bYp[1].h, bY_fin[1])
                ldC[j] = cLp[par].inc(sy.dma_start(
                    out=act_row[par][:, :],
                    in_=dram["acts"][j * 128:(j + 1) * 128, :]), 16)
            YE.batch(y_act)

            def v_q(v, j=j, par=par):
                v.wait_ge(cLp[par].h, ldC[j])
                rmC[j] = cV.inc(v.tensor_reduce(
                    m2_all[:, j:j + 1], m2p[:, j * NO:(j + 1) * NO], AX.X,
                    ALU.max))
                v.wait_ge(cV.h, cV.n)
                cV.inc(v.tensor_scalar(s2_all[:, j:j + 1], m2_all[:, j:j + 1],
                                       INV127, EPS, ALU.mult, ALU.add))
                v.wait_ge(cV.h, cV.n)
                cV.inc(v.reciprocal(sinv2_all[:, j:j + 1], s2_all[:, j:j + 1]))
                v.wait_ge(cV.h, cV.n)
                dnvC[j] = cV.inc(v.tensor_scalar(
                    dnv_all[:, j:j + 1], s2_all[:, j:j + 1],
                    pers["ws_b"][:, 2:3], 0.5, ALU.mult, ALU.mult))
            VE.batch(v_q)

            for half in range(2):
                def s_r1(s, j=j, par=par, half=half):
                    s.wait_ge(cV.h, dnvC[j])
                    if j >= 1:
                        s.wait_ge(cV.h, r2C[(j - 1, half)])
                    r1C[(j, half)] = cS.inc(s.activation(
                        rt2[half][:, :],
                        act_row[par][:, half * (I // 2):(half + 1) * (I // 2)],
                        AF.Identity, bias=pers["magic_b"][:, 0:1],
                        scale=sinv2_all[:, j:j + 1]))
                SE.batch(s_r1)

                def v_r2(v, j=j, par=par, half=half):
                    v.wait_ge(cS.h, r1C[(j, half)])
                    if j >= 2 and half == 0:
                        v.wait_ge(cT.h, trC[(j - 2, 7)])
                    r2C[(j, half)] = cV.inc(v.tensor_scalar(
                        aq_tm[par][:, half * (I // 2):(half + 1) * (I // 2)],
                        rt2[half][:, :], -MAGIC, None, ALU.add))
                VE.batch(v_r2)
            for gi in range(8):
                def t_tr(t, j=j, jj=jj, gi=gi, par=par, p=p):
                    t.wait_ge(cV.h, r2C[(j, 1)] if gi >= 4 else r2C[(j, 0)])
                    gl = j * 8 + gi
                    if gl >= 2:
                        pjj, pg = divmod(gl - 2, 8)
                        t.wait_ge(cE.h, evC[(pjj, pg)])
                    if jj == 0 and gi == 0 and p >= 1:
                        t.wait_ge(cT.h, mmC[(p - 1, 3, 7)])
                    last = None
                    for k in range(8):
                        kk = gi * 8 + k
                        last = t.matmul(pstf[gi % 2][:, k * 128:(k + 1) * 128],
                                        aq_tm[par][:, kk * 128:(kk + 1) * 128],
                                        pers["ident"][:, :], is_transpose=True,
                                        start=True, stop=True)
                    trC[(j, gi)] = cT.inc(last)
                TE.batch(t_tr)

                def v_ev(v, j=j, jj=jj, gi=gi):
                    v.wait_ge(cT.h, trC[(j, gi)])
                    out_ap = aq_fm[:].rearrange("p (k c) -> p k c", c=512)[
                        :, gi * 8:(gi + 1) * 8, jj * 128:(jj + 1) * 128]
                    evC[(j, gi)] = cE.inc(v.tensor_copy(out_ap,
                                                        pstf[gi % 2][:, :]))
                VE.batch(v_ev)

        for h in range(4):
            for kg in range(8):
                def g_wd(g, p=p, h=h, kg=kg):
                    par = kg % 2
                    lin = h * 8 + kg
                    if lin >= 2:
                        hh, kk = divmod(lin - 2, 8)
                        g.wait_ge(cT.h, mmC[(p, hh, kk)])
                    elif p >= 1:
                        g.wait_ge(cT.h, mmC[(p - 1, 3, 6 + lin)])
                    o0 = h * 512
                    nr = 512 // RCD
                    SZl = D * I // NC
                    for ri in range(nr):
                        rank = o0 // RCD + ri
                        base = rank * (3 * SZl) + 2 * SZl
                        src = bass.AP(dram["agw"], base + (kg * 8 * 128) * RCD,
                                      [[RCD, 128], [128 * RCD, 8], [1, RCD]])
                        dst = wdt[par][:].rearrange("p (k c) -> p k c", c=512)[
                            :, :, ri * RCD:(ri + 1) * RCD]
                        cWp[par].inc(g.dma_start(out=dst, in_=src), 16)
                    wldC[(p, h, kg)] = cWp[par].n
                GE.batch(g_wd)

                def t_dn(t, p=p, h=h, kg=kg):
                    par = kg % 2
                    t.wait_ge(cWp[par].h, wldC[(p, h, kg)])
                    if kg == 0 and h >= 1:
                        for jj in range(4):
                            t.wait_ge(cS.h, evacC[(p, h - 1, jj)])
                    last = None
                    for jj in range(4):
                        if kg == 0 and h == 0:
                            t.wait_ge(cE.h, evC[(p * 4 + jj, 7)])
                        for k in range(8):
                            kk = kg * 8 + k
                            last = t.matmul(
                                pb[jj][:, :],
                                aq_fm[:, kk * 512 + jj * 128:
                                      kk * 512 + (jj + 1) * 128],
                                wdt[par][:, k * 512:(k + 1) * 512],
                                start=(kg == 0 and k == 0),
                                stop=(kg == 7 and k == 7))
                    mmC[(p, h, kg)] = cT.inc(last)
                TE.batch(t_dn)

            for jj in range(4):
                j = p * 4 + jj
                lin = (p * 4 + h) * 4 + jj

                def y_x(sy, j=j, h=h, lin=lin):
                    if lin >= 2:
                        sy.wait_ge(cV.h, addC[lin - 2])
                    xldC[lin] = cXp[lin % 2].inc(sy.dma_start(
                        out=xc[lin % 2][:, :],
                        in_=x_src[j * 128:(j + 1) * 128,
                                  h * 512:(h + 1) * 512]), 16)
                YE.batch(y_x)

                def s_ev(s, p=p, h=h, jj=jj, j=j, lin=lin):
                    s.wait_ge(cT.h, mmC[(p, h, 7)])
                    s.wait_ge(cV.h, dnvC[j])
                    if lin >= 2:
                        s.wait_ge(cV.h, addC[lin - 2])
                    evacC[(p, h, jj)] = cS.inc(s.activation(
                        td[lin % 2][:, :], pb[jj][:, :], AF.Identity,
                        bias=0.0, scale=dnv_all[:, j:j + 1]))
                SE.batch(s_ev)

                def v_add(v, p=p, h=h, jj=jj, lin=lin):
                    v.wait_ge(cS.h, evacC[(p, h, jj)])
                    v.wait_ge(cXp[lin % 2].h, xldC[lin])
                    if lin >= 2:
                        v.wait_ge(cYp[lin % 2].h, stC[lin - 2])
                    addC[lin] = cV.inc(v.tensor_tensor(
                        xadd[lin % 2][:, :], td[lin % 2][:, :],
                        xc[lin % 2][:, :], ALU.add))
                VE.batch(v_add)

                def y_st(sy, p=p, h=h, jj=jj, j=j, lin=lin):
                    sy.wait_ge(cV.h, addC[lin])
                    stC[lin] = cYp[lin % 2].inc(sy.dma_start(
                        out=x_dst[j * 128:(j + 1) * 128,
                                  h * 512:(h + 1) * 512],
                        in_=xadd[lin % 2][:, :]), 16)
                YE.batch(y_st)

    YE.batch(lambda sy: (sy.wait_ge(cYp[0].h, cYp[0].n),
                         sy.wait_ge(cYp[1].h, cYp[1].n)))
    esC.close()
    es2.close()


def build_nc(ncores=8, upto="all"):
    nc = bass.Bass("TRN2", target_bir_lowering=False)
    NC = ncores
    ROWG = I // NC
    ROWD = D // NC

    dram = {}
    dram["x_sl"] = nc.dram_tensor("x_sl", [T, D], F32, kind="ExternalInput")
    for nm in ("wq", "wk", "wv", "wgp", "wo"):
        dram[nm] = nc.dram_tensor(nm, [D, D], F32, kind="ExternalInput")
    dram["w1"] = nc.dram_tensor("w1", [D, HID], F32, kind="ExternalInput")
    dram["w2"] = nc.dram_tensor("w2", [D, HID], F32, kind="ExternalInput")
    dram["w3"] = nc.dram_tensor("w3", [HID, D], F32, kind="ExternalInput")
    dram["wg_sl"] = nc.dram_tensor("wg_sl", [ROWG, D], F32, kind="ExternalInput")
    dram["wu_sl"] = nc.dram_tensor("wu_sl", [ROWG, D], F32, kind="ExternalInput")
    dram["wd_sl"] = nc.dram_tensor("wd_sl", [ROWD, I], F32, kind="ExternalInput")
    for nm in ("n1w", "n1b", "n2w", "n2b"):
        dram[nm] = nc.dram_tensor(nm, [1, D], F32, kind="ExternalInput")
    dram["ident"] = nc.dram_tensor("ident", [128, 128], F16, kind="ExternalInput")
    dram["dmask"] = nc.dram_tensor("dmask", [128, NH * 128], F16, kind="ExternalInput")
    dram["qdec"] = nc.dram_tensor("qdec", [128, NH * 128], F16, kind="ExternalInput")
    dram["kdec"] = nc.dram_tensor("kdec", [128, NH * 128], F16, kind="ExternalInput")
    dram["flag"] = nc.dram_tensor("flag", [1, 1], F32, kind="ExternalInput")
    dram["out"] = nc.dram_tensor("out", [T, D], F32, kind="ExternalOutput")

    dram["x1s"] = nc.dram_tensor("x1s", [T, D], F32)
    dram["x2s"] = nc.dram_tensor("x2s", [T, D], F32)
    dram["xas"] = nc.dram_tensor("xas", [T, D], F32)
    dram["xbs"] = nc.dram_tensor("xbs", [T, D], F32)
    dram["acts"] = nc.dram_tensor("acts", [T, I], F16)
    dram["retgns"] = nc.dram_tensor("retgns", [T, D], F16)
    dram["stashs"] = nc.dram_tensor("stashs", [T, D], F16)
    dram["hs"] = nc.dram_tensor("hs", [D, T], F16)
    dram["act3s"] = nc.dram_tensor("act3s", [43 * 128, T], F16)
    dram["ar_in"] = nc.dram_tensor("ar_in", [1, 4], F32)
    dram["ar_out"] = nc.dram_tensor("ar_out", [1, 4], F32)
    SZ = D * I // NC
    dram["agw_in"] = nc.dram_tensor("agw_in", [3 * SZ], F8)
    dram["agw"] = nc.dram_tensor("agw", [NC * 3 * SZ], F8, addr_space="Shared")
    dram["st_in"] = [nc.dram_tensor(f"st_in{i}", [8 * 128, 128], F16)
                     for i in range(2)]
    dram["st_out"] = [nc.dram_tensor(f"st_out{i}", [2 * 8 * 128, 128], F16)
                      for i in range(2)]

    es = ExitStack()
    C = Ctx(nc, es, ncores)
    pers = {}
    pers["ident"] = C.sb("ident_sb", [128, 128], F16)
    pers["ones_col"] = C.sb("ones_col", [128, 1], F32)
    pers["ones_row"] = C.sb("ones_row", [1, 512], F32)
    pers["ws_b"] = C.sb("ws_b", [128, 4], F32)
    pers["winv_b"] = C.sb("winv_b", [128, 4], F32)
    pers["flag_sb"] = C.sb("flag_sb", [1, 1], F32)
    pers["flag_b"] = C.sb("flag_b", [128, 1], F32)
    pers["magic_b"] = C.sb("magic_b", [128, 1], F32)
    pers["eps_b"] = C.sb("eps_b", [128, 1], F32)

    block = es.enter_context(nc.Block())
    C.T = Eng(block, "tensor")
    C.V = Eng(block, "vector")
    C.S = Eng(block, "scalar")
    C.G = Eng(block, "gpsimd")
    C.Y = Eng(block, "sync")

    emit_prologue(C, dram, pers)
    if upto in ("ffn1", "attn", "all"):
        emit_ffn(C, dram, pers, dram["x_sl"], dram["x1s"], "f1",
                 after_prologue=True)
        C.barrier()
    if upto in ("attn", "all"):
        emit_attn(C, dram, pers)
        C.barrier()
    if upto == "all":
        emit_ffn(C, dram, pers, dram["x2s"], dram["out"], "f2")

    es.close()
    return nc


def emit_attn(C, dram, pers):
    nc = C.nc
    TE, VE, SE, GE, YE = C.T, C.V, C.S, C.G, C.Y
    esA = ExitStack()
    sba = lambda n, s, d: esA.enter_context(nc.sbuf_tensor(n, s, d))
    psa = lambda n, s, d: esA.enter_context(nc.psum_tensor(n, s, d))

    dmask_sb = sba("at_dmask", [128, NH * 128], F16)
    qdec_sb = sba("at_qdec", [128, NH * 128], F16)
    kdec_sb = sba("at_kdec", [128, NH * 128], F16)
    n1w_t = sba("at_n1wt", [128, KC], F32)
    n1b_t = sba("at_n1bt", [128, KC], F16)
    n2w_t = sba("at_n2wt", [128, KC], F32)
    n2b_t = sba("at_n2bt", [128, KC], F16)
    pa = [psa(f"at_pa{i}", [128, 512], F32) for i in range(6)]
    pkt = [psa(f"at_pkt{i}", [128, 1024], F16) for i in range(2)]

    cst = C.sem("at_cst")
    YE.batch(lambda sy: (
        cst.inc(sy.dma_start(out=dmask_sb[:, :], in_=dram["dmask"][:, :]), 16),
        cst.inc(sy.dma_start(out=qdec_sb[:, :], in_=dram["qdec"][:, :]), 16),
        cst.inc(sy.dma_start(out=kdec_sb[:, :], in_=dram["kdec"][:, :]), 16),
        cst.inc(sy.dma_start(
            out=n1w_t[:, :],
            in_=dram["n1w"].ap().rearrange("o (c p) -> p (o c)", p=128),
            allow_slow_non_contiguous=True), 16),
        cst.inc(sy.dma_start(
            out=n2w_t[:, :],
            in_=dram["n2w"].ap().rearrange("o (c p) -> p (o c)", p=128),
            allow_slow_non_contiguous=True), 16),
    ))
    cstg = C.sem("at_cstg")
    GE.batch(lambda g_: (
        cstg.inc(g_.dma_start(
            out=n1b_t[:, :],
            in_=dram["n1b"].ap().rearrange("o (c p) -> p (o c)", p=128),
            allow_slow_non_contiguous=True), 16),
        cstg.inc(g_.dma_start(
            out=n2b_t[:, :],
            in_=dram["n2b"].ap().rearrange("o (c p) -> p (o c)", p=128),
            allow_slow_non_contiguous=True), 16),
    ))

    aHs = C.sem("at_aHs")

    def build_norm_fm(tag2, src_dram, wrow_dram, wt_unused, store_xa, hs_dst):
        esH = ExitStack()
        h_fm = esH.enter_context(nc.sbuf_tensor(f"{tag2}hfm", [128, KC * T], F16, side="right"))
        esHt = ExitStack()
        sbh = lambda n, s, d: esHt.enter_context(nc.sbuf_tensor(n, s, d))
        xt = [sbh(f"{tag2}xt{i}", [128, D], F32) for i in range(2)]
        xa_t = [sbh(f"{tag2}xat{i}", [128, D], F32) for i in range(2)]
        t2 = [sbh(f"{tag2}t2{i}", [128, D], F32) for i in range(2)]
        t3 = [sbh(f"{tag2}t3{i}", [128, D], F16) for i in range(2)]
        nwb = sbh(f"{tag2}nwb", [128, D], F32)
        bnb = sbh(f"{tag2}bnb", [128, 24], F32)
        mv2 = sbh(f"{tag2}mv", [128, 4], F32)
        vecs = sbh(f"{tag2}vecs", [128, 6], F32)

        aL = [C.sem("bn_aL0"), C.sem("bn_aL1")]
        aV = C.sem("bn_aV")
        aS = C.sem("bn_aS")
        aT = C.sem("bn_aT")
        aE = C.sem("bn_aE")
        aXa = [C.sem("bn_aXa0"), C.sem("bn_aXa1")]
        aG2 = C.sem("bn_aG2")
        aWr = C.sem("bn_aWr")
        wrow = sbh(f"{tag2}wrow", [1, D], F32)
        YE.batch(lambda sy: aWr.inc(
            sy.dma_start(out=wrow[:, :], in_=wrow_dram[:, :]), 16))
        wrc = aWr.n

        TE.batch(lambda t: (
            t.wait_ge(cst.h, cst.n),
            t.wait_ge(aWr.h, wrc),
            [aT.inc(t.matmul(pa[i][:, :], pers["ones_row"][0:1, 0:128],
                             wrow[0:1, i * 512:(i + 1) * 512],
                             start=True, stop=True)) for i in range(4)],
        ))
        VE.batch(lambda v: (
            v.wait_ge(aT.h, aT.n),
            [aV.inc(v.tensor_copy(nwb[:, i * 512:(i + 1) * 512], pa[i][:, :]))
             for i in range(4)],
        ))
        nwb_cnt = aV.n
        bnc = {}
        bnc2 = {}
        xastc = {}
        apc = {}
        ap2c = {}
        t3c = {}
        trc = {}
        evc = {}
        alc = {}
        for j in range(NT):
            par = j % 2

            def yl(sy, j=j, par=par):
                if j >= 2:
                    sy.wait_ge(aV.h, bnc[j - 2])
                    sy.wait_ge(aS.h, apc[j - 2])
                alc[j] = aL[par].inc(sy.dma_start(
                    out=xt[par][:, :],
                    in_=src_dram[j * 128:(j + 1) * 128, :]), 16)
            YE.batch(yl)

            def v_s1(v, j=j, par=par):
                v.wait_ge(aL[par].h, alc[j])
                if j >= 1:
                    v.wait_ge(aS.h, apc[j - 1])
                    v.wait_ge(aG2.h, t3c.get(j - 1, 0))
                for q in range(4):
                    aV.inc(v.bn_stats(bnb[:, q * 6:(q + 1) * 6],
                                      xt[par][:, q * 512:(q + 1) * 512]))
                v.wait_ge(aV.h, aV.n)
                bnc[j] = aV.inc(v.bn_aggr(mv2[:, 0:2], bnb[:, :]))
            VE.batch(v_s1)

            def s_r1(s, j=j):
                s.wait_ge(aV.h, aV.n)
                aS.inc(s.activation(vecs[:, 4:5], mv2[:, 1:2], AF.Ln,
                                    bias=pers["eps_b"][:, 0:1], scale=1.0))
                s.wait_ge(aS.h, aS.n)
                aS.inc(s.activation(vecs[:, 0:1], vecs[:, 4:5], AF.Exp,
                                    bias=0.0, scale=-0.5))
            SE.batch(s_r1)

            def v_n1(v, j=j):
                v.wait_ge(aS.h, aS.n)
                aV.inc(v.tensor_scalar(vecs[:, 1:2], mv2[:, 0:1], vecs[:, 0:1],
                                       -1.0, ALU.mult, ALU.mult))
            VE.batch(v_n1)

            if store_xa:
                def s_a1(s, j=j, par=par):
                    s.wait_ge(aV.h, aV.n)
                    if j >= 2:
                        s.wait_ge(aV.h, bnc2[j - 2])
                        s.wait_ge(aXa[par].h, xastc[j - 2])
                        s.wait_ge(aS.h, ap2c[j - 2])
                    apc[j] = aS.inc(s.activation(
                        xa_t[par][:, :], xt[par][:, :], AF.Identity,
                        bias=vecs[:, 1:2], scale=vecs[:, 0:1]))
                SE.batch(s_a1)

                def y_xa(sy, j=j, par=par):
                    sy.wait_ge(aS.h, apc[j])
                    xastc[j] = aXa[par].inc(sy.dma_start(
                        out=dram["xas"][j * 128:(j + 1) * 128, :],
                        in_=xa_t[par][:, :]), 16)
                YE.batch(y_xa)

                def v_s2(v, j=j, par=par):
                    v.wait_ge(aS.h, apc[j])
                    for q in range(4):
                        aV.inc(v.bn_stats(bnb[:, q * 6:(q + 1) * 6],
                                          xa_t[par][:, q * 512:(q + 1) * 512]))
                    v.wait_ge(aV.h, aV.n)
                    bnc2[j] = aV.inc(v.bn_aggr(mv2[:, 2:4], bnb[:, :]))
                VE.batch(v_s2)

                def s_r2(s, j=j):
                    s.wait_ge(aV.h, bnc2[j])
                    aS.inc(s.activation(vecs[:, 5:6], mv2[:, 3:4], AF.Ln,
                                        bias=pers["eps_b"][:, 0:1], scale=1.0))
                    s.wait_ge(aS.h, aS.n)
                    aS.inc(s.activation(vecs[:, 2:3], vecs[:, 5:6], AF.Exp,
                                        bias=0.0, scale=-0.5))
                SE.batch(s_r2)

                def v_n2(v, j=j):
                    v.wait_ge(aS.h, aS.n)
                    aV.inc(v.tensor_scalar(vecs[:, 3:4], mv2[:, 2:3],
                                           vecs[:, 2:3], -1.0,
                                           ALU.mult, ALU.mult))
                VE.batch(v_n2)

                def s_a2(s, j=j, par=par):
                    s.wait_ge(aV.h, aV.n)
                    if j >= 2:
                        s.wait_ge(aG2.h, t3c[j - 2])
                    ap2c[j] = aS.inc(s.activation(
                        t2[par][:, :], xa_t[par][:, :], AF.Identity,
                        bias=vecs[:, 3:4], scale=vecs[:, 2:3]))
                SE.batch(s_a2)
            else:
                def s_a2b(s, j=j, par=par):
                    s.wait_ge(aV.h, aV.n)
                    if j >= 2:
                        s.wait_ge(aG2.h, t3c[j - 2])
                    r = aS.inc(s.activation(
                        t2[par][:, :], xt[par][:, :], AF.Identity,
                        bias=vecs[:, 1:2], scale=vecs[:, 0:1]))
                    apc[j] = r
                    ap2c[j] = r
                SE.batch(s_a2b)

            def v_t3(v, j=j, par=par):
                v.wait_ge(aS.h, ap2c[j])
                v.wait_ge(aV.h, nwb_cnt)
                if j >= 2:
                    v.wait_ge(aT.h, trc[(j - 2, 1)])
                t3c[j] = aG2.inc(v.tensor_mul(t3[par][:, :], t2[par][:, :],
                                              nwb[:, :]))
            VE.batch(v_t3)

            for gi in range(2):
                def t_tr(t, j=j, gi=gi, par=par):
                    t.wait_ge(aG2.h, t3c[j])
                    gl = j * 2 + gi
                    if gl >= 2:
                        pj, pg = divmod(gl - 2, 2)
                        t.wait_ge(aE.h, evc[(pj, pg)])
                    last = None
                    for k in range(8):
                        kk = gi * 8 + k
                        last = t.matmul(
                            pkt[gi][:, k * 128:(k + 1) * 128],
                            t3[par][:, kk * 128:(kk + 1) * 128],
                            pers["ident"][:, :], is_transpose=True,
                            start=True, stop=True)
                    trc[(j, gi)] = aT.inc(last)
                TE.batch(t_tr)

                def v_ev(v, j=j, gi=gi):
                    v.wait_ge(aT.h, trc[(j, gi)])
                    out_ap = h_fm[:].rearrange("p (k c) -> p k c", c=T)[
                        :, gi * 8:(gi + 1) * 8, j * 128:(j + 1) * 128]
                    evc[(j, gi)] = aE.inc(v.tensor_copy(out_ap,
                                                        pkt[gi][:, :]))
                VE.batch(v_ev)

        if store_xa:
            YE.batch(lambda sy: (sy.wait_ge(aXa[0].h, aXa[0].n),
                                 sy.wait_ge(aXa[1].h, aXa[1].n)))
        if hs_dst is not None:
            def y_hs(sy):
                sy.wait_ge(aE.h, aE.n)
                for k in range(KC):
                    aHs.inc(sy.dma_start(
                        out=hs_dst[k * 128:(k + 1) * 128, :],
                        in_=h_fm[:, k * T:(k + 1) * T]), 16)
            YE.batch(y_hs)
        VE.batch(lambda v: v.wait_ge(aE.h, aE.n))
        esHt.close()
        return esH, h_fm

    esH, h_fm = build_norm_fm("b0", dram["x1s"], dram["n1w"], None, True, dram["hs"])
    C.barrier()

    stash_wr = {}
    retg_wr = [None, None]
    for g in range(2):
        if g == 1:
            esH = ExitStack()
            sbh2 = lambda n, s, d: esH.enter_context(nc.sbuf_tensor(n, s, d))
            h_fm = esH.enter_context(nc.sbuf_tensor("at_hfm1", [128, KC * T], F16, side="right"))
            hre = C.sem("bn_aWr")

            def y_hre(sy):
                sy.wait_ge(aHs.h, aHs.n)
                for k in range(KC):
                    hre.inc(sy.dma_start(
                        out=h_fm[:, k * T:(k + 1) * T],
                        in_=dram["hs"][k * 128:(k + 1) * 128, :]), 16)
            YE.batch(y_hre)
            YE.batch(lambda sy: sy.wait_ge(hre.h, hre.n))
            C.barrier()

        esP = ExitStack()
        sbp = lambda n, s, d: esP.enter_context(nc.sbuf_tensor(n, s, d))
        q_fm = sbp(f"at_qfm{g}", [128, 8 * T], F16)
        k_fm = sbp(f"at_kfm{g}", [128, 8 * T], F16)
        v_tm = sbp(f"at_vtm{g}", [128, NT * 1024], F16)
        esPw = ExitStack()
        sbpw = lambda n, s, d: esPw.enter_context(nc.sbuf_tensor(n, s, d))
        wsl = [sbpw(f"at_wsl{g}{i}", [128, KC * 128], F16) for i in range(2)]
        bvq = sbpw(f"at_bvq{g}", [1, 1024], F32)
        bvcol = sbpw(f"at_bvcol{g}", [128, 1], F32)
        bvcol_b = bass.AP(bvcol, 0, [[1, 128], [0, 512]])
        bvv_b = sbpw(f"at_bvvb{g}", [128, 1024], F32)

        pW = [C.sem("gr_pWa"), C.sem("gr_pWb")]
        pT = C.sem("gr_pT")
        pE = C.sem("gr_pE")
        pEv = C.sem("gr_pEv")
        pV = C.sem("gr_pV")

        mmq = {}
        evq = {}
        wlq = {}
        bvc = {}
        for pi, (wnm, dst) in enumerate((("wk", k_fm), ("wq", q_fm))):
            for hloc in range(8):
                hh = g * 8 + hloc
                lin = pi * 8 + hloc
                par = lin % 2

                def g_w(g_, hh=hh, lin=lin, par=par, wnm=wnm):
                    if lin >= 2:
                        g_.wait_ge(pT.h, mmq[lin - 2])
                    src = bass.AP(dram[wnm], hh * 128,
                                  [[D, 128], [128 * D, KC], [1, 128]])
                    wlq[lin] = pW[par].inc(g_.dma_start(
                        out=wsl[par][:].rearrange("p (k c) -> p k c", c=128),
                        in_=src), 16)
                GE.batch(g_w)

                def t_p(t, lin=lin, par=par):
                    t.wait_ge(pW[par].h, wlq[lin])
                    t.wait_ge(cstg.h, 32)
                    if lin >= 1:
                        t.wait_ge(pE.h, evq[(lin - 1, "s")])
                        t.wait_ge(pEv.h, evq[lin - 1])
                    if lin >= 1:
                        t.wait_ge(pV.h, bvc[lin - 1])
                    last = None
                    for k in range(KC):
                        lhs = wsl[par][:, k * 128:(k + 1) * 128]
                        for tt in range(4):
                            t.matmul(
                                pa[tt][:, :], lhs,
                                h_fm[:, k * T + tt * 512:k * T + (tt + 1) * 512],
                                start=(k == 0), stop=(k == KC - 1))
                        last = t.matmul(pa[4][:, 0:1], lhs, n1b_t[:, k:k + 1],
                                        start=(k == 0), stop=(k == KC - 1))
                    mmq[lin] = pT.inc(last)
                TE.batch(t_p)

                def v_bv(v, lin=lin):
                    v.wait_ge(pT.h, mmq[lin])
                    if lin >= 1:
                        v.wait_ge(pE.h, evq[(lin - 1, "s")])
                        v.wait_ge(pEv.h, evq[lin - 1])
                    bvc[lin] = pV.inc(v.tensor_copy(bvcol[:, 0:1],
                                                    pa[4][:, 0:1]))
                VE.batch(v_bv)

                def s_ev(s, lin=lin, hloc=hloc, dst=dst):
                    s.wait_ge(pT.h, mmq[lin])
                    s.wait_ge(pV.h, bvc[lin])
                    last = None
                    for tt in range(2):
                        last = s.activation(
                            dst[:, hloc * T + tt * 512:hloc * T + (tt + 1) * 512],
                            pa[tt][:, :], AF.Identity, bias=bvcol[:, 0:1],
                            scale=1.0)
                    evq[(lin, "s")] = pE.inc(last)
                SE.batch(s_ev)

                def v_ev2(v, lin=lin, hloc=hloc, dst=dst):
                    v.wait_ge(pT.h, mmq[lin])
                    v.wait_ge(pV.h, bvc[lin])
                    last = None
                    for tt in range(2, 4):
                        last = v.scalar_tensor_tensor(
                            dst[:, hloc * T + tt * 512:hloc * T + (tt + 1) * 512],
                            pa[tt][:, :], 1.0, bvcol_b,
                            ALU.mult, ALU.add)
                    evq[lin] = pEv.inc(last)
                VE.batch(v_ev2)

        wvb = sbpw(f"at_wvb{g}", [128, KC * 512], F16)
        wlv = C.sem("gr_wlv")
        mmv = {}
        evv = {}
        for n in range(2):
            def g_wv(g_, n=n, g2=g):
                if n == 1:
                    g_.wait_ge(pT.h, mmv[(0, NT - 1)])
                src = bass.AP(dram["wv"], g2 * 1024 + n * 512,
                              [[D, 128], [128 * D, KC], [1, 512]])
                wlv.inc(g_.dma_start(
                    out=wvb[:].rearrange("p (k c) -> p k c", c=512),
                    in_=src), 16)
            GE.batch(g_wv)

            def t_bv(t, n=n):
                t.wait_ge(wlv.h, wlv.n)
                t.wait_ge(cstg.h, 32)
                if n == 0:
                    t.wait_ge(pE.h, pE.n)
                    t.wait_ge(pV.h, pV.n)
                else:
                    t.wait_ge(pV.h, pV.n)
                last = None
                for k in range(KC):
                    last = t.matmul(
                        pa[4][0:1, 0:512], n1b_t[:, k:k + 1],
                        wvb[:, k * 512:(k + 1) * 512],
                        start=(k == 0), stop=(k == KC - 1))
                mmv[("bv", n)] = pT.inc(last)
            TE.batch(t_bv)

            def v_cbv(v, n=n):
                v.wait_ge(pT.h, mmv[("bv", n)])
                bvc[("v", n)] = pV.inc(v.tensor_copy(
                    bvq[0:1, n * 512:(n + 1) * 512], pa[4][0:1, 0:512]))
            VE.batch(v_cbv)

            def t_bcast(t, n=n):
                t.wait_ge(pV.h, bvc[("v", n)])
                mmv[("bc", n)] = pT.inc(t.matmul(
                    pa[5][:, :], pers["ones_row"][0:1, 0:128],
                    bvq[0:1, n * 512:(n + 1) * 512], start=True, stop=True))
            TE.batch(t_bcast)

            def v_cbc(v, n=n):
                v.wait_ge(pT.h, mmv[("bc", n)])
                bvc[("vb", n)] = pV.inc(v.tensor_copy(
                    bvv_b[:, n * 512:(n + 1) * 512], pa[5][:, :]))
            VE.batch(v_cbc)

            for t_i in range(NT):
                def t_v(t, t_i=t_i, n=n):
                    t.wait_ge(pV.h, bvc[("vb", n)])
                    if t_i >= 2:
                        t.wait_ge(pEv.h, evv[(n, t_i - 2)])
                    bank = pa[2 * (t_i % 2) + n]
                    last = None
                    for k in range(KC):
                        last = t.matmul(
                            bank[:, :],
                            h_fm[:, k * T + t_i * 128:k * T + (t_i + 1) * 128],
                            wvb[:, k * 512:(k + 1) * 512],
                            start=(k == 0), stop=(k == KC - 1))
                    mmv[(n, t_i)] = pT.inc(last)
                TE.batch(t_v)

                def v_vv(v, t_i=t_i, n=n):
                    v.wait_ge(pT.h, mmv[(n, t_i)])
                    bank = pa[2 * (t_i % 2) + n]
                    last = v.tensor_tensor(
                        v_tm[:, t_i * 1024 + n * 512:t_i * 1024 + (n + 1) * 512],
                        bank[:, :], bvv_b[:, n * 512:(n + 1) * 512], ALU.add)
                    evv[(n, t_i)] = pEv.inc(last)
                VE.batch(v_vv)

        C.barrier()
        esPw.close()
        esH.close()
        esR2 = ExitStack()
        esR = ExitStack()
        sbr = lambda n, s, d: esR.enter_context(nc.sbuf_tensor(n, s, d))
        R_hist = sbr(f"at_Rh{g}", [128, (NT + 1) * 1024], F16)
        s_til = sbr(f"at_stil{g}", [128, 1024], F16)
        kt_all = sbr(f"at_kta{g}", [128, NT * 1024], F16)
        stash_sb = [sbr(f"at_stsh{g}{i}", [128, 1024], F16) for i in range(2)]
        qtb = sbr(f"at_qtb{g}", [128, 2 * 1024], F16)

        rT = C.sem("gr_rT")
        rS2 = C.sem("gr_rS2")
        rV = C.sem("gr_rV")
        rY = [C.sem("gr_rYa"), C.sem("gr_rYb")]
        rSnd = C.sem("gr_rSnd")
        rCc = C.sem("gr_rCc")

        VE.batch(lambda v: (
            v.wait_ge(pE.h, pE.n),
            rV.inc(v.memset(R_hist[:, 0:1024], 0.0)),
        ))
        ktr = {}
        ktv = {}
        for i2 in range(NT):
            def t_kt0(t, i2=i2):
                t.wait_ge(pE.h, evq[(7, "s")])
                t.wait_ge(pEv.h, evq[7])
                if i2 >= 2:
                    t.wait_ge(rV.h, ktv[i2 - 2])
                last = None
                for hloc in range(8):
                    last = t.matmul(
                        pkt[i2 % 2][:, hloc * 128:(hloc + 1) * 128],
                        k_fm[:, hloc * T + i2 * 128:hloc * T + (i2 + 1) * 128],
                        pers["ident"][:, :], is_transpose=True,
                        start=True, stop=True)
                ktr[i2] = rT.inc(last)
            TE.batch(t_kt0)

            def v_kt0(v, i2=i2, g=g):
                v.wait_ge(rT.h, ktr[i2])
                ktv[i2] = rV.inc(v.tensor_tensor(
                    kt_all[:, i2 * 1024:(i2 + 1) * 1024], pkt[i2 % 2][:, :],
                    kdec_sb[:, g * 1024:(g + 1) * 1024], ALU.mult))
            VE.batch(v_kt0)

        msk = {}
        sstc = {}
        ktl = {}
        qtl = {}
        stu = {}
        rhc = {}
        sev = {}
        mmst = {}
        mmA = {}
        mmI = {}
        mmK = {}
        for i in range(NT):
            def t_s(t, i=i, g=g):
                t.wait_ge(pE.h, pE.n)
                t.wait_ge(pEv.h, pEv.n)
                if i >= 1:
                    t.wait_ge(rV.h, msk[i - 1])
                last = None
                for hloc in range(8):
                    bank = pa[hloc // 4]
                    last = t.matmul(
                        bank[:, (hloc % 4) * 128:(hloc % 4 + 1) * 128],
                        k_fm[:, hloc * T + i * 128:hloc * T + (i + 1) * 128],
                        q_fm[:, hloc * T + i * 128:hloc * T + (i + 1) * 128],
                        start=True, stop=True)
                mmst[i] = rT.inc(last)
            TE.batch(t_s)

            def v_msk(v, i=i, g=g):
                v.wait_ge(rT.h, mmst[i])
                if i >= 1:
                    v.wait_ge(rT.h, mmI[i - 1])
                v.tensor_tensor(s_til[:, 0:512], pa[0][:, :],
                                dmask_sb[:, g * 1024:g * 1024 + 512], ALU.mult)
                msk[i] = rV.inc(v.tensor_tensor(
                    s_til[:, 512:1024], pa[1][:, :],
                    dmask_sb[:, g * 1024 + 512:g * 1024 + 1024], ALU.mult))
            VE.batch(v_msk)

            def t_A(t, i=i):
                t.wait_ge(rV.h, ktv[i])
                if i >= 1:
                    t.wait_ge(rV.h, stu[i - 1])
                last = None
                for hloc in range(8):
                    bank = pa[4 + hloc // 4]
                    last = t.matmul(
                        bank[:, (hloc % 4) * 128:(hloc % 4 + 1) * 128],
                        kt_all[:, i * 1024 + hloc * 128:i * 1024 + (hloc + 1) * 128],
                        v_tm[:, i * 1024 + hloc * 128:i * 1024 + (hloc + 1) * 128],
                        start=True, stop=True)
                mmA[i] = rT.inc(last)
            TE.batch(t_A)

            def v_qt(v, i=i, g=g):
                if i >= 2:
                    v.wait_ge(rT.h, mmI[i - 2])
                qsrc = bass.AP(q_fm, i * 128, [[8 * T, 128], [T, 8], [1, 128]])
                qd = bass.AP(qdec_sb, g * 1024,
                             [[NH * 128, 128], [128, 8], [1, 128]])
                dst = bass.AP(qtb, (i % 2) * 1024,
                              [[2 * 1024, 128], [128, 8], [1, 128]])
                qtl[i] = rV.inc(v.tensor_tensor(dst, qsrc, qd, ALU.mult))
            VE.batch(v_qt)

            def t_ic(t, i=i):
                t.wait_ge(rV.h, msk[i])
                t.wait_ge(rV.h, qtl[i])
                if i >= 1:
                    t.wait_ge(rS2.h, sev[i - 1])
                    t.wait_ge(rV.h, rhc[i - 1])
                last = None
                for hloc in range(8):
                    bank = pa[2 + hloc // 4]
                    sl = bank[:, (hloc % 4) * 128:(hloc % 4 + 1) * 128]
                    t.matmul(sl, s_til[:, hloc * 128:(hloc + 1) * 128],
                             v_tm[:, i * 1024 + hloc * 128:i * 1024 + (hloc + 1) * 128],
                             start=True, stop=False)
                    last = t.matmul(
                        sl, qtb[:, (i % 2) * 1024 + hloc * 128:
                                (i % 2) * 1024 + (hloc + 1) * 128],
                        R_hist[:, i * 1024 + hloc * 128:i * 1024 + (hloc + 1) * 128],
                        start=False, stop=True)
                mmI[i] = rT.inc(last)
            TE.batch(t_ic)

            def v_st(v, i=i, g=g):
                v.wait_ge(rT.h, mmA[i])
                last = None
                for hloc in range(8):
                    gam = float(GAMMA[g * 8 + hloc] ** 128)
                    last = v.scalar_tensor_tensor(
                        R_hist[:, (i + 1) * 1024 + hloc * 128:
                               (i + 1) * 1024 + (hloc + 1) * 128],
                        R_hist[:, i * 1024 + hloc * 128:
                               i * 1024 + (hloc + 1) * 128], gam,
                        pa[4 + hloc // 4][:, (hloc % 4) * 128:(hloc % 4 + 1) * 128],
                        ALU.mult, ALU.add)
                stu[i] = rV.inc(last)
                rhc[i] = stu[i]
            VE.batch(v_st)

            def s_sev(s, i=i):
                s.wait_ge(rT.h, mmI[i])
                if i >= 2:
                    s.wait_ge(rY[i % 2].h, sstc[i - 2])
                s.activation(stash_sb[i % 2][:, 0:512], pa[2][:, :],
                             AF.Identity, bias=0.0, scale=1.0)
                sev[i] = rS2.inc(s.activation(
                    stash_sb[i % 2][:, 512:1024], pa[3][:, :],
                    AF.Identity, bias=0.0, scale=1.0))
            SE.batch(s_sev)

            def y_sst(sy, i=i, g=g):
                sy.wait_ge(rS2.h, sev[i])
                sstc[i] = rY[i % 2].inc(sy.dma_start(
                    out=dram["stashs"][i * 128:(i + 1) * 128,
                                       g * 1024:(g + 1) * 1024],
                    in_=stash_sb[i % 2][:, :]), 16)
            YE.batch(y_sst)

        def y_snd(sy, g=g):
            sy.wait_ge(rV.h, stu[NT - 1])
            dst = bass.AP(dram["st_in"][g], 0,
                          [[128, 128], [128 * 128, 8], [1, 128]])
            rSnd.inc(sy.dma_start(
                out=dst, in_=R_hist[:, NT * 1024:(NT + 1) * 1024]), 16)
        YE.batch(y_snd)

        pairs = [[2 * i, 2 * i + 1] for i in range(C.ncores // 2)]
        GE.batch(lambda g_, g=g: (
            g_.wait_ge(rSnd.h, rSnd.n),
            rCc.inc(g_.collective_compute(
                "AllGather", ALU.bypass, replica_groups=pairs,
                ins=[dram["st_in"][g].ap().opt()],
                outs=[dram["st_out"][g].ap().opt()])),
        ))

        C.barrier()
        esR.close()
        esF = ExitStack()
        sbf = lambda n, s, d: esF.enter_context(nc.sbuf_tensor(n, s, d))
        PF32 = sbf(f"at_PF{g}", [128, 1024], F32)
        P16 = [sbf(f"at_P16{g}{i}", [128, 1024], F16) for i in range(2)]
        stld = [sbf(f"at_stld{g}{i}", [128, 1024], F16) for i in range(2)]
        tf32 = sbf(f"at_tf{g}", [128, 1024], F32)
        sq = sbf(f"at_sq{g}", [128, 1024], F32)
        gnv = sbf(f"at_gnv{g}", [128, 40], F32)
        retg = [sbf(f"at_retg{g}{i}", [128, 1024], F16) for i in range(2)]
        qtb2 = sbf(f"at_qtb2{g}", [128, 2 * 1024], F16)

        fL = [C.sem("gr_fLa"), C.sem("gr_fLb")]
        fV = C.sem("gr_fV")
        fS = C.sem("gr_fS")
        fT = C.sem("gr_fT")
        fY = [C.sem("gr_fYa"), C.sem("gr_fYb")]
        fP = C.sem("gr_fP")

        def g_pld(g_, g2=g):
            g_.wait_ge(rCc.h, rCc.n)
            src = bass.AP(dram["st_out"][g2], 0,
                          [[128, 128], [128 * 128, 8], [1, 128]])
            fP.inc(g_.dma_start(out=PF32[:, :], in_=src), 16)
        GE.batch(g_pld)
        fPc = fP.n

        VE.batch(lambda v: (
            v.wait_ge(fP.h, fPc),
            fV.inc(v.tensor_scalar(PF32[:, :], PF32[:, :],
                                   pers["flag_b"][:, 0:1], None, ALU.mult)),
        ))
        pf_cnt = fV.n

        ld2 = {}
        qtl2 = {}
        rgstc = {}
        p16c = {}
        mmP = {}
        gnc = {}
        rgc = {}
        stc2 = {}
        for i in range(NT):
            par = i % 2

            def y_ld(sy, i=i, par=par, g=g):
                if i >= 2:
                    sy.wait_ge(fV.h, gnc[(i - 2, 0)])
                sy.wait_ge(rY[i % 2].h, rY[i % 2].n)
                ld2[i] = fL[par].inc(sy.dma_start(
                    out=stld[par][:, :],
                    in_=dram["stashs"][i * 128:(i + 1) * 128,
                                       g * 1024:(g + 1) * 1024]), 16)
            YE.batch(y_ld)

            def v_p16(v, i=i, par=par, g=g):
                v.wait_ge(fV.h, pf_cnt)
                if i >= 2:
                    v.wait_ge(fT.h, mmP[i - 2])
                last = None
                for hloc in range(8):
                    gam = float(GAMMA[g * 8 + hloc] ** (128 * i))
                    last = v.tensor_scalar(
                        P16[par][:, hloc * 128:(hloc + 1) * 128],
                        PF32[:, hloc * 128:(hloc + 1) * 128], gam, None,
                        ALU.mult)
                p16c[i] = fV.inc(last)
            VE.batch(v_p16)

            def v_qt2(v, i=i, par=par, g=g):
                if i >= 2:
                    v.wait_ge(fT.h, mmP[i - 2])
                qsrc = bass.AP(q_fm, i * 128, [[8 * T, 128], [T, 8], [1, 128]])
                qd = bass.AP(qdec_sb, g * 1024,
                             [[NH * 128, 128], [128, 8], [1, 128]])
                dst = bass.AP(qtb2, (i % 2) * 1024,
                              [[2 * 1024, 128], [128, 8], [1, 128]])
                qtl2[i] = fV.inc(v.tensor_tensor(dst, qsrc, qd, ALU.mult))
            VE.batch(v_qt2)

            def t_P(t, i=i, par=par):
                t.wait_ge(fV.h, p16c[i])
                t.wait_ge(fV.h, qtl2[i])
                if i == 0:
                    t.wait_ge(rS2.h, sev[NT - 1])
                if i == 1:
                    t.wait_ge(rV.h, msk[NT - 1])
                if i >= 2:
                    t.wait_ge(fV.h, gnc[(i - 2, 0)])
                last = None
                base = 2 - 2 * (i % 2)
                for hloc in range(8):
                    bank = pa[base + hloc // 4]
                    last = t.matmul(
                        bank[:, (hloc % 4) * 128:(hloc % 4 + 1) * 128],
                        qtb2[:, (i % 2) * 1024 + hloc * 128:
                             (i % 2) * 1024 + (hloc + 1) * 128],
                        P16[par][:, hloc * 128:(hloc + 1) * 128],
                        start=True, stop=True)
                mmP[i] = fT.inc(last)
            TE.batch(t_P)

            def v_gn1(v, i=i, par=par):
                v.wait_ge(fT.h, mmP[i])
                v.wait_ge(fL[par].h, ld2[i])
                if i >= 1:
                    v.wait_ge(fS.h, gnc[(i - 1, 1)])
                    v.wait_ge(fV.h, gnc[(i - 1, 2)])
                base = 2 - 2 * (i % 2)
                v.tensor_tensor(tf32[:, 0:512], pa[base][:, :],
                                stld[par][:, 0:512], ALU.add)
                a = v.tensor_tensor(tf32[:, 512:1024], pa[base + 1][:, :],
                                    stld[par][:, 512:1024], ALU.add)
                gnc[(i, 0)] = fV.inc(a)
                v.wait_ge(fV.h, gnc[(i, 0)])
                gnc[(i, 3)] = fV.inc(v.tensor_reduce(
                    gnv[:, 0:8],
                    tf32[:].rearrange("p (h c) -> p h c", c=128),
                    AX.X, ALU.add))
            VE.batch(v_gn1)

            def s_sq(s, i=i):
                s.wait_ge(fV.h, gnc[(i, 0)])
                if i >= 1:
                    s.wait_ge(fV.h, gnc[(i - 1, 4)])
                gnc[(i, 1)] = fS.inc(s.activation(sq[:, :], tf32[:, :],
                                                  AF.Square, bias=0.0,
                                                  scale=1.0))
            SE.batch(s_sq)

            def v_gn2(v, i=i):
                v.wait_ge(fS.h, gnc[(i, 1)])
                v.wait_ge(fV.h, gnc[(i, 3)])
                a = v.tensor_reduce(
                    gnv[:, 8:16],
                    sq[:].rearrange("p (h c) -> p h c", c=128),
                    AX.X, ALU.add)
                gnc[(i, 4)] = fV.inc(a)
                v.wait_ge(fV.h, gnc[(i, 4)])
                fV.inc(v.tensor_scalar(gnv[:, 16:24], gnv[:, 0:8], 1.0 / 128,
                                       None, ALU.mult))
                fV.inc(v.tensor_scalar(gnv[:, 24:32], gnv[:, 8:16], 1.0 / 128,
                                       None, ALU.mult))
                v.wait_ge(fV.h, fV.n)
                fV.inc(v.tensor_tensor(gnv[:, 32:40], gnv[:, 16:24],
                                       gnv[:, 16:24], ALU.mult))
                v.wait_ge(fV.h, fV.n)
                gnc[(i, 5)] = fV.inc(v.tensor_tensor(
                    gnv[:, 24:32], gnv[:, 24:32], gnv[:, 32:40], ALU.subtract))
            VE.batch(v_gn2)

            def s_rstd(s, i=i):
                s.wait_ge(fV.h, gnc[(i, 5)])
                fS.inc(s.activation(gnv[:, 32:40], gnv[:, 24:32], AF.Ln,
                                    bias=pers["eps_b"][:, 0:1], scale=1.0))
                s.wait_ge(fS.h, fS.n)
                gnc[(i, 6)] = fS.inc(s.activation(
                    gnv[:, 24:32], gnv[:, 32:40], AF.Exp, bias=0.0,
                    scale=-0.5))
            SE.batch(s_rstd)

            def v_app(v, i=i, par=par, g=g):
                v.wait_ge(fS.h, gnc[(i, 6)])
                if i >= 2:
                    v.wait_ge(fY[par].h, rgstc[i - 2])
                t3d = tf32[:].rearrange("p (h c) -> p h c", c=128)
                mu3 = bass.AP(gnv, 16, [[40, 128], [1, 8], [0, 128]])
                rs3 = bass.AP(gnv, 24, [[40, 128], [1, 8], [0, 128]])
                fV.inc(v.tensor_tensor(t3d, t3d, mu3, ALU.subtract))
                v.wait_ge(fV.h, fV.n)
                gnc[(i, 2)] = fV.inc(v.tensor_tensor(
                    retg[par][:].rearrange("p (h c) -> p h c", c=128),
                    t3d, rs3, ALU.mult))
            VE.batch(v_app)

            def y_rg(sy, i=i, par=par, g=g):
                sy.wait_ge(fV.h, gnc[(i, 2)])
                rgstc[i] = fY[par].inc(sy.dma_start(
                    out=dram["retgns"][i * 128:(i + 1) * 128,
                                       g * 1024:(g + 1) * 1024],
                    in_=retg[par][:, :]), 16)
            YE.batch(y_rg)

        YE.batch(lambda sy: (sy.wait_ge(fY[0].h, fY[0].n),
                             sy.wait_ge(fY[1].h, fY[1].n)))
        retg_wr[g] = (fY[0].n, fY[1].n, fY)
        esF.close()
        esR2.close()
        esP.close()
        C.barrier()

    esG = ExitStack()
    sbg = lambda n, s, d: esG.enter_context(nc.sbuf_tensor(n, s, d))
    esGh = ExitStack()
    h_fm = esGh.enter_context(nc.sbuf_tensor("at_hfmg", [128, KC * T], F16, side="right"))
    yin_fm = sbg("at_yinfm", [128, KC * T], F16)
    esG1 = ExitStack()
    sbg1 = lambda n, s, d: esG1.enter_context(nc.sbuf_tensor(n, s, d))
    wgsb = [sbg1(f"at_wgsb{i}", [128, KC * 512], F16) for i in range(2)]
    gtmp = [sbg1(f"at_gtmp{i}", [128, 512], F32) for i in range(2)]
    bvg_b = sbg1("at_bvgb", [128, 512], F32)
    sgt = [sbg1(f"at_sgt{i}", [128, 512], F16) for i in range(2)]
    rgt = [sbg1(f"at_rgt{i}", [128, 512], F16) for i in range(2)]
    ymt = [sbg1(f"at_ymt{i}", [128, 512], F16) for i in range(2)]
    bvg = sbg1("at_bvg", [1, 512], F32)

    hre2 = C.sem("bn_aWr")

    def y_hre2(sy):
        sy.wait_ge(aHs.h, aHs.n)
        for k in range(KC):
            hre2.inc(sy.dma_start(
                out=h_fm[:, k * T:(k + 1) * T],
                in_=dram["hs"][k * 128:(k + 1) * 128, :]), 16)
    YE.batch(y_hre2)
    YE.batch(lambda sy: sy.wait_ge(hre2.h, hre2.n))
    C.barrier()

    gW = [C.sem("gr_pWa"), C.sem("gr_pWb")]
    gT = C.sem("gr_pT")
    gS = C.sem("ffn_bS")
    gV = C.sem("ffn_bV")
    gE = C.sem("gr_pE")
    gG = C.sem("gr_gG")
    gL = [C.sem("gr_fLa"), C.sem("gr_fLb")]
    gB = C.sem("gr_fP")

    mmg = {}
    silg = {}
    gaddc = {}
    ttg = {}
    trg = {}
    evg = {}
    ldg = {}
    wlg = {}
    bvgc = {}
    for n in range(4):
        def g_wg(g_, n=n):
            par = n % 2
            if n >= 2:
                g_.wait_ge(gT.h, mmg[(n - 2, NT - 1)])
            src = bass.AP(dram["wgp"], n * 512,
                          [[D, 128], [128 * D, KC], [1, 512]])
            wlg[n] = gW[par].inc(g_.dma_start(
                out=wgsb[par][:].rearrange("p (k c) -> p k c", c=512),
                in_=src), 16)
        GE.batch(g_wg)

        def t_bvg(t, n=n):
            par = n % 2
            t.wait_ge(gW[par].h, wlg[n])
            t.wait_ge(cstg.h, 32)
            if n >= 1:
                t.wait_ge(gV.h, bvgc[n - 1])
            last = None
            for k in range(KC):
                last = t.matmul(pa[4][0:1, 0:512], n1b_t[:, k:k + 1],
                                wgsb[par][:, k * 512:(k + 1) * 512],
                                start=(k == 0), stop=(k == KC - 1))
            mmg[(n, "bv")] = gT.inc(last)
        TE.batch(t_bvg)

        def v_bvg(v, n=n):
            v.wait_ge(gT.h, mmg[(n, "bv")])
            bvgc[n] = gV.inc(v.tensor_copy(bvg[0:1, :], pa[4][0:1, 0:512]))
        VE.batch(v_bvg)

        def t_bcg(t, n=n):
            t.wait_ge(gV.h, bvgc[n])
            mmg[(n, "bc")] = gT.inc(t.matmul(
                pa[5][:, :], pers["ones_row"][0:1, 0:128], bvg[0:1, :],
                start=True, stop=True))
        TE.batch(t_bcg)

        def v_bcg(v, n=n):
            v.wait_ge(gT.h, mmg[(n, "bc")])
            bvgc[n] = gV.inc(v.tensor_copy(bvg_b[:, :], pa[5][:, :]))
        VE.batch(v_bcg)

        for t_i in range(NT):
            par = n % 2
            tp = t_i % 2
            lin = n * NT + t_i

            def t_g(t, n=n, t_i=t_i, par=par, tp=tp, lin=lin):
                t.wait_ge(gV.h, bvgc[n])
                if lin >= 2:
                    nn, jt = divmod(lin - 2, NT)
                    t.wait_ge(gV.h, gaddc[(nn, jt)])
                last = None
                for k in range(KC):
                    last = t.matmul(
                        pa[tp][:, :],
                        h_fm[:, k * T + t_i * 128:k * T + (t_i + 1) * 128],
                        wgsb[par][:, k * 512:(k + 1) * 512],
                        start=(k == 0), stop=(k == KC - 1))
                mmg[(n, t_i)] = gT.inc(last)
            TE.batch(t_g)

            def v_gadd(v, n=n, t_i=t_i, tp=tp, lin=lin):
                v.wait_ge(gT.h, mmg[(n, t_i)])
                v.wait_ge(gV.h, bvgc[n])
                if lin >= 2:
                    nn, jt = divmod(lin - 2, NT)
                    v.wait_ge(gS.h, silg[(nn, jt)])
                gaddc[(n, t_i)] = gV.inc(v.tensor_tensor(
                    gtmp[tp][:, :], pa[tp][:, :], bvg_b[:, :], ALU.add))
            VE.batch(v_gadd)

            def s_g(s, n=n, t_i=t_i, tp=tp, lin=lin):
                s.wait_ge(gV.h, gaddc[(n, t_i)])
                if lin >= 2:
                    nn, jt = divmod(lin - 2, NT)
                    s.wait_ge(gG.h, ttg[(nn, jt)])
                silg[(n, t_i)] = gS.inc(s.activation(
                    sgt[tp][:, :], gtmp[tp][:, :], AF.Silu, bias=0.0,
                    scale=1.0))
            SE.batch(s_g)

            def y_rgl(sy, n=n, t_i=t_i, tp=tp, lin=lin, g2=None):
                if lin >= 2:
                    nn, jt = divmod(lin - 2, NT)
                    sy.wait_ge(gG.h, ttg[(nn, jt)])
                for gg2 in range(2):
                    sy.wait_ge(retg_wr[gg2][2][0].h, retg_wr[gg2][0])
                    sy.wait_ge(retg_wr[gg2][2][1].h, retg_wr[gg2][1])
                ldg[(n, t_i)] = gL[tp].inc(sy.dma_start(
                    out=rgt[tp][:, :],
                    in_=dram["retgns"][t_i * 128:(t_i + 1) * 128,
                                       n * 512:(n + 1) * 512]), 16)
            YE.batch(y_rgl)

            def v_g(v, n=n, t_i=t_i, tp=tp, lin=lin):
                v.wait_ge(gS.h, silg[(n, t_i)])
                v.wait_ge(gL[tp].h, ldg[(n, t_i)])
                if lin >= 2:
                    nn, jt = divmod(lin - 2, NT)
                    v.wait_ge(gT.h, trg[(nn, jt)])
                ttg[(n, t_i)] = gG.inc(v.tensor_mul(ymt[tp][:, :],
                                                    sgt[tp][:, :],
                                                    rgt[tp][:, :]))
            VE.batch(v_g)

            def t_try(t, n=n, t_i=t_i, tp=tp, lin=lin):
                t.wait_ge(gG.h, ttg[(n, t_i)])
                if lin >= 2:
                    nn, jt = divmod(lin - 2, NT)
                    t.wait_ge(gE.h, evg[(nn, jt)])
                last = None
                for ii in range(4):
                    last = t.matmul(
                        pkt[lin % 2][:, ii * 128:(ii + 1) * 128],
                        ymt[tp][:, ii * 128:(ii + 1) * 128],
                        pers["ident"][:, :], is_transpose=True,
                        start=True, stop=True)
                trg[(n, t_i)] = gT.inc(last)
            TE.batch(t_try)

            def v_evy(v, n=n, t_i=t_i, lin=lin):
                v.wait_ge(gT.h, trg[(n, t_i)])
                out_ap = yin_fm[:].rearrange("p (k c) -> p k c", c=T)[
                    :, 4 * n:4 * n + 4, t_i * 128:(t_i + 1) * 128]
                evg[(n, t_i)] = gE.inc(v.tensor_copy(
                    out_ap, pkt[lin % 2][:, 0:512]))
            VE.batch(v_evy)

    C.barrier()
    esG1.close()
    esGh.close()
    woW = [C.sem("ffn_bW0"), C.sem("ffn_bW1")]
    woT = C.sem("ffn_bT")
    woV = C.sem("ffn_cV")
    woXa = [C.sem("ffn_cX0"), C.sem("ffn_cX1")]
    woY = [C.sem("ffn_cY0"), C.sem("ffn_cY1")]
    wosb = [sbg(f"at_wosb{i}", [128, KC * 512], F16) for i in range(2)]
    xac = [sbg(f"at_xac{i}", [128, 512], F32) for i in range(2)]
    xbc = [sbg(f"at_xbc{i}", [128, 512], F32) for i in range(2)]

    mmo = {}
    addo = {}
    sto = {}
    xal = {}
    wlo = {}
    for n in range(4):
        def g_wo(g_, n=n):
            par = n % 2
            if n >= 2:
                g_.wait_ge(woT.h, mmo[(n - 2, NT - 1)])
            src = bass.AP(dram["wo"], n * 512,
                          [[D, 128], [128 * D, KC], [1, 512]])
            wlo[n] = woW[par].inc(g_.dma_start(
                out=wosb[par][:].rearrange("p (k c) -> p k c", c=512),
                in_=src), 16)
        GE.batch(g_wo)

        for t_i in range(NT):
            par = n % 2
            lin = n * NT + t_i
            lp = lin % 2

            def y_xal(sy, n=n, t_i=t_i, lin=lin, lp=lp):
                if lin >= 2:
                    sy.wait_ge(woV.h, addo[lin - 2])
                xal[lin] = woXa[lp].inc(sy.dma_start(
                    out=xac[lp][:, :],
                    in_=dram["xas"][t_i * 128:(t_i + 1) * 128,
                                    n * 512:(n + 1) * 512]), 16)
            YE.batch(y_xal)

            def t_o(t, n=n, t_i=t_i, par=par, lin=lin):
                t.wait_ge(woW[par].h, wlo[n])
                if n == 0 and t_i == 0:
                    t.wait_ge(gE.h, gE.n)
                if lin >= 4:
                    t.wait_ge(woV.h, addo[lin - 4])
                last = None
                for k in range(KC):
                    last = t.matmul(
                        pa[lin % 4][:, :],
                        yin_fm[:, k * T + t_i * 128:k * T + (t_i + 1) * 128],
                        wosb[par][:, k * 512:(k + 1) * 512],
                        start=(k == 0), stop=(k == KC - 1))
                mmo[(n, t_i)] = woT.inc(last)
            TE.batch(t_o)

            def v_o(v, n=n, t_i=t_i, lin=lin, lp=lp):
                v.wait_ge(woT.h, mmo[(n, t_i)])
                v.wait_ge(woXa[lp].h, xal[lin])
                if lin >= 2:
                    v.wait_ge(woY[lp].h, sto[lin - 2])
                addo[lin] = woV.inc(v.tensor_tensor(
                    xbc[lp][:, :], pa[lin % 4][:, :], xac[lp][:, :], ALU.add))
            VE.batch(v_o)

            def y_o(sy, n=n, t_i=t_i, lin=lin, lp=lp):
                sy.wait_ge(woV.h, addo[lin])
                sto[lin] = woY[lp].inc(sy.dma_start(
                    out=dram["xbs"][t_i * 128:(t_i + 1) * 128,
                                    n * 512:(n + 1) * 512],
                    in_=xbc[lp][:, :]), 16)
            YE.batch(y_o)

    YE.batch(lambda sy: (sy.wait_ge(woY[0].h, woY[0].n),
                         sy.wait_ge(woY[1].h, woY[1].n)))
    esG.close()
    C.barrier()

    esH2, h2_fm = build_norm_fm("b9", dram["xbs"], dram["n2w"], None, False, None)
    C.barrier()

    es10 = ExitStack()
    sb10 = lambda n, s, d: es10.enter_context(nc.sbuf_tensor(n, s, d))
    w1sb = [sb10(f"at_w1sb{i}", [128, KC * 128], F16) for i in range(2)]
    w2sb = [sb10(f"at_w2sb{i}", [128, KC * 128], F16) for i in range(2)]
    bv12 = sb10("at_bv12", [128, 2], F32)
    sgf = [sb10(f"at_sgf{i}", [128, 512], F16) for i in range(2)]
    a3t = [sb10(f"at_a3t{i}", [128, 512], F16) for i in range(2)]

    iW = [C.sem("gr_pWa"), C.sem("gr_pWb")]
    iT = C.sem("gr_pT")
    iS = C.sem("gr_iS")
    iV = C.sem("gr_iV")
    iY = [C.sem("gr_fYa"), C.sem("gr_fYb")]

    NOC = 43
    mmi = {}
    sili = {}
    tti = {}
    sti = {}
    wli = {}
    bvi = {}
    for oc in range(NOC):
        ow = 85 if oc == NOC - 1 else 128
        par = oc % 2

        def g_w12(g_, oc=oc, ow=ow, par=par):
            if oc >= 2:
                g_.wait_ge(iT.h, mmi[(oc - 2, 3)])
            src1 = bass.AP(dram["w1"], oc * 128,
                           [[HID, 128], [128 * HID, KC], [1, ow]])
            src2 = bass.AP(dram["w2"], oc * 128,
                           [[HID, 128], [128 * HID, KC], [1, ow]])
            wr1 = w1sb[par][:].rearrange("p (k c) -> p k c", c=128)[:, :, 0:ow]
            wr2 = w2sb[par][:].rearrange("p (k c) -> p k c", c=128)[:, :, 0:ow]
            g_.dma_start(out=wr1, in_=src1).then_inc(iW[par].h, 16)
            iW[par].n += 16
            wli[oc] = iW[par].inc(g_.dma_start(out=wr2, in_=src2), 16)
        GE.batch(g_w12)

        def t_bv12(t, oc=oc, ow=ow, par=par):
            t.wait_ge(iW[par].h, wli[oc])
            t.wait_ge(cstg.h, 32)
            if oc == 0:
                t.wait_ge(cst.h, cst.n)
            if oc >= 1:
                t.wait_ge(iV.h, bvi[oc - 1])
            last = None
            for k in range(KC):
                t.matmul(pa[4][0:ow, 0:1],
                         w1sb[par][:, k * 128:k * 128 + ow],
                         n2b_t[:, k:k + 1], start=(k == 0), stop=(k == KC - 1))
                last = t.matmul(pa[5][0:ow, 0:1],
                                w2sb[par][:, k * 128:k * 128 + ow],
                                n2b_t[:, k:k + 1], start=(k == 0),
                                stop=(k == KC - 1))
            mmi[(oc, "bv")] = iT.inc(last)
        TE.batch(t_bv12)

        def v_bv12(v, oc=oc, ow=ow):
            v.wait_ge(iT.h, mmi[(oc, "bv")])
            if oc >= 1:
                v.wait_ge(iV.h, tti[(oc - 1, 3)])
                v.wait_ge(iS.h, sili[(oc - 1, 3)])
            iV.inc(v.tensor_copy(bv12[0:ow, 0:1], pa[4][0:ow, 0:1]))
            bvi[oc] = iV.inc(v.tensor_copy(bv12[0:ow, 1:2], pa[5][0:ow, 0:1]))
        VE.batch(v_bv12)

        for tt in range(4):
            lin = oc * 4 + tt

            def t_12(t, oc=oc, ow=ow, par=par, tt=tt, lin=lin):
                t.wait_ge(iV.h, bvi[oc])
                if lin >= 2:
                    po, pt = divmod(lin - 2, 4)
                    t.wait_ge(iS.h, sili[(po, pt)])
                    t.wait_ge(iV.h, tti[(po, pt)])
                last = None
                for k in range(KC):
                    t.matmul(pa[2 * (tt % 2)][0:ow, :],
                             w1sb[par][:, k * 128:k * 128 + ow],
                             h2_fm[:, k * T + tt * 512:k * T + (tt + 1) * 512],
                             start=(k == 0), stop=(k == KC - 1))
                    last = t.matmul(
                        pa[2 * (tt % 2) + 1][0:ow, :],
                        w2sb[par][:, k * 128:k * 128 + ow],
                        h2_fm[:, k * T + tt * 512:k * T + (tt + 1) * 512],
                        start=(k == 0), stop=(k == KC - 1))
                mmi[(oc, tt)] = iT.inc(last)
            TE.batch(t_12)

            def s_12(s, oc=oc, ow=ow, tt=tt, lin=lin):
                s.wait_ge(iT.h, mmi[(oc, tt)])
                if lin >= 2:
                    po, pt = divmod(lin - 2, 4)
                    s.wait_ge(iV.h, tti[(po, pt)])
                sili[(oc, tt)] = iS.inc(s.activation(
                    sgf[tt % 2][0:ow, :], pa[2 * (tt % 2)][0:ow, :], AF.Silu,
                    bias=bv12[0:ow, 0:1], scale=1.0))
            SE.batch(s_12)

            def v_12(v, oc=oc, ow=ow, tt=tt, lin=lin):
                v.wait_ge(iS.h, sili[(oc, tt)])
                if lin >= 2:
                    po, pt = divmod(lin - 2, 4)
                    v.wait_ge(iY[lin % 2].h, sti[(po, pt)])
                tti[(oc, tt)] = iV.inc(v.scalar_tensor_tensor(
                    a3t[tt % 2][0:ow, :], pa[2 * (tt % 2) + 1][0:ow, :],
                    bv12[0:ow, 1:2], sgf[tt % 2][0:ow, :],
                    ALU.add, ALU.mult))
            VE.batch(v_12)

            def y_12(sy, oc=oc, ow=ow, tt=tt, lin=lin):
                sy.wait_ge(iV.h, tti[(oc, tt)])
                sti[(oc, tt)] = iY[lin % 2].inc(sy.dma_start(
                    out=dram["act3s"][oc * 128:oc * 128 + ow,
                                      tt * 512:(tt + 1) * 512],
                    in_=a3t[tt % 2][0:ow, :]), 16)
            YE.batch(y_12)

    YE.batch(lambda sy: (sy.wait_ge(iY[0].h, iY[0].n),
                         sy.wait_ge(iY[1].h, iY[1].n)))
    iY_fin = (iY[0].n, iY[1].n)
    es10.close()
    esH2.close()
    C.barrier()

    es11 = ExitStack()
    sb11 = lambda n, s, d: es11.enter_context(nc.sbuf_tensor(n, s, d))
    a3 = sb11("at_a3", [128, NOC * 1024], F16)
    w3sb = [sb11(f"at_w3sb{i}", [128, NOC * 512], F16) for i in range(2)]
    xbld = [sb11(f"at_xbld{i}", [128, 512], F32) for i in range(2)]
    x1ld = [sb11(f"at_x1ld{i}", [128, 512], F32) for i in range(2)]
    ures = [sb11(f"at_ures{i}", [128, 512], F32) for i in range(2)]
    x2c = [sb11(f"at_x2c{i}", [128, 512], F32) for i in range(2)]

    zWp = [C.sem("gr_zWa"), C.sem("gr_zWb")]
    zA = C.sem("gr_zA")
    zT = C.sem("gr_pT")
    zV = C.sem("gr_iV")
    zXb = [C.sem("ffn_cX0"), C.sem("ffn_cX1")]
    zX1 = [C.sem("gr_fLa"), C.sem("gr_fLb")]
    zY = [C.sem("ffn_cY0"), C.sem("ffn_cY1")]

    mm3 = {}
    add3 = {}
    st3 = {}
    xbl = {}
    x1l = {}
    a3l = {}
    w3l = {}
    for pnl in range(2):
        def g_a3(g_, pnl=pnl):
            if pnl == 1:
                g_.wait_ge(zT.h, mm3[(0, 3, 7)])
            src = bass.AP(dram["act3s"], pnl * 1024,
                          [[T, 128], [128 * T, 42], [1, 1024]])
            dst = a3[:].rearrange("p (k c) -> p k c", c=1024)[:, 0:42, :]
            g_.dma_start(out=dst, in_=src).then_inc(zA.h, 16)
            zA.n += 16
            src2 = bass.AP(dram["act3s"], 42 * 128 * T + pnl * 1024,
                           [[T, 85], [1, 1024]])
            a3l[pnl] = zA.inc(g_.dma_start(
                out=a3[0:85, 42 * 1024:43 * 1024], in_=src2), 16)
        GE.batch(g_a3)

        for n in range(4):
            def g_w3(g_, pnl=pnl, n=n):
                lin2 = pnl * 4 + n
                wpar = lin2 % 2
                if lin2 >= 2:
                    pp, pn = divmod(lin2 - 2, 4)
                    g_.wait_ge(zT.h, mm3[(pp, pn, 7)])
                src = bass.AP(dram["w3"], n * 512,
                              [[D, 128], [128 * D, 42], [1, 512]])
                g_.dma_start(
                    out=w3sb[wpar][:].rearrange("p (k c) -> p k c", c=512)[:, 0:42, :],
                    in_=src).then_inc(zWp[wpar].h, 16)
                zWp[wpar].n += 16
                src2 = bass.AP(dram["w3"], 42 * 128 * D + n * 512,
                               [[D, 85], [1, 512]])
                w3l[(pnl, n)] = zWp[wpar].inc(g_.dma_start(
                    out=w3sb[wpar][0:85, 42 * 512:43 * 512], in_=src2), 16)
            GE.batch(g_w3)

            for ti in range(8):
                t_i = pnl * 8 + ti
                lin = (pnl * 4 + n) * 8 + ti
                lp = lin % 2

                def y_res(sy, t_i=t_i, n=n, lin=lin, lp=lp):
                    if lin >= 2:
                        sy.wait_ge(zV.h, add3[lin - 2])
                    xbl[lin] = zXb[lp].inc(sy.dma_start(
                        out=xbld[lp][:, :],
                        in_=dram["xbs"][t_i * 128:(t_i + 1) * 128,
                                        n * 512:(n + 1) * 512]), 16)
                    x1l[lin] = zX1[lp].inc(sy.dma_start(
                        out=x1ld[lp][:, :],
                        in_=dram["x1s"][t_i * 128:(t_i + 1) * 128,
                                        n * 512:(n + 1) * 512]), 16)
                YE.batch(y_res)

                def t_3(t, pnl=pnl, n=n, ti=ti, t_i=t_i, lin=lin):
                    wpar0 = (pnl * 4 + n) % 2
                    t.wait_ge(zWp[wpar0].h, w3l[(pnl, n)])
                    t.wait_ge(zA.h, a3l[pnl])
                    if lin >= 4:
                        t.wait_ge(zV.h, add3[lin - 4])
                    wpar = (pnl * 4 + n) % 2
                    last = None
                    for k in range(NOC):
                        kw = 85 if k == NOC - 1 else 128
                        last = t.matmul(
                            pa[lin % 4][:, :],
                            a3[0:kw, k * 1024 + ti * 128:k * 1024 + (ti + 1) * 128],
                            w3sb[wpar][0:kw, k * 512:(k + 1) * 512],
                            start=(k == 0), stop=(k == NOC - 1))
                    mm3[(pnl, n, ti)] = zT.inc(last)
                TE.batch(t_3)

                def v_3(v, pnl=pnl, n=n, ti=ti, lin=lin, lp=lp):
                    v.wait_ge(zT.h, mm3[(pnl, n, ti)])
                    v.wait_ge(zXb[lp].h, xbl[lin])
                    v.wait_ge(zX1[lp].h, x1l[lin])
                    if lin >= 2:
                        v.wait_ge(zY[lp].h, st3[lin - 2])
                    zV.inc(v.tensor_tensor(ures[lp][:, :], pa[lin % 4][:, :],
                                           xbld[lp][:, :], ALU.add))
                    v.wait_ge(zV.h, zV.n)
                    add3[lin] = zV.inc(v.tensor_tensor(
                        x2c[lp][:, :], ures[lp][:, :], x1ld[lp][:, :],
                        ALU.add))
                VE.batch(v_3)

                def y_3(sy, t_i=t_i, n=n, lin=lin, lp=lp):
                    sy.wait_ge(zV.h, add3[lin])
                    st3[lin] = zY[lp].inc(sy.dma_start(
                        out=dram["x2s"][t_i * 128:(t_i + 1) * 128,
                                        n * 512:(n + 1) * 512],
                        in_=x2c[lp][:, :]), 16)
                YE.batch(y_3)

    YE.batch(lambda sy: (sy.wait_ge(zY[0].h, zY[0].n),
                         sy.wait_ge(zY[1].h, zY[1].n)))
    es11.close()
    esA.close()


def host_consts():
    ident = np.eye(128, dtype=np.float16)
    dmask = np.zeros((128, NH * 128), dtype=np.float64)
    qdec = np.zeros((128, NH * 128), dtype=np.float64)
    kdec = np.zeros((128, NH * 128), dtype=np.float64)
    w = np.arange(128)
    u = np.arange(128)
    for h in range(NH):
        g = GAMMA[h]
        blk = np.where(u[None, :] >= w[:, None],
                       np.power(g, (u[None, :] - w[:, None]).astype(np.float64)),
                       0.0) * RDH
        dmask[:, h * 128:(h + 1) * 128] = blk
        qdec[:, h * 128:(h + 1) * 128] = np.power(g, (u + 1).astype(np.float64))[None, :]
        kdec[:, h * 128:(h + 1) * 128] = (np.power(g, (127 - w).astype(np.float64)) * RDH)[:, None]
    return (ident, dmask.astype(np.float16), qdec.astype(np.float16),
            kdec.astype(np.float16))


_BUILD_CACHE = {}


def kernel(**inputs):
    NC = 8
    nc = _BUILD_CACHE.get(NC)
    if nc is None:
        nc = build_nc(NC, "all")
        _BUILD_CACHE[NC] = nc

    x = np.asarray(inputs["L0_attn_x"])
    ident, dmask, qdec, kdec = host_consts()
    ROWG = I // NC
    ROWD = D // NC

    def asf32(a):
        return np.ascontiguousarray(np.asarray(a), dtype=np.float32)

    wg = asf32(inputs["L0_ffn_w_gate"])
    wu = asf32(inputs["L0_ffn_w_up"])
    wd = asf32(inputs["L0_ffn_w_down"])
    common = {
        "wq": asf32(inputs["L0_attn_wq"]),
        "wk": asf32(inputs["L0_attn_wk"]),
        "wv": asf32(inputs["L0_attn_wv"]),
        "wgp": asf32(inputs["L0_attn_wg"]),
        "wo": asf32(inputs["L0_attn_wo"]),
        "w1": asf32(inputs["L0_attn_w1"]),
        "w2": asf32(inputs["L0_attn_w2"]),
        "w3": asf32(inputs["L0_attn_w3"]),
        "n1w": asf32(inputs["L0_attn_norm1_w"]).reshape(1, D),
        "n1b": asf32(inputs["L0_attn_norm1_b"]).reshape(1, D),
        "n2w": asf32(inputs["L0_attn_norm2_w"]).reshape(1, D),
        "n2b": asf32(inputs["L0_attn_norm2_b"]).reshape(1, D),
        "ident": ident,
        "dmask": dmask,
        "qdec": qdec,
        "kdec": kdec,
    }
    in_maps = []
    for c in range(NC):
        b, half = c // 2, c % 2
        m = dict(common)
        m["x_sl"] = np.ascontiguousarray(
            x[b, half * T:(half + 1) * T, :], dtype=np.float32)
        m["wg_sl"] = np.ascontiguousarray(wg[c * ROWG:(c + 1) * ROWG, :])
        m["wu_sl"] = np.ascontiguousarray(wu[c * ROWG:(c + 1) * ROWG, :])
        m["wd_sl"] = np.ascontiguousarray(wd[c * ROWD:(c + 1) * ROWD, :])
        m["flag"] = np.array([[1.0 if half == 1 else 0.0]], dtype=np.float32)
        in_maps.append(m)

    res = run_bass_kernel_spmd(nc, in_maps, core_ids=list(range(NC)))
    out = np.empty((4, 4096, D), dtype=np.float32)
    for c in range(NC):
        b, half = c // 2, c % 2
        out[b, half * T:(half + 1) * T, :] = res.results[c]["out"]
    return out
